# version 04
# baseline (speedup 1.0000x reference)
import os
os.environ.setdefault("TRN_ALLOW_ICI_ALL_TO_ALL", "1")
import numpy as np
import bass_rust
import concourse.bass as bass
import concourse.mybir as mybir
from contextlib import ExitStack

F32 = mybir.dt.float32
BF16 = mybir.dt.bfloat16
I32 = mybir.dt.int32
AF = mybir.ActivationFunctionType
ALU = mybir.AluOpType

B, S, D = 2, 8192, 1024
NC, TPC, NT = 8, 2048, 16
H, DH = 16, 64
I0, I1 = 2816, 4096
EPS_DN, LN_EPS = 1e-6, 1e-5
PI = float(np.pi)

def _cw():
    tp = 2.0 * np.float64(np.pi)
    def short(v):
        f = np.float32(v)
        return (f.view(np.uint32) & np.uint32(0xFFFFF000)).view(np.float32)
    C1 = short(tp); C2 = short(tp - np.float64(C1))
    C3 = np.float32(tp - np.float64(C1) - np.float64(C2))
    return float(C1), float(C2), float(C3)
CW1, CW2, CW3 = _cw()
INV2PI = float(1.0 / (2.0 * np.pi))

def num_slot(h):
    if h < 7:
        return 0, h * 65
    if h < 14:
        return 1, (h - 7) * 65
    return 2, (h - 14) * 65
SPANS = [(0, 0, 455), (1, 455, 455), (2, 910, 130)]


def host_consts():
    c = {}
    idn = np.eye(128, dtype=np.float32)
    c["IDN"] = idn
    pp, ff = np.mgrid[0:128, 0:128]
    c["CMASK"] = (pp <= ff).astype(np.float32)
    hc, p = np.mgrid[0:32, 0:128]
    c["CPAT"] = ((p // 64) == (hc % 2)).astype(np.float32)
    hc2, f = np.mgrid[0:32, 0:1040]
    c["HSEL"] = ((f // 65) == (hc2 // 2)).astype(np.float32)
    return c


def host_inputs(d):
    x = np.ascontiguousarray(np.asarray(d["L0_attn_x"], np.float32))
    cst = host_consts()
    in_maps = []
    for c in range(NC):
        b, qd = c // 4, c % 4
        m = {}
        m["x_in"] = np.ascontiguousarray(x[b, qd * TPC:(qd + 1) * TPC, :])
        for k in d:
            if k != "L0_attn_x":
                m[k] = np.ascontiguousarray(np.asarray(d[k], np.float32))
        m["wg_sl"] = np.ascontiguousarray(np.asarray(d["L0_ffn_w_gate"], np.float32)[c * 352:(c + 1) * 352])
        m["wu_sl"] = np.ascontiguousarray(np.asarray(d["L0_ffn_w_up"], np.float32)[c * 352:(c + 1) * 352])
        m["wd_sl"] = np.ascontiguousarray(np.asarray(d["L0_ffn_w_down"], np.float32)[c * 128:(c + 1) * 128])
        m["w1_sl"] = np.ascontiguousarray(np.asarray(d["L1_ffn_W1"], np.float32)[c * 512:(c + 1) * 512])
        for pn, key in (("lar_sl", "L1_attn_log_A_real"), ("aim_sl", "L1_attn_A_imag"),
                        ("cre_sl", "L1_attn_C_re"), ("cim_sl", "L1_attn_C_im")):
            m[pn] = np.ascontiguousarray(np.asarray(d[key], np.float32)[c * 128:(c + 1) * 128])
        m["ldt_sl"] = np.ascontiguousarray(np.asarray(d["L1_attn_log_dt"], np.float32)[c * 128:(c + 1) * 128, None])
        m["d_sl"] = np.ascontiguousarray(np.asarray(d["L1_attn_D"], np.float32)[c * 128:(c + 1) * 128, None])
        m["w2_sl"] = np.ascontiguousarray(np.asarray(d["L1_ffn_W2"], np.float32)[c * 128:(c + 1) * 128])
        m["pos"] = np.arange(qd * TPC, (qd + 1) * TPC, dtype=np.float32).reshape(16, 128)
        pred = np.zeros((8, 1), np.float32)
        betwT = np.zeros((8, 8), np.float32)
        for i in range(NC):
            if i // 4 == b and i < c:
                pred[i, 0] = 1.0
                for k2 in range(i + 1, c):
                    betwT[k2, i] = 1.0
        m["chain_pred"] = pred
        m["chain_betwT"] = np.ascontiguousarray(betwT)
        for k, v in cst.items():
            m[k] = v
        in_maps.append(m)
    return in_maps


def build(upto="l0a", n_loc=16, do_ag=True, do_fin=True, n_fin=16, fstage=99):
    nc = bass.Bass("TRN2", target_bir_lowering=False)
    stack = ExitStack()
    t = {}
    def din(name, shape, dtype=F32):
        t[name] = nc.dram_tensor(name, shape, dtype, kind="ExternalInput").ap()
    def dout(name, shape, dtype=F32):
        t[name] = nc.dram_tensor(name, shape, dtype, kind="ExternalOutput").ap()
    def dint(name, shape, dtype=F32):
        t[name] = nc.dram_tensor(name, shape, dtype).ap()

    din("x_in", [TPC, D])
    for n, sh in [("L0_attn_Wq", [D, D]), ("L0_attn_Wk", [D, D]), ("L0_attn_Wv", [D, D]),
                  ("L0_attn_Wo", [D, D]), ("L0_attn_Wbeta", [D, 32]),
                  ("L0_attn_bbeta", [32]), ("L0_attn_bo", [D]),
                  ("L0_attn_ln_gamma", [D]), ("L0_attn_ln_beta", [D]),
                  ("L0_ffn_w_gate", [I0, D]), ("L0_ffn_w_up", [I0, D]), ("L0_ffn_w_down", [D, I0]),
                  ("L1_attn_log_A_real", [D, 64]), ("L1_attn_A_imag", [D, 64]),
                  ("L1_attn_C_re", [D, 64]), ("L1_attn_C_im", [D, 64]),
                  ("L1_attn_log_dt", [D]), ("L1_attn_D", [D]),
                  ("L1_attn_ln_w", [D]), ("L1_attn_ln_b", [D]),
                  ("L1_ffn_W1", [I1, D]), ("L1_ffn_b1", [I1]),
                  ("L1_ffn_W2", [D, I1]), ("L1_ffn_b2", [D]),
                  ("lar_sl", [128, 64]), ("aim_sl", [128, 64]), ("cre_sl", [128, 64]),
                  ("cim_sl", [128, 64]), ("ldt_sl", [128, 1]), ("d_sl", [128, 1]),
                  ("wg_sl", [352, D]), ("wu_sl", [352, D]), ("wd_sl", [128, I0]),
                  ("w1_sl", [512, D]), ("w2_sl", [128, I1]),
                  ("pos", [16, 128]), ("chain_pred", [8, 1]), ("chain_betwT", [8, 8]),
                  ("IDN", [128, 128]), ("CMASK", [128, 128]), ("CPAT", [32, 128]),
                  ("HSEL", [32, 1040])]:
        din(n, sh)
    dout("out", [TPC, D])

    dint("q_park", [NT, 128, D])
    dint("num_park", [NT, 128, 1040])
    dint("ag3_in", [128, 1072])
    dint("ag3_out", [NC * 128, 1072])
    dint("resid1", [NT, 128, D])
    dint("resid2", [NT, 128, D])
    dint("hqt_park", [8, 3, 128, 2048])
    SZ_GU = D * 352
    SZ_DN = I0 * 128
    AG2N = 2 * SZ_GU + SZ_DN
    OFF_G, OFF_U, OFF_DN = 0, SZ_GU, 2 * SZ_GU
    dint("ag1_in", [1, 16])
    dint("ag1_out", [NC, 16])
    dint("ag2a_in", [1, AG2N])
    dint("ag2a_out", [NC, AG2N])
    AG2BN = D * 512 + I1 * 128
    OFF_W1, OFF_W2 = 0, D * 512
    dint("ag2b_in", [1, AG2BN], BF16)
    dint("ag2b_out", [NC, AG2BN], BF16)
    dint("ag4_in", [D, TPC])
    dint("ag4_out", [D, TPC])
    dint("xch_d", [128, 16384])
    dint("khank", [128, 256])
    dint("cub_re", [128, 8256])
    dint("cub_im", [128, 8256])
    dint("az_in", [16384, 128])
    dint("az_out", [16384, 128])
    dint("resid3", [NT, 128, D])
    dint("hpark", [NT, 128, I1])
    dint("ar6_in", [1, 2])
    dint("ar6_out", [1, 2])
    dint("ar7_in", [1, 2])
    dint("ar7_out", [1, 2])

    DMA_NAMES = ["cld", "wld", "posld", "xldA", "xldB", "xld2A", "xld2B", "agr", "msl",
                 "wold", "gld", "bld", "cqldA", "cqldB", "qpk", "numpk", "stupd", "agw",
                 "mged", "r1w", "outw", "cc3",
                 "pwld", "ag1w", "cc1", "ag1r", "ag2w", "cc2a", "cc2b",
                 "wgldA", "wgldB", "r2ldA", "r2ldB", "hqpk", "wdld", "hqldA", "hqldB",
                 "r2w", "o2w", "x2ldA", "x2ldB",
                 "pd2", "xchw", "agw4", "cc4", "sd2", "agr4", "cc5", "aldA", "aldB",
                 "hld", "zrd", "zrdB", "r3w", "o3w", "x3ldA", "x3ldB", "prm4",
                 "a1r4", "b14", "ar6w", "ar6r", "cc6", "ar7w", "ar7r", "cc7", "hpwA", "hpwB",
                 "hpldA", "hpldB", "w1ld4A", "w1ld4B", "w2ld4", "x4ldA", "x4ldB", "o4w"]
    KINDS = ["te", "v", "s", "p"] + DMA_NAMES
    PHYS = {k: stack.enter_context(nc.semaphore("sem_" + k)) for k in KINDS}
    NAME2KIND = {
        **{n: n for n in DMA_NAMES},
        "cst": "p", "g0": "p", "g1": "p", "vone": "p", "mgz": "p", "stup": "p", "z1": "p", "ph1": "p", "ph2": "p",
        "fin": "p", "done": "p",
        "te": "te", "frq": "te", "xtr": "te", "proj": "te", "btr": "te", "etr": "te",
        "etr2": "te", "qktr": "te", "scor": "te", "inj": "te", "edec": "te",
        "lttr": "te", "cmb0": "te", "ctr": "te", "mgd": "te", "boc0": "te",
        "gbc0": "te", "bbc0": "te", "cqtr": "te", "corr": "te", "ytr": "te", "oproj": "te",
        "bbc": "v", "frqe": "v", "frq2": "v", "var": "v", "ln1": "v", "b0": "v",
        "b2": "v", "btr2": "v", "lam": "v", "lrel": "v", "ett": "v", "ett2": "v",
        "scal": "v", "ksc": "v", "qkev": "v", "ssev": "v", "numev": "v", "stupv": "v",
        "wpat": "v", "lttc": "v", "cmb2": "v", "wpc": "v", "mgv": "v", "mgj": "v",
        "mge": "p", "boc": "v", "gbc": "v", "bb2": "v", "cscal": "v", "cqev": "v",
        "cnume": "v", "yt": "v", "ytev": "v", "zt": "v", "zv": "v", "zr": "v",
        "x1": "v", "xtev": "v",
        "pq": "v", "pqs": "s", "ptr": "te", "pev": "v", "pabs": "v", "pabt": "te",
        "pabe": "v", "pabe2": "v", "agc1": "v", "w1q": "v", "w1tr": "te", "w1ev": "v",
        "s3st": "s", "s3ln": "v", "s3tr": "te", "s3ev": "v", "sprm": "v", "sprs": "s",
        "sptr": "te", "kcub": "v", "kcs": "s", "kfin": "v", "gln": "v", "gls": "s",
        "glt": "te", "arev": "v", "atr": "te", "aev": "v", "smm": "te", "sev": "v",
        "scan": "v", "zas": "v", "zw": "pd2", "rfin": "v", "ph3": "p", "ph4": "p",
        "szro": "p", "riot": "p", "rtg": "v", "rts": "s", "sev2": "v", "smm2": "te",
        "f4st": "s", "f4ln": "v", "f4mx": "v", "f4mx2": "v", "f4tr": "te", "f4tr2": "te",
        "f4ev": "v", "q4": "v", "q4s": "s", "g4mm": "te", "g4e": "v", "gel4": "s",
        "h4q": "p", "h4qv": "v", "h4tr": "te", "h4ev": "v", "d4mm": "te", "d4e": "v", "arbc": "te", "w1c": "v",
        "xchw": "xchw", "agw4": "agw4", "cc4": "cc4", "agr4": "agr4", "cc5": "cc5", "prm4": "prm4",
        "aldA": "aldA", "aldB": "aldB", "hld": "hld", "zrd": "zrd", "zrdB": "zrdB", "r3w": "r3w",
        "o3w": "o3w", "x3ldA": "x3ldA", "x3ldB": "x3ldB",
        "f2st": "s", "f2ln": "s", "f2q": "v", "f2tr": "te", "f2ev": "v",
        "gmm": "te", "silu": "s", "hmul": "v", "hq": "v", "hqtr": "te", "hqev": "v",
        "dmm": "te", "dev": "v", "r2x": "v", "rcp6": "v",
        "ivf": "s", "tab": "s", "stats": "s", "std": "s", "xuse": "s", "phi0": "v",
        "phi1": "s", "phi": "v", "rope": "v", "b1": "s", "b3": "s", "erow": "s",
        "erow2": "s", "wp1": "s", "vev": "s", "zst": "s", "zs2": "s", "zn": "s",
        "cxln": "s", "xtevs": "s", "qkevs": "s", "cmb1": "s",
    }
    phys_count = {k: 0 for k in KINDS}
    name_val = {}
    sems = {}
    def inc(instr, name, k=1):
        kind = NAME2KIND[name]
        instr.then_inc(PHYS[kind], k)
        phys_count[kind] += k
        sems[name] = sems.get(name, 0) + k
        name_val.setdefault(name, {})[sems[name]] = (kind, phys_count[kind])
        return sems[name]
    def wait(eng, name, v=None):
        kind = NAME2KIND[name]
        if v is None:
            assert name in sems, ("wait before any inc", name)
            pv = phys_count[kind]
        else:
            assert v in name_val.get(name, {}), ("wait on unknown ticket", name, v, sems.get(name))
            pv = name_val[name][v][1]
        eng.wait_ge(PHYS[kind], pv)

    sbt = lambda name, shape, dtype=F32: stack.enter_context(nc.sbuf_tensor(name, shape, dtype))

    _DRAIN_OPS = {"tensor_copy", "tensor_tensor", "tensor_scalar", "scalar_tensor_tensor",
                  "tensor_single_scalar", "reciprocal", "tensor_tensor_scan", "activation",
                  "copy", "add", "mul", "memset", "iota", "tensor_reduce", "reduce_sum",
                  "reduce_max", "dma_start", "collective_compute", "tensor_sub", "tensor_add",
                  "tensor_mul", "select", "transpose", "stream_shuffle"}

    class _DrainWrap:
        def __init__(self, eng):
            self._e = eng
            self._n = 0
        def __getattr__(self, k):
            f = getattr(self._e, k)
            if k in _DRAIN_OPS:
                def g(*a, **kw):
                    if self._n > 0:
                        self._e.drain()
                    self._n += 1
                    return f(*a, **kw)
                return g
            return f

    _WV = _DrainWrap(nc.vector)
    _WS = _DrainWrap(nc.scalar)
    _WG = _DrainWrap(nc.gpsimd)

    class _WrapBlk:
        def __init__(self, b):
            self._b = b
        def vector(self, fn):
            return self._b.vector(lambda e: fn(_WV))
        def scalar(self, fn):
            return self._b.scalar(lambda e: fn(_WS))
        def gpsimd(self, fn):
            return self._b.gpsimd(lambda e: fn(_WG))
        def tensor(self, fn):
            return self._b.tensor(fn)
        def sync(self, fn):
            return self._b.sync(fn)

    IDN = sbt("IDNs", [128, 128])
    CMASK = sbt("CMASKs", [128, 128])
    CPAT = sbt("CPATs", [32, 128])
    HSEL = sbt("HSELs", [32, 1040])
    ONES1 = sbt("ONES1", [1, 128])
    PS = [stack.enter_context(nc.psum_tensor("psb%d" % _i, [128, 512], F32)) for _i in range(8)]
    blk = _WrapBlk(stack.enter_context(nc.Block()))

    @blk.sync
    def _(s):
        inc(s.dma_start(out=IDN[:, :], in_=t["IDN"][:, :]), "cld", 16)
        inc(s.dma_start(out=CMASK[:, :], in_=t["CMASK"][:, :]), "cld", 16)
        inc(s.dma_start(out=CPAT[:, :], in_=t["CPAT"][:, :]), "cld", 16)
        inc(s.dma_start(out=HSEL[:, :], in_=t["HSEL"][:, :]), "cld", 16)
    @blk.gpsimd
    def _(g):
        inc(g.memset(ONES1[:, :], 1.0), "cst", 1)

    st2 = ExitStack()
    sb2 = lambda name, shape, dtype=F32: st2.enter_context(nc.sbuf_tensor(name, shape, dtype))
    XIN = sb2("XIN", [128, 2048])
    XLN = sb2("XLN", [128, 1024])
    XT = sb2("XT", [128, 1024])
    WQ = sb2("WQ", [128, 8 * 1024])
    QT = sb2("QT", [128, 1024])
    TMPA = sb2("TMPA", [128, 1024])
    QS = sb2("QS", [128, 2048])
    QTT = sb2("QTT", [128, 2048])
    MU = sb2("MU", [128, 96])
    NUMT = sb2("NUMT", [128, 1040])
    RD = sb2("RD", [128, 16])
    ECORR = sb2("ECORR", [128, 512])
    LBT = sb2("LBT", [32, 128])
    st3 = ExitStack()
    sb3 = lambda name, shape, dtype=F32: st3.enter_context(nc.sbuf_tensor(name, shape, dtype))
    WK = sb3("WK", [128, 8 * 1024])
    WV = sb3("WV", [128, 8 * 1024])
    WBt = sb3("WBt", [128, 8 * 32])
    KT = sb3("KT", [128, 1024])
    VB = sb3("VB", [128, 1040])
    KH = sb3("KH", [128, 2048])
    KTT = sb3("KTT", [128, 2048])
    SS = sb3("SS", [128, 256])
    BBC = sb3("BBC", [128, 32])
    LB = sb3("LB", [128, 32])
    LTT = sb3("LTT", [128, 128])
    LROWS = sb3("LROWS", [32, 128])
    STM = sb3("STM", [128, 1040])
    STM2 = sb3("STM2", [128, 1040])
    TAB = sb3("TAB", [128, 1024])
    LAM = sb3("LAM", [32, 2048])
    FRQ = sb3("FRQ", [128, 64])
    RK = sb3("RK", [128, 64])
    RKI = sb3("RKI", [128, 64], I32)
    IVF = sb3("IVF", [1, 32])
    IVI = sb3("IVI", [1, 32], I32)
    BRO = sb3("BRO", [1, 32])

    @blk.sync
    def _(s):
        for j in range(8):
            inc(s.dma_start(out=WQ[:, j * 1024:(j + 1) * 1024], in_=t["L0_attn_Wq"][j * 128:(j + 1) * 128, :]), "wld", 16)
            inc(s.dma_start(out=WK[:, j * 1024:(j + 1) * 1024], in_=t["L0_attn_Wk"][j * 128:(j + 1) * 128, :]), "wld", 16)
            inc(s.dma_start(out=WV[:, j * 1024:(j + 1) * 1024], in_=t["L0_attn_Wv"][j * 128:(j + 1) * 128, :]), "wld", 16)
            inc(s.dma_start(out=WBt[:, j * 32:(j + 1) * 32], in_=t["L0_attn_Wbeta"][j * 128:(j + 1) * 128, :]), "wld", 16)
        inc(s.dma_start(out=BRO[:, :], in_=t["L0_attn_bbeta"][None, :]), "wld", 16)

    @blk.tensor
    def _(te):
        wait(te, "wld"); wait(te, "cld"); wait(te, "cst")
        inc(te.matmul(PS[7][:, 0:32], ONES1[:, :], BRO[:, :], start=True, stop=True), "te", 1)
    @blk.vector
    def _(v):
        wait(v, "te")
        inc(v.tensor_copy(BBC[:, :], PS[7][:, 0:32]), "bbc", 1)

    @blk.gpsimd
    def _(g):
        inc(g.iota(IVI[:, :], [[1, 32]], base=0, channel_multiplier=0), "g0", 1)
        wait(g, "g0")
        inc(g.tensor_copy(IVF[:, :], IVI[:, :]), "g1", 1)
    @blk.scalar
    def _(sc):
        wait(sc, "g1")
        inc(sc.activation(IVF[:, :], IVF[:, :], AF.Exp, scale=float(-np.log(10000.0) / 32.0)), "ivf", 1)
    with nc.sbuf_tensor("POSS", [1, 2048], F32) as POSS:
      @blk.sync
      def _(s):
        inc(s.dma_start(out=POSS[:, :], in_=t["pos"].rearrange("a b -> (a b)").unsqueeze(0)), "posld", 16)
      for i in range(16):
        @blk.tensor
        def _(te, i=i):
            wait(te, "ivf"); wait(te, "bbc"); wait(te, "posld")
            if i > 0:
                wait(te, "frqe", i)
            inc(te.matmul(PS[7][:, 0:32], POSS[:, i * 128:(i + 1) * 128], IVF[:, :], start=True, stop=True), "frq", 1)
        @blk.vector
        def _(v, i=i):
            wait(v, "frq")
            v.tensor_copy(FRQ[:, 0:32], PS[7][:, 0:32])
            inc(v.tensor_scalar(FRQ[:, 32:64], FRQ[:, 0:32], 1.0, PI / 2.0, ALU.mult, ALU.add), "frqe", 1)
            v.tensor_scalar(RK[:, :], FRQ[:, :], INV2PI, None, ALU.mult)
            v.tensor_copy(RKI[:, :], RK[:, :])
            v.tensor_copy(RK[:, :], RKI[:, :])
            v.scalar_tensor_tensor(FRQ[:, :], RK[:, :], -CW1, FRQ[:, :], ALU.mult, ALU.add)
            v.scalar_tensor_tensor(FRQ[:, :], RK[:, :], -CW2, FRQ[:, :], ALU.mult, ALU.add)
            v.scalar_tensor_tensor(FRQ[:, :], RK[:, :], -CW3, FRQ[:, :], ALU.mult, ALU.add)
            v.tensor_single_scalar(RK[:, :], FRQ[:, :], PI, ALU.is_gt)
            v.scalar_tensor_tensor(FRQ[:, :], RK[:, :], -(CW1 + CW2), FRQ[:, :], ALU.mult, ALU.add)
            v.tensor_single_scalar(RK[:, :], FRQ[:, :], -PI, ALU.is_lt)
            v.scalar_tensor_tensor(FRQ[:, :], RK[:, :], (CW1 + CW2), FRQ[:, :], ALU.mult, ALU.add)
            inc(v.tensor_scalar(FRQ[:, :], FRQ[:, :], 3.1415925, -3.1415925, ALU.min, ALU.max), "frq2", 1)
        @blk.scalar
        def _(sc, i=i):
            wait(sc, "frq2")
            sc.activation(TAB[:, i * 64 + 32:i * 64 + 64], FRQ[:, 0:32], AF.Sin)
            inc(sc.activation(TAB[:, i * 64:i * 64 + 32], FRQ[:, 32:64], AF.Sin), "tab", 1)

    for i in range(16):
        @blk.sync
        def _(s, i=i):
            if i >= 2:
                wait(s, "stats", i - 1)
            inc(s.dma_start(out=XIN[:, (i % 2) * 1024:(i % 2) * 1024 + 1024], in_=t["x_in"][i * 128:(i + 1) * 128, :]), "xldA" if i % 2 == 0 else "xldB", 16)
        @blk.scalar
        def _(sc, i=i):
            wait(sc, "xldA" if i % 2 == 0 else "xldB", (i // 2 + 1) * 16)
            sc.activation(XLN[:, :], XIN[:, (i % 2) * 1024:(i % 2) * 1024 + 1024], AF.Identity, accum_out=MU[:, i:i + 1])
            inc(sc.activation(XLN[:, :], XIN[:, (i % 2) * 1024:(i % 2) * 1024 + 1024], AF.Square, accum_out=MU[:, 16 + i:17 + i]), "stats", 1)
    @blk.vector
    def _(v):
        wait(v, "stats")
        v.tensor_scalar(MU[:, 32:48], MU[:, 0:16], 1.0 / D, None, ALU.mult)
        v.tensor_scalar(MU[:, 48:64], MU[:, 16:32], 1.0 / D, None, ALU.mult)
        v.tensor_tensor(MU[:, 80:96], MU[:, 32:48], MU[:, 32:48], ALU.mult)
        v.tensor_tensor(MU[:, 48:64], MU[:, 48:64], MU[:, 80:96], ALU.subtract)
        inc(v.tensor_scalar(MU[:, 48:64], MU[:, 48:64], LN_EPS, None, ALU.add), "var", 1)
    @blk.scalar
    def _(sc):
        wait(sc, "var")
        inc(sc.activation(MU[:, 48:64], MU[:, 48:64], AF.Sqrt), "std", 1)
    @blk.vector
    def _(v):
        wait(v, "std")
        v.reciprocal(MU[:, 48:64], MU[:, 48:64])
        v.tensor_tensor(MU[:, 64:80], MU[:, 32:48], MU[:, 48:64], ALU.mult)
        inc(v.tensor_scalar(MU[:, 64:80], MU[:, 64:80], -1.0, None, ALU.mult), "ln1", 1)

    @blk.gpsimd
    def _(g):
        g.memset(STM[:, :], 0.0)
        inc(g.memset(STM2[:, :], 0.0), "stup", 1)

    for i in range(n_loc):
        @blk.sync
        def _(s, i=i):
            wait(s, "ln1")
            if i >= 2:
                wait(s, "xuse", i - 1)
            inc(s.dma_start(out=XIN[:, (i % 2) * 1024:(i % 2) * 1024 + 1024], in_=t["x_in"][i * 128:(i + 1) * 128, :]), "xld2A" if i % 2 == 0 else "xld2B", 16)
        @blk.scalar
        def _(sc, i=i):
            wait(sc, "xld2A" if i % 2 == 0 else "xld2B", (i // 2 + 1) * 16)
            inc(sc.activation(XLN[:, :], XIN[:, (i % 2) * 1024:(i % 2) * 1024 + 1024], AF.Identity,
                              bias=MU[:, 64 + i:65 + i], scale=MU[:, 48 + i:49 + i]), "xuse", 1)
        @blk.tensor
        def _(te, i=i):
            wait(te, "xuse")
            wait(te, "stup", i + 1)
            for j in range(4):
                te.transpose(PS[0][:, j * 128:(j + 1) * 128], XLN[:, j * 128:(j + 1) * 128], IDN[:, :])
            for j in range(3):
                te.transpose(PS[1][:, j * 128:(j + 1) * 128], XLN[:, 512 + j * 128:512 + (j + 1) * 128], IDN[:, :])
            inc(te.transpose(PS[1][:, 384:512], XLN[:, 896:1024], IDN[:, :]), "xtr", 1)
        @blk.vector
        def _(v, i=i):
            wait(v, "xtr")
            inc(v.tensor_copy(XT[:, 0:512], PS[0][:, :]), "xtev", 1)
        @blk.scalar
        def _(sc, i=i):
            wait(sc, "xtr")
            inc(sc.copy(XT[:, 512:1024], PS[1][:, :]), "xtevs", 1)
        @blk.tensor
        def _(te, i=i):
            wait(te, "xtev"); wait(te, "xtevs")
            mm = None
            for j in range(8):
                st = (j == 0); sp = (j == 7)
                lhsT = XT[:, j * 128:(j + 1) * 128]
                te.matmul(PS[2][:, :], lhsT, WQ[:, j * 1024:j * 1024 + 512], start=st, stop=sp)
                te.matmul(PS[3][:, :], lhsT, WQ[:, j * 1024 + 512:j * 1024 + 1024], start=st, stop=sp)
                te.matmul(PS[4][:, :], lhsT, WK[:, j * 1024:j * 1024 + 512], start=st, stop=sp)
                te.matmul(PS[5][:, :], lhsT, WK[:, j * 1024 + 512:j * 1024 + 1024], start=st, stop=sp)
                te.matmul(PS[6][:, :], lhsT, WV[:, j * 1024:j * 1024 + 512], start=st, stop=sp)
                te.matmul(PS[7][:, :], lhsT, WV[:, j * 1024 + 512:j * 1024 + 1024], start=st, stop=sp)
                mm = te.matmul(PS[0][:, 0:32], lhsT, WBt[:, j * 32:(j + 1) * 32], start=st, stop=sp)
            inc(mm, "proj", 1)
        @blk.vector
        def _(v, i=i):
            wait(v, "proj")
            wait(v, "tab", 16)
            if i > 0:
                wait(v, "qpk", 16 * i)
            cos = TAB[:, i * 64:i * 64 + 32].unsqueeze(1).broadcast_to([128, 8, 32])
            sin = TAB[:, i * 64 + 32:i * 64 + 64].unsqueeze(1).broadcast_to([128, 8, 32])
            for dst, blo, bhi in ((QT, 2, 3), (KT, 4, 5)):
                for half, bank in ((0, blo), (1, bhi)):
                    pr = PS[bank][:, :].rearrange("p (h d) -> p h d", d=64)
                    u1, u2 = pr[:, :, 0:32], pr[:, :, 32:64]
                    do = dst[:, half * 512:(half + 1) * 512].rearrange("p (h d) -> p h d", d=64)
                    o1, o2 = do[:, :, 0:32], do[:, :, 32:64]
                    ta = TMPA[:, 0:256].rearrange("p (h d) -> p h d", d=32)
                    v.tensor_tensor(ta, u2, sin, ALU.mult)
                    v.tensor_tensor(o1, u1, cos, ALU.mult)
                    v.tensor_tensor(o1, o1, ta, ALU.subtract)
                    v.tensor_tensor(ta, u1, sin, ALU.mult)
                    v.tensor_tensor(o2, u2, cos, ALU.mult)
                    v.tensor_tensor(o2, o2, ta, ALU.add)
            inc(nc.vector.drain(), "rope", 1)
        @blk.vector
        def _(v, i=i):
            wait(v, "rope")
            v.tensor_single_scalar(QS[:, 0:1024], QT[:, :], 0.0, ALU.min)
            inc(v.tensor_single_scalar(QS[:, 1024:2048], KT[:, :], 0.0, ALU.min), "phi0", 1)
        @blk.scalar
        def _(sc, i=i):
            wait(sc, "phi0")
            sc.activation(QS[:, 0:1024], QS[:, 0:1024], AF.Exp)
            inc(sc.activation(QS[:, 1024:2048], QS[:, 1024:2048], AF.Exp), "phi1", 1)
        @blk.vector
        def _(v, i=i):
            wait(v, "phi1")
            v.scalar_tensor_tensor(QT[:, :], QT[:, :], 0.0, QS[:, 0:1024], ALU.max, ALU.add)
            inc(v.scalar_tensor_tensor(KT[:, :], KT[:, :], 0.0, QS[:, 1024:2048], ALU.max, ALU.add), "phi", 1)
        @blk.gpsimd
        def _(g, i=i):
            wait(g, "proj")
            inc(g.memset(VB[:, :].rearrange("p (h e) -> p h e", e=65)[:, :, 64:65], 1.0), "vone", 1)
        @blk.scalar
        def _(sc, i=i):
            wait(sc, "vone"); wait(sc, "phi1")
            sc.copy(VB[:, :].rearrange("p (h e) -> p h e", e=65)[:, 0:8, 0:64], PS[6][:, :].rearrange("p (h d) -> p h d", d=64))
            inc(sc.copy(VB[:, :].rearrange("p (h e) -> p h e", e=65)[:, 8:16, 0:64], PS[7][:, :].rearrange("p (h d) -> p h d", d=64)), "vev", 1)
        @blk.gpsimd
        def _(g, i=i):
            wait(g, "phi")
            inc(g.dma_start(out=t["q_park"][i], in_=QT[:, :]), "qpk", 16)
        @blk.vector
        def _(v, i=i):
            wait(v, "proj")
            inc(v.tensor_tensor(LB[:, :], PS[0][:, 0:32], BBC[:, :], ALU.add), "b0", 1)
        @blk.scalar
        def _(sc, i=i):
            wait(sc, "b0")
            inc(sc.activation(LB[:, :], LB[:, :], AF.Sigmoid), "b1", 1)
        @blk.vector
        def _(v, i=i):
            wait(v, "b1")
            inc(v.tensor_scalar(LB[:, :], LB[:, :], 0.999, 0.001, ALU.min, ALU.max), "b2", 1)
        @blk.scalar
        def _(sc, i=i):
            wait(sc, "b2")
            inc(sc.activation(LB[:, :], LB[:, :], AF.Ln), "b3", 1)
        @blk.tensor
        def _(te, i=i):
            wait(te, "b3"); wait(te, "vev")
            inc(te.transpose(PS[7][0:32, 0:128], LB[:, :], IDN[:, :]), "btr", 1)
        @blk.vector
        def _(v, i=i):
            wait(v, "btr")
            inc(v.tensor_copy(LBT[:, :], PS[7][0:32, 0:128]), "btr2", 1)
        @blk.vector
        def _(v, i=i):
            wait(v, "btr2")
            init = 0.0 if i == 0 else LAM[:, i * 128 - 1:i * 128]
            inc(v.tensor_tensor_scan(LAM[:, i * 128:(i + 1) * 128], LBT[:, :], LBT[:, :], init, ALU.add, ALU.bypass), "lam", 1)
        @blk.vector
        def _(v, i=i):
            wait(v, "lam")
            if i == 0:
                v.tensor_copy(LROWS[:, :], LAM[:, 0:128])
            else:
                v.tensor_scalar(LROWS[:, :], LAM[:, i * 128:(i + 1) * 128], LAM[:, i * 128 - 1:i * 128], None, ALU.subtract)
            inc(nc.vector.drain(), "lrel", 1)
        for slot, (scale_, use_bias, dst0) in enumerate((( 1.0, False, 0), (-1.0, False, 32), (-1.0, True, 64))):
            @blk.scalar
            def _(sc, i=i, slot=slot, scale_=scale_, use_bias=use_bias):
                wait(sc, "lrel")
                if slot > 0:
                    wait(sc, "ett", sems["ett"])
                kw = dict(scale=scale_)
                if use_bias:
                    kw["bias"] = LROWS[:, 127:128]
                inc(sc.activation(LBT[:, :], LROWS[:, :], AF.Exp, **kw), "erow", 1)
            @blk.tensor
            def _(te, i=i, slot=slot):
                wait(te, "erow")
                inc(te.transpose(PS[7][:, 0:32], LBT[:, :], IDN[0:32, 0:32]), "etr", 1)
            @blk.vector
            def _(v, i=i, slot=slot, dst0=dst0):
                wait(v, "etr")
                inc(v.tensor_copy(LTT[:, dst0:dst0 + 32], PS[7][:, 0:32]), "ett", 1)
        @blk.scalar
        def _(sc, i=i):
            wait(sc, "ett")
            inc(sc.activation(LBT[:, :], LAM[:, i * 128:(i + 1) * 128], AF.Exp), "erow2", 1)
        @blk.tensor
        def _(te, i=i):
            wait(te, "erow2")
            inc(te.transpose(PS[7][:, 0:32], LBT[:, :], IDN[0:32, 0:32]), "etr2", 1)
        @blk.vector
        def _(v, i=i):
            wait(v, "etr2")
            inc(v.tensor_copy(ECORR[:, i * 32:(i + 1) * 32], PS[7][:, 0:32]), "ett2", 1)
        @blk.vector
        def _(v, i=i):
            wait(v, "ett2"); wait(v, "phi")
            for c in range(2):
                v.tensor_tensor(QS[:, c * 1024:(c + 1) * 1024].rearrange("p (h d) -> p h d", d=64),
                                QT[:, :].rearrange("p (h d) -> p h d", d=64),
                                LTT[:, c:32:2].unsqueeze(2).broadcast_to([128, 16, 64]), ALU.mult)
                v.tensor_tensor(KH[:, c * 1024:(c + 1) * 1024].rearrange("p (h d) -> p h d", d=64),
                                KT[:, :].rearrange("p (h d) -> p h d", d=64),
                                LTT[:, 64 + c:96:2].unsqueeze(2).broadcast_to([128, 16, 64]), ALU.mult)
            inc(nc.vector.drain(), "scal", 1)
        for c in range(2):
            @blk.vector
            def _(v, i=i, c=c):
                wait(v, "scal")
                if c == 1:
                    wait(v, "qktr")
                inc(v.tensor_tensor(XLN[:, :].rearrange("p (h d) -> p h d", d=64),
                                    KT[:, :].rearrange("p (h d) -> p h d", d=64),
                                    LTT[:, 32 + c:64:2].unsqueeze(2).broadcast_to([128, 16, 64]), ALU.mult), "ksc", 1)
            @blk.tensor
            def _(te, i=i, c=c):
                wait(te, "ksc")
                if c == 1:
                    wait(te, "qkev"); wait(te, "qkevs")
                for j in range(4):
                    te.transpose(PS[2][:, j * 128:(j + 1) * 128], QS[:, c * 1024 + j * 128:c * 1024 + (j + 1) * 128], IDN[:, :])
                    te.transpose(PS[3][:, j * 128:(j + 1) * 128], QS[:, c * 1024 + 512 + j * 128:c * 1024 + 512 + (j + 1) * 128], IDN[:, :])
                    te.transpose(PS[4][:, j * 128:(j + 1) * 128], XLN[:, j * 128:(j + 1) * 128], IDN[:, :])
                    mm = te.transpose(PS[5][:, j * 128:(j + 1) * 128], XLN[:, 512 + j * 128:512 + (j + 1) * 128], IDN[:, :])
                inc(mm, "qktr", 1)
            @blk.vector
            def _(v, i=i, c=c):
                wait(v, "qktr")
                v.tensor_copy(QTT[:, c * 1024:c * 1024 + 512], PS[2][:, :])
                inc(v.tensor_copy(QTT[:, c * 1024 + 512:c * 1024 + 1024], PS[3][:, :]), "qkev", 1)
            @blk.scalar
            def _(sc, i=i, c=c):
                wait(sc, "qktr")
                sc.copy(KTT[:, c * 1024:c * 1024 + 512], PS[4][:, :])
                inc(sc.copy(KTT[:, c * 1024 + 512:c * 1024 + 1024], PS[5][:, :]), "qkevs", 1)
        for h in range(H):
            base = (h % 2) * 64
            jt = h // 2
            nb_i, ncol = num_slot(h)
            nb = 4 + nb_i
            ib = (7, 0, 1)[nb_i]
            @blk.tensor
            def _(te, i=i, h=h, base=base, jt=jt):
                wait(te, "qkev"); wait(te, "qkevs")
                if h > 0:
                    wait(te, "ssev")
                else:
                    wait(te, "xtev"); wait(te, "xtevs")
                    wait(te, "b0")
                te.matmul(PS[2][:, 0:128], KTT[base:base + 64, jt * 128:(jt + 1) * 128],
                          QTT[base:base + 64, jt * 128:(jt + 1) * 128], start=True, stop=True)
                inc(te.matmul(PS[3][:, 0:128], KTT[base:base + 64, 1024 + jt * 128:1024 + (jt + 1) * 128],
                              QTT[base:base + 64, 1024 + jt * 128:1024 + (jt + 1) * 128], start=True, stop=True), "scor", 1)
            @blk.vector
            def _(v, i=i, h=h):
                wait(v, "scor")
                v.tensor_tensor(SS[:, 0:128], PS[2][:, 0:128], CMASK[:, :], ALU.mult)
                inc(v.tensor_tensor(SS[:, 128:256], PS[3][:, 0:128], CMASK[:, :], ALU.mult), "ssev", 1)
            @blk.tensor
            def _(te, i=i, h=h, base=base, jt=jt, nb=nb, ncol=ncol, ib=ib):
                wait(te, "ssev")
                te.matmul(PS[nb][:, ncol:ncol + 65], SS[:, 0:128], VB[:, h * 65:(h + 1) * 65], start=True, stop=False)
                te.matmul(PS[nb][:, ncol:ncol + 65], SS[:, 128:256], VB[:, h * 65:(h + 1) * 65], start=False, stop=False)
                stm_a = STM if base == 0 else STM2
                stm_b = STM2 if base == 0 else STM
                te.matmul(PS[nb][:, ncol:ncol + 65], QTT[base:base + 64, jt * 128:(jt + 1) * 128],
                          stm_a[base:base + 64, h * 65:(h + 1) * 65], start=False, stop=False)
                te.matmul(PS[nb][:, ncol:ncol + 65], QTT[base:base + 64, 1024 + jt * 128:1024 + (jt + 1) * 128],
                          stm_b[base:base + 64, h * 65:(h + 1) * 65], start=False, stop=True)
                te.matmul(PS[ib][0:64, ncol:ncol + 65], KH[:, h * 64:(h + 1) * 64], VB[:, h * 65:(h + 1) * 65],
                          start=True, stop=True)
                inc(te.matmul(PS[ib][64:128, ncol:ncol + 65], KH[:, 1024 + h * 64:1024 + (h + 1) * 64], VB[:, h * 65:(h + 1) * 65],
                              start=True, stop=True, tile_position=(0, 64)), "inj", 1)
        @blk.vector
        def _(v, i=i):
            wait(v, "inj")
            if i > 0:
                wait(v, "numpk", 16 * i)
            v.tensor_copy(NUMT[:, 0:455], PS[4][:, 0:455])
            v.tensor_copy(NUMT[:, 455:910], PS[5][:, 0:455])
            inc(v.tensor_copy(NUMT[:, 910:1040], PS[6][:, 0:130]), "numev", 1)
        @blk.gpsimd
        def _(g, i=i):
            wait(g, "numev")
            inc(g.dma_start(out=t["num_park"][i], in_=NUMT[:, :]), "numpk", 16)
        @blk.scalar
        def _(sc, i=i):
            wait(sc, "numev")
            inc(sc.activation(LBT[:, 0:1], LROWS[:, 127:128], AF.Exp), "wp1", 1)
        @blk.vector
        def _(v, i=i):
            wait(v, "wp1")
            inc(v.tensor_scalar(LTT[0:32, 0:128], CPAT[:, :], LBT[:, 0:1], None, ALU.mult), "wpat", 1)
        for sp, (nb_i, col0, w) in enumerate(SPANS):
            ib = (7, 0, 1)[nb_i]
            @blk.tensor
            def _(te, i=i, sp=sp, col0=col0, w=w):
                wait(te, "wpat"); wait(te, "ssev")
                if sp > 0:
                    wait(te, "stupv")
                inc(te.matmul(PS[2][:, 0:w], LTT[0:32, 0:128], HSEL[:, col0:col0 + w], start=True, stop=True), "edec", 1)
            @blk.vector
            def _(v, i=i, sp=sp, col0=col0, w=w, ib=ib):
                wait(v, "edec")
                v.tensor_tensor(STM[:, col0:col0 + w], STM[:, col0:col0 + w], PS[2][:, 0:w], ALU.mult)
                inc(v.tensor_tensor(STM[:, col0:col0 + w], STM[:, col0:col0 + w], PS[ib][:, 0:w], ALU.add), "stupv", 1)
        @blk.gpsimd
        def _(g, i=i):
            wait(g, "stupv")
            inc(g.dma_start(out=STM2[0:64, :], in_=STM[64:128, :]), "stupd", 16)
            inc(g.dma_start(out=STM2[64:128, :], in_=STM[0:64, :]), "stupd", 16)
        @blk.gpsimd
        def _(g, i=i):
            wait(g, "stupd")
            inc(nc.gpsimd.drain(), "stup", 1)

    def SEC(cond, deco):
        return deco if cond else (lambda fn: None)

    def _stub_out():
        @blk.gpsimd
        def _(g):
            for nm in ["qpk", "numpk", "r1w", "stupd", "agw", "xld2A", "xld2B"]:
                if nm in sems:
                    wait(g, nm)
            g.memset(TMPA[:, :], 0.0)
            for i2 in range(16):
                inc(g.dma_start(out=t["out"][i2 * 128:(i2 + 1) * 128, :], in_=TMPA[:, :]), "outw", 16)
            wait(g, "outw")
            inc(nc.gpsimd.drain(), "done", 1)

    if not do_ag:
        _stub_out()
        st3.close()
        st2.close()
        return nc, t, stack, sems

    @blk.tensor
    def _(te):
        wait(te, "stup")
        inc(te.transpose(PS[2][0:1, 0:32], LAM[:, 2047:2048], IDN[0:32, 0:32]), "lttr", 1)
    @blk.vector
    def _(v):
        wait(v, "lttr")
        inc(v.tensor_copy(LBT[0:1, 0:32], PS[2][0:1, 0:32]), "lttc", 1)
    @blk.gpsimd
    def _(g):
        wait(g, "stup"); wait(g, "lttc")
        g.memset(LTT[:, 0:32], 0.0)
        inc(g.dma_start(out=t["ag3_in"][:, 0:1040], in_=STM[:, :]), "agw", 16)
        inc(g.dma_start(out=t["ag3_in"][1:128, 1040:1072], in_=LTT[1:128, 0:32]), "agw", 16)
        inc(g.dma_start(out=t["ag3_in"][0:1, 1040:1072], in_=LBT[0:1, 0:32]), "agw", 16)
    @blk.gpsimd
    def _(g):
        wait(g, "agw")
        inc(g.collective_compute("AllGather", ALU.bypass, replica_groups=[list(range(8))],
                                 ins=[t["ag3_in"].opt()], outs=[t["ag3_out"].opt()]), "cc3", 1)
    st3.close()
    MGE = sb2("MGE", [128, 1040])
    MGE2 = sb2("MGE2", [128, 1040])
    with nc.sbuf_tensor("LT8", [8, 32], F32) as LT8, nc.sbuf_tensor("PRD", [8, 1], F32) as PRD, \
         nc.sbuf_tensor("BWT", [8, 8], F32) as BWT, nc.sbuf_tensor("COEF", [8, 32], F32) as COEF, \
         nc.sbuf_tensor("MS", [128, 1040], F32) as MS, nc.sbuf_tensor("WPC", [32, 128], F32) as WPC:
        @blk.sync
        def _(s):
            wait(s, "cc3")
            inc(s.dma_start(out=LT8[:, :], in_=bass_rust.AP(t["ag3_out"].tensor, 1040, [[128 * 1072, 8], [1, 32]])), "agr", 16)
            inc(s.dma_start(out=PRD[:, :], in_=t["chain_pred"][:, :]), "agr", 16)
            inc(s.dma_start(out=BWT[:, :], in_=t["chain_betwT"][:, :]), "agr", 16)
        @blk.tensor
        def _(te):
            wait(te, "agr")
            inc(te.matmul(PS[2][0:8, 0:32], BWT[:, :], LT8[:, :], start=True, stop=True), "cmb0", 1)
        @blk.scalar
        def _(sc):
            wait(sc, "cmb0")
            inc(sc.activation(COEF[:, :], PS[2][0:8, 0:32], AF.Exp), "cmb1", 1)
        @blk.vector
        def _(v):
            wait(v, "cmb1")
            inc(v.tensor_scalar(COEF[:, :], COEF[:, :], PRD[:, 0:1], None, ALU.mult), "cmb2", 1)
        @blk.gpsimd
        def _(g):
            g.memset(MGE[:, :], 0.0)
            inc(nc.gpsimd.drain(), "mgz", 1)
        for j in range(8):
            @blk.sync
            def _(s, j=j):
                wait(s, "cc3")
                if j > 0:
                    wait(s, "mgj", j)
                inc(s.dma_start(out=MS[:, :], in_=t["ag3_out"][j * 128:(j + 1) * 128, 0:1040]), "msl", 16)
            @blk.tensor
            def _(te, j=j):
                wait(te, "msl"); wait(te, "cmb2"); wait(te, "mgz")
                inc(te.transpose(PS[3][0:32, 0:8], COEF[:, :], IDN[0:8, 0:8]), "ctr", 1)
            @blk.vector
            def _(v, j=j):
                wait(v, "ctr")
                inc(v.tensor_scalar(WPC[:, :], CPAT[:, :], PS[3][0:32, j:j + 1], None, ALU.mult), "wpc", 1)
            for sp, (nb_i, col0, w) in enumerate(SPANS):
                @blk.tensor
                def _(te, j=j, sp=sp, col0=col0, w=w):
                    wait(te, "wpc")
                    if sp > 0:
                        wait(te, "mgv")
                    inc(te.matmul(PS[2][:, 0:w], WPC[:, :], HSEL[:, col0:col0 + w], start=True, stop=True), "mgd", 1)
                @blk.vector
                def _(v, j=j, sp=sp, col0=col0, w=w):
                    wait(v, "mgd")
                    v.tensor_tensor(MS[:, col0:col0 + w] if False else TMPA[:, 0:w], MS[:, col0:col0 + w], PS[2][:, 0:w], ALU.mult)
                    mgv = v.tensor_tensor(MGE[:, col0:col0 + w], MGE[:, col0:col0 + w], TMPA[:, 0:w], ALU.add)
                    inc(mgv, "mgv", 1)
            @blk.vector
            def _(v, j=j):
                wait(v, "mgv")
                inc(nc.vector.drain(), "mgj", 1)
        @blk.gpsimd
        def _(g):
            wait(g, "mgj", 8)
            inc(g.dma_start(out=MGE2[0:64, :], in_=MGE[64:128, :]), "mged", 16)
            inc(g.dma_start(out=MGE2[64:128, :], in_=MGE[0:64, :]), "mged", 16)
        @blk.gpsimd
        def _(g):
            wait(g, "mged")
            inc(nc.gpsimd.drain(), "mge", 1)

    WO = WQ
    Z1 = sb2("Z1", [1, 128])
    BOC = sb2("BOC", [128, 1024])
    GBC = sb2("GBC", [128, 1024])
    BBC2 = sb2("BBC2", [128, 1024])
    YT = sb2("YT", [128, 1024])
    with nc.sbuf_tensor("BROW", [1, 1024], F32) as BROW:
        @blk.sync
        def _(s):
            wait(s, "mge")
            for j in range(8):
                inc(s.dma_start(out=WO[:, j * 1024:(j + 1) * 1024], in_=t["L0_attn_Wo"][j * 128:(j + 1) * 128, :]), "wold", 16)
            inc(s.dma_start(out=BROW[:, :], in_=t["L0_attn_bo"][None, :]), "wold", 16)
        @blk.tensor
        def _(te):
            wait(te, "wold")
            te.matmul(PS[2][:, :], ONES1[:, :], BROW[:, 0:512], start=True, stop=True)
            inc(te.matmul(PS[3][:, :], ONES1[:, :], BROW[:, 512:1024], start=True, stop=True), "boc0", 1)
        @blk.vector
        def _(v):
            wait(v, "boc0")
            v.tensor_copy(BOC[:, 0:512], PS[2][:, :])
            inc(v.tensor_copy(BOC[:, 512:1024], PS[3][:, :]), "boc", 1)
        @blk.sync
        def _(s):
            wait(s, "boc")
            inc(s.dma_start(out=BROW[:, :], in_=t["L0_attn_ln_gamma"][None, :]), "gld", 16)
        @blk.tensor
        def _(te):
            wait(te, "gld")
            te.matmul(PS[2][:, :], ONES1[:, :], BROW[:, 0:512], start=True, stop=True)
            inc(te.matmul(PS[3][:, :], ONES1[:, :], BROW[:, 512:1024], start=True, stop=True), "gbc0", 1)
        @blk.vector
        def _(v):
            wait(v, "gbc0")
            v.tensor_copy(GBC[:, 0:512], PS[2][:, :])
            inc(v.tensor_copy(GBC[:, 512:1024], PS[3][:, :]), "gbc", 1)
        @blk.sync
        def _(s):
            wait(s, "gbc")
            inc(s.dma_start(out=BROW[:, :], in_=t["L0_attn_ln_beta"][None, :]), "bld", 16)
        @blk.tensor
        def _(te):
            wait(te, "bld")
            te.matmul(PS[2][:, :], ONES1[:, :], BROW[:, 0:512], start=True, stop=True)
            inc(te.matmul(PS[3][:, :], ONES1[:, :], BROW[:, 512:1024], start=True, stop=True), "bbc0", 1)
        @blk.vector
        def _(v):
            wait(v, "bbc0")
            v.tensor_copy(BBC2[:, 0:512], PS[2][:, :])
            inc(v.tensor_copy(BBC2[:, 512:1024], PS[3][:, :]), "bb2", 1)

    @blk.gpsimd
    def _(g):
        inc(g.memset(Z1[:, :], 0.0), "z1", 1)

    if not do_fin:
        _stub_out()
        st2.close()
        return nc, t, stack, sems

    for i in range(n_fin):
        @blk.sync
        def _(s, i=i):
            wait(s, "bb2")
            if i < 2:
                wait(s, "qpk")
                wait(s, "numpk")
            if i > 0:
                wait(s, "cscal", i)
                wait(s, "yt", i)
            if i >= 2:
                wait(s, "x1", i - 1)
            nm = "cqldA" if i % 2 == 0 else "cqldB"
            inc(s.dma_start(out=QT[:, :], in_=t["q_park"][i]), nm, 16)
            inc(s.dma_start(out=NUMT[:, :], in_=t["num_park"][i]), nm, 16)
            inc(s.dma_start(out=XIN[:, (i % 2) * 1024:(i % 2) * 1024 + 1024], in_=t["x_in"][i * 128:(i + 1) * 128, :]), nm, 16)
        @blk.vector
        def _(v, i=i):
            wait(v, "cqldA" if i % 2 == 0 else "cqldB", 48 * (i // 2 + 1))
            for c in range(2):
                v.tensor_tensor(QS[:, c * 1024:(c + 1) * 1024].rearrange("p (h d) -> p h d", d=64),
                                QT[:, :].rearrange("p (h d) -> p h d", d=64),
                                ECORR[:, i * 32 + c:i * 32 + 32:2].unsqueeze(2).broadcast_to([128, 16, 64]), ALU.mult)
            inc(nc.vector.drain(), "cscal", 1)
        for c in range(2):
            @SEC(fstage >= 2, blk.tensor)
            def _(te, i=i, c=c):
                wait(te, "cscal")
                if c == 1 or i > 0:
                    wait(te, "cqev")
                for j in range(4):
                    te.transpose(PS[2][:, j * 128:(j + 1) * 128], QS[:, c * 1024 + j * 128:c * 1024 + (j + 1) * 128], IDN[:, :])
                    mm = te.transpose(PS[3][:, j * 128:(j + 1) * 128], QS[:, c * 1024 + 512 + j * 128:c * 1024 + 512 + (j + 1) * 128], IDN[:, :])
                inc(mm, "cqtr", 1)
            @SEC(fstage >= 2, blk.vector)
            def _(v, i=i, c=c):
                wait(v, "cqtr")
                v.tensor_copy(QTT[:, c * 1024:c * 1024 + 512], PS[2][:, :])
                inc(v.tensor_copy(QTT[:, c * 1024 + 512:c * 1024 + 1024], PS[3][:, :]), "cqev", 1)
        @SEC(fstage >= 3 or fstage == 36, blk.tensor)
        def _(te, i=i):
            wait(te, "cqev"); wait(te, "z1")
            if i > 0:
                wait(te, "cnume")
                wait(te, "zt", i)
            mm = None
            for h in range(H):
                base = (h % 2) * 64
                jt = h // 2
                nb_i, ncol = num_slot(h)
                nb = 4 + nb_i
                mge_a = MGE if base == 0 else MGE2
                mge_b = MGE2 if base == 0 else MGE
                te.matmul(PS[nb][:, ncol:ncol + 65], Z1[:, :], ONES1[:, 0:65], start=True, stop=False)
                te.matmul(PS[nb][:, ncol:ncol + 65], QTT[base:base + 64, jt * 128:(jt + 1) * 128],
                          mge_a[base:base + 64, h * 65:(h + 1) * 65], start=False, stop=False)
                mm = te.matmul(PS[nb][:, ncol:ncol + 65], QTT[base:base + 64, 1024 + jt * 128:1024 + (jt + 1) * 128],
                               mge_b[base:base + 64, h * 65:(h + 1) * 65], start=False, stop=True)
            inc(mm, "corr", 1)
        @SEC(fstage >= 35, blk.vector)
        def _(v, i=i):
            wait(v, "corr")
            v.tensor_tensor(NUMT[:, 0:455], NUMT[:, 0:455], PS[4][:, 0:455], ALU.add)
            v.tensor_tensor(NUMT[:, 455:910], NUMT[:, 455:910], PS[5][:, 0:455], ALU.add)
            inc(v.tensor_tensor(NUMT[:, 910:1040], NUMT[:, 910:1040], PS[6][:, 0:130], ALU.add), "cnume", 1)
        @SEC(fstage >= 4, blk.vector)
        def _(v, i=i):
            wait(v, "cnume")
            v.tensor_scalar(RD[:, :], NUMT[:, :].rearrange("p (h e) -> p h e", e=65)[:, :, 64], EPS_DN, None, ALU.add)
            v.reciprocal(RD[:, :], RD[:, :])
            inc(v.tensor_tensor(YT[:, :].rearrange("p (h d) -> p h d", d=64),
                                NUMT[:, :].rearrange("p (h e) -> p h e", e=65)[:, :, 0:64],
                                RD[:, :].unsqueeze(2).broadcast_to([128, 16, 64]), ALU.mult), "yt", 1)
        @SEC(fstage >= 5, blk.tensor)
        def _(te, i=i):
            wait(te, "yt")
            for j in range(4):
                te.transpose(PS[2][:, j * 128:(j + 1) * 128], YT[:, j * 128:(j + 1) * 128], IDN[:, :])
                mm = te.transpose(PS[3][:, j * 128:(j + 1) * 128], YT[:, 512 + j * 128:512 + (j + 1) * 128], IDN[:, :])
            inc(mm, "ytr", 1)
        @SEC(fstage >= 5, blk.vector)
        def _(v, i=i):
            wait(v, "ytr")
            v.tensor_copy(QS[:, 0:512], PS[2][:, :])
            inc(v.tensor_copy(QS[:, 512:1024], PS[3][:, :]), "ytev", 1)
        @SEC(fstage >= 6, blk.tensor)
        def _(te, i=i):
            wait(te, "ytev")
            for j in range(8):
                st = (j == 0); sp = (j == 7)
                lhsT = QS[:, j * 128:(j + 1) * 128]
                te.matmul(PS[5][:, :], lhsT, WO[:, j * 1024:j * 1024 + 512], start=st, stop=sp)
                mm = te.matmul(PS[6][:, :], lhsT, WO[:, j * 1024 + 512:j * 1024 + 1024], start=st, stop=sp)
            inc(mm, "oproj", 1)
        @SEC(fstage >= 7, blk.scalar)
        def _(sc, i=i):
            wait(sc, "cqldA" if i % 2 == 0 else "cqldB", 48 * (i // 2 + 1))
            inc(sc.activation(XLN[:, :], XIN[:, (i % 2) * 1024:(i % 2) * 1024 + 1024], AF.Identity,
                              bias=MU[:, 64 + i:65 + i], scale=MU[:, 48 + i:49 + i]), "cxln", 1)
        @SEC(fstage >= 7, blk.vector)
        def _(v, i=i):
            wait(v, "oproj"); wait(v, "cxln")
            v.tensor_tensor(YT[:, 0:512], XLN[:, 0:512], PS[5][:, :], ALU.add)
            v.tensor_tensor(YT[:, 512:1024], XLN[:, 512:1024], PS[6][:, :], ALU.add)
            inc(v.tensor_tensor(YT[:, :], YT[:, :], BOC[:, :], ALU.add), "zt", 1)
        @SEC(fstage >= 8, blk.scalar)
        def _(sc, i=i):
            wait(sc, "zt")
            if i > 0:
                wait(sc, "r1w", 16 * i)
                if upto == "l0a":
                    wait(sc, "outw", 16 * i)
            sc.activation(TMPA[:, :], YT[:, :], AF.Identity, accum_out=MU[:, 80:81])
            inc(sc.activation(TMPA[:, :], YT[:, :], AF.Square, accum_out=MU[:, 81:82]), "zst", 1)
        @SEC(fstage >= 8, blk.vector)
        def _(v, i=i):
            wait(v, "zst")
            v.tensor_scalar(MU[:, 82:83], MU[:, 80:81], 1.0 / D, None, ALU.mult)
            v.tensor_scalar(MU[:, 83:84], MU[:, 81:82], 1.0 / D, None, ALU.mult)
            v.tensor_tensor(MU[:, 84:85], MU[:, 82:83], MU[:, 82:83], ALU.mult)
            v.tensor_tensor(MU[:, 83:84], MU[:, 83:84], MU[:, 84:85], ALU.subtract)
            inc(v.tensor_scalar(MU[:, 83:84], MU[:, 83:84], LN_EPS, None, ALU.add), "zv", 1)
        @SEC(fstage >= 8, blk.scalar)
        def _(sc, i=i):
            wait(sc, "zv")
            inc(sc.activation(MU[:, 83:84], MU[:, 83:84], AF.Sqrt), "zs2", 1)
        @SEC(fstage >= 8, blk.vector)
        def _(v, i=i):
            wait(v, "zs2")
            v.reciprocal(MU[:, 83:84], MU[:, 83:84])
            v.tensor_tensor(MU[:, 84:85], MU[:, 82:83], MU[:, 83:84], ALU.mult)
            inc(v.tensor_scalar(MU[:, 84:85], MU[:, 84:85], -1.0, None, ALU.mult), "zr", 1)
        @SEC(fstage >= 8, blk.scalar)
        def _(sc, i=i):
            wait(sc, "zr")
            inc(sc.activation(TMPA[:, :], YT[:, :], AF.Identity,
                              bias=MU[:, 84:85], scale=MU[:, 83:84]), "zn", 1)
        @SEC(fstage >= 8, blk.vector)
        def _(v, i=i):
            wait(v, "zn")
            v.tensor_tensor(TMPA[:, :], TMPA[:, :], GBC[:, :], ALU.mult)
            v.tensor_tensor(TMPA[:, :], TMPA[:, :], BBC2[:, :], ALU.add)
            inc(v.tensor_tensor(TMPA[:, :], TMPA[:, :], XIN[:, (i % 2) * 1024:(i % 2) * 1024 + 1024], ALU.add), "x1", 1)
        @SEC(fstage >= 8, blk.gpsimd)
        def _(g, i=i):
            wait(g, "x1")
            inc(g.dma_start(out=t["resid1"][i], in_=TMPA[:, :]), "r1w", 16)
            if upto == "l0a":
                inc(g.dma_start(out=t["out"][i * 128:(i + 1) * 128, :], in_=TMPA[:, :]), "outw", 16)
            inc(nc.gpsimd.drain(), "fin", 1)

    if upto == "l0a":
        @blk.gpsimd
        def _(g):
            if n_fin < 16:
                if "r1w" in sems:
                    wait(g, "r1w")
                if "outw" in sems:
                    wait(g, "outw")
                g.memset(TMPA[:, :], 0.0)
                for i2 in range(n_fin, 16):
                    inc(g.dma_start(out=t["out"][i2 * 128:(i2 + 1) * 128, :], in_=TMPA[:, :]), "outw", 16)
            wait(g, "outw")
            if "r1w" in sems:
                wait(g, "r1w")
            wait(g, "qpk")
            wait(g, "numpk")
            inc(nc.gpsimd.drain(), "done", 1)
        st2.close()
        return nc, t, stack, sems

    @blk.gpsimd
    def _(g):
        wait(g, "fin")
        wait(g, "r1w")
        wait(g, "qpk")
        wait(g, "numpk")
        inc(nc.gpsimd.drain(), "ph1", 1)
    st2.close()

    AX = mybir.AxisListType.X

    stp = ExitStack()
    sbp = lambda name, shape, dtype=F32: stp.enter_context(nc.sbuf_tensor(name, shape, dtype))
    PIN = sbp("PIN", [128, I1])
    POUT = sbp("POUT", [128, I0])
    PTMP = sbp("PTMP", [128, I1])
    PTI = sbp("PTI", [128, I1], I32)
    PSGN = sbp("PSGN", [128, I0])
    PQz = sbp("PQz", [128, I1])
    PRz = sbp("PRz", [128, I1])
    PPP = sbp("PPP", [128, I0])
    PSCL = sbp("PSCL", [128, 384])
    PST = sbp("PST", [128, 1024])
    PSTB = sbp("PSTB", [128, 1024], BF16)
    RC6 = sbp("RC6", [1, 16])
    RCB = sbp("RCB", [128, 16])
    AMX = sbp("AMX", [128, 16])
    RSB = sbp("RSB", [128, 8])

    @blk.gpsimd
    def _(g):
        wait(g, "ph1")
        g.memset(RC6[:, :], 0.0)
        g.memset(RC6[:, 0:1], 6.0)
        g.memset(RC6[:, 1:2], 127.0)
        g.memset(RC6[:, 2:3], 255.0)
        inc(nc.gpsimd.drain(), "cst", 1)
    @blk.vector
    def _(v):
        wait(v, "cst")
        inc(v.reciprocal(RC6[:, 4:7], RC6[:, 0:3]), "rcp6", 1)
    @blk.tensor
    def _(te):
        wait(te, "rcp6"); wait(te, "ph1")
        inc(te.matmul(PS[1][:, 0:16], ONES1[:, :], RC6[:, :], start=True, stop=True), "pabt", 1)
    @blk.vector
    def _(v):
        wait(v, "pabt")
        inc(v.tensor_copy(RCB[:, :], PS[1][:, 0:16]), "rcp6", 1)

    def emit_fp4q(P_, N, IN, OUT, SCL, TMP, TMPI, SGN, Q, R, PP, extra_wait=None):
        nb = N // 16
        @blk.vector
        def _(v):
            if extra_wait is not None:
                extra_wait(v)
            v.tensor_reduce(SCL[0:P_, 0:nb].unsqueeze(2), IN[0:P_, 0:N].rearrange("p (b k) -> p b k", k=16),
                            AX, ALU.max, apply_absolute_value=True)
            v.tensor_scalar(SCL[0:P_, 0:nb], SCL[0:P_, 0:nb], RCB[0:P_, 4:5], 1e-12, ALU.mult, ALU.max)
            v.reciprocal(SCL[0:P_, 176:176 + nb], SCL[0:P_, 0:nb])
            v.tensor_tensor(TMP[0:P_, 0:N].rearrange("p (b k) -> p b k", k=16),
                            IN[0:P_, 0:N].rearrange("p (b k) -> p b k", k=16),
                            SCL[0:P_, 176:176 + nb].unsqueeze(2).broadcast_to([P_, nb, 16]), ALU.mult)
            inc(nc.vector.drain(), "pq", 1)
        @blk.scalar
        def _(sc):
            wait(sc, "pq")
            sc.activation(SGN[0:P_, 0:N], TMP[0:P_, 0:N], AF.Sign)
            inc(sc.activation(TMP[0:P_, 0:N], TMP[0:P_, 0:N], AF.Abs), "pqs", 1)
        @blk.vector
        def _(v):
            wait(v, "pqs")
            v.tensor_scalar(OUT[0:P_, 0:N], TMP[0:P_, 0:N], 2.0, 0.5, ALU.mult, ALU.add)
            v.tensor_copy(TMPI[0:P_, 0:N], OUT[0:P_, 0:N])
            v.tensor_copy(Q[0:P_, 0:N], TMPI[0:P_, 0:N])
            v.tensor_tensor(R[0:P_, 0:N], Q[0:P_, 0:N], OUT[0:P_, 0:N], ALU.is_gt)
            v.tensor_tensor(Q[0:P_, 0:N], Q[0:P_, 0:N], R[0:P_, 0:N], ALU.subtract)
            v.tensor_scalar(OUT[0:P_, 0:N], TMP[0:P_, 0:N], 1.0, 0.5, ALU.mult, ALU.add)
            v.tensor_copy(TMPI[0:P_, 0:N], OUT[0:P_, 0:N])
            v.tensor_copy(R[0:P_, 0:N], TMPI[0:P_, 0:N])
            v.tensor_tensor(PP[0:P_, 0:N], R[0:P_, 0:N], OUT[0:P_, 0:N], ALU.is_gt)
            v.tensor_tensor(R[0:P_, 0:N], R[0:P_, 0:N], PP[0:P_, 0:N], ALU.subtract)
            v.tensor_scalar(OUT[0:P_, 0:N], Q[0:P_, 0:N], 0.5, None, ALU.mult)
            v.tensor_single_scalar(PP[0:P_, 0:N], TMP[0:P_, 0:N], 2.25, ALU.is_gt)
            v.tensor_tensor(Q[0:P_, 0:N], R[0:P_, 0:N], OUT[0:P_, 0:N], ALU.subtract)
            v.tensor_tensor(PP[0:P_, 0:N], PP[0:P_, 0:N], Q[0:P_, 0:N], ALU.mult)
            v.tensor_tensor(OUT[0:P_, 0:N], OUT[0:P_, 0:N], PP[0:P_, 0:N], ALU.add)
            v.tensor_single_scalar(PP[0:P_, 0:N], TMP[0:P_, 0:N], 4.5, ALU.is_gt)
            v.tensor_single_scalar(Q[0:P_, 0:N], TMP[0:P_, 0:N], 5.0, ALU.is_gt)
            v.tensor_scalar(Q[0:P_, 0:N], Q[0:P_, 0:N], 2.0, 4.0, ALU.mult, ALU.add)
            v.tensor_tensor(Q[0:P_, 0:N], Q[0:P_, 0:N], R[0:P_, 0:N], ALU.subtract)
            v.tensor_tensor(PP[0:P_, 0:N], PP[0:P_, 0:N], Q[0:P_, 0:N], ALU.mult)
            v.tensor_tensor(OUT[0:P_, 0:N], OUT[0:P_, 0:N], PP[0:P_, 0:N], ALU.add)
            v.tensor_tensor(OUT[0:P_, 0:N], OUT[0:P_, 0:N], SGN[0:P_, 0:N], ALU.mult)
            v.tensor_tensor(OUT[0:P_, 0:N].rearrange("p (b k) -> p b k", k=16),
                            OUT[0:P_, 0:N].rearrange("p (b k) -> p b k", k=16),
                            SCL[0:P_, 0:nb].unsqueeze(2).broadcast_to([P_, nb, 16]), ALU.mult)
            inc(nc.vector.drain(), "pq", 1)

    @blk.sync
    def _(s):
        wait(s, "ph1")
        for u in range(4):
            inc(s.dma_start(out=PIN[:, u * 1024:(u + 1) * 1024], in_=t["w1_sl"][u * 128:(u + 1) * 128, :]), "pwld", 16)
    @blk.vector
    def _(v):
        wait(v, "pwld")
        v.tensor_reduce(AMX[:, 0:4].unsqueeze(2), PIN[:, 0:4096].rearrange("p (a n) -> p a n", a=4), AX, ALU.max, apply_absolute_value=True)
        inc(v.tensor_reduce(AMX[:, 4:5].unsqueeze(2), AMX[:, 0:4].unsqueeze(1), AX, ALU.max, apply_absolute_value=True), "pabs", 1)
    @blk.sync
    def _(s):
        wait(s, "pabs")
        inc(s.dma_start(out=PIN[:, 0:4096], in_=t["w2_sl"][:, :]), "pwld", 16)
    @blk.vector
    def _(v):
        wait(v, "pwld")
        inc(v.tensor_reduce(AMX[:, 5:6].unsqueeze(2), PIN[:, 0:4096].unsqueeze(1), AX, ALU.max, apply_absolute_value=True), "pabs", 1)
    @blk.tensor
    def _(te):
        wait(te, "pabs")
        inc(te.transpose(PS[0][0:2, 0:128], AMX[:, 4:6], IDN[:, :]), "pabt", 1)
    @blk.vector
    def _(v):
        wait(v, "pabt")
        inc(v.tensor_reduce(AMX[0:2, 1:2].unsqueeze(2), PS[0][0:2, 0:128].unsqueeze(1), AX, ALU.max), "pabe2", 1)
    @blk.gpsimd
    def _(g):
        wait(g, "pabe2")
        g.memset(AMX[0:1, 2:16], 0.0)
        inc(g.dma_start(out=t["ag1_in"][0:1, 2:16], in_=AMX[0:1, 2:16]), "ag1w", 16)
        inc(g.dma_start(out=t["ag1_in"][0:1, 0:2], in_=AMX[0:2, 1:2]), "ag1w", 16)
        wait(g, "ag1w")
        inc(g.collective_compute("AllGather", ALU.bypass, replica_groups=[list(range(8))],
                                 ins=[t["ag1_in"].opt()], outs=[t["ag1_out"].opt()]), "cc1", 1)
    @blk.sync
    def _(s):
        wait(s, "cc1")
        inc(s.dma_start(out=AMX[0:8, 0:2], in_=t["ag1_out"][:, 0:2]), "ag1r", 16)
    @blk.tensor
    def _(te):
        wait(te, "ag1r")
        inc(te.transpose(PS[0][0:2, 0:8], AMX[0:8, 0:2], IDN[0:8, 0:8]), "pabt", 1)
    @blk.vector
    def _(v):
        wait(v, "pabt")
        inc(v.tensor_reduce(AMX[0:2, 8:9].unsqueeze(2), PS[0][0:2, 0:8].unsqueeze(1), AX, ALU.max), "pabe", 1)
    @blk.tensor
    def _(te):
        wait(te, "pabe")
        inc(te.transpose(PS[1][0:1, 0:2], AMX[0:2, 8:9], IDN[0:2, 0:2]), "pabt", 1)
    @blk.vector
    def _(v):
        wait(v, "pabt")
        v.tensor_copy(RC6[:, 8:10], PS[1][0:1, 0:2])
        v.tensor_tensor(RC6[:, 10:12], RC6[:, 8:10], RC6[:, 5:6].unsqueeze(2).broadcast_to([1, 2, 1])[:, :, 0], ALU.mult)
        v.tensor_single_scalar(RC6[:, 10:12], RC6[:, 10:12], 1e-8, ALU.max)
        v.reciprocal(RC6[:, 12:14], RC6[:, 10:12])
        inc(nc.vector.drain(), "pabe", 1)
    @blk.tensor
    def _(te):
        wait(te, "pabe")
        inc(te.matmul(PS[1][:, 0:8], ONES1[:, :], RC6[:, 8:16], start=True, stop=True), "pabt", 1)
    @blk.vector
    def _(v):
        wait(v, "pabt")
        inc(v.tensor_copy(RSB[:, :], PS[1][:, 0:8]), "agc1", 1)

    def prep_fp4_slice(src_name, nrows, NN, off_base, colstride, col0_fn):
        nsub = (nrows + 127) // 128
        for u in range(nsub):
            p_ = min(128, nrows - u * 128)
            @blk.sync
            def _(s, u=u, p_=p_):
                if "pq" in sems:
                    wait(s, "pq")
                inc(s.dma_start(out=PIN[0:p_, 0:NN], in_=t[src_name][u * 128:u * 128 + p_, :]), "pwld", 16)
            def ew(v, u=u):
                wait(v, "pwld")
            emit_fp4q(p_, NN, PIN, POUT, PSCL, PTMP, PTI, PSGN, PQz, PRz, PPP, extra_wait=ew)
            for kt in range(NN // 128):
                @blk.tensor
                def _(te, u=u, p_=p_, kt=kt):
                    wait(te, "pq")
                    if "pev" in sems and kt % 4 == 0:
                        wait(te, "pev")
                    inc(te.transpose(PS[kt % 4][:, 0:p_], POUT[0:p_, kt * 128:(kt + 1) * 128], IDN[0:p_, 0:p_]), "ptr", 1)
                @blk.vector
                def _(v, u=u, p_=p_, kt=kt):
                    wait(v, "ptr")
                    if "ag2w" in sems and kt % 4 == 0:
                        wait(v, "ag2w")
                    inc(v.tensor_copy(PST[:, (kt % 4) * 256:(kt % 4) * 256 + p_], PS[kt % 4][:, 0:p_]), "pev", 1)
                @blk.gpsimd
                def _(g, u=u, p_=p_, kt=kt):
                    wait(g, "pev")
                    dst = bass_rust.AP(t["ag2a_in"].tensor,
                                      off_base + kt * 128 * colstride + col0_fn(u),
                                      [[colstride, 128], [1, p_]])
                    inc(g.dma_start(out=dst, in_=PST[:, (kt % 4) * 256:(kt % 4) * 256 + p_]), "ag2w", 16)
    prep_fp4_slice("wg_sl", 352, D, OFF_G, 352, lambda u: u * 128)
    prep_fp4_slice("wu_sl", 352, D, OFF_U, 352, lambda u: u * 128)
    prep_fp4_slice("wd_sl", 128, I0, OFF_DN, 128, lambda u: 0)

    def prep_int_slice(src_name, nrows, NN, off_base, colstride, amx_row):
        nsub = (nrows + 127) // 128
        for u in range(nsub):
            @blk.sync
            def _(s, u=u):
                if "w1q" in sems:
                    wait(s, "w1q")
                else:
                    wait(s, "pq")
                inc(s.dma_start(out=PIN[:, 0:NN], in_=t[src_name][u * 128:(u + 1) * 128, :]), "pwld", 16)
            @blk.vector
            def _(v, u=u):
                wait(v, "pwld"); wait(v, "agc1")
                v.tensor_scalar(PTMP[:, 0:NN], PIN[:, 0:NN], RSB[:, 4 + amx_row:5 + amx_row], None, ALU.mult)
                v.tensor_copy(PTI[:, 0:NN], PTMP[:, 0:NN])
                v.tensor_copy(PQz[:, 0:NN], PTI[:, 0:NN])
                v.tensor_tensor(PRz[:, 0:NN], PTMP[:, 0:NN], PQz[:, 0:NN], ALU.subtract)
                v.tensor_single_scalar(PRz[:, 0:NN], PRz[:, 0:NN], 0.5, ALU.is_ge)
                v.tensor_tensor(PQz[:, 0:NN], PQz[:, 0:NN], PRz[:, 0:NN], ALU.add)
                v.tensor_tensor(PRz[:, 0:NN], PQz[:, 0:NN], PTMP[:, 0:NN], ALU.subtract)
                v.tensor_single_scalar(PRz[:, 0:NN], PRz[:, 0:NN], 0.5, ALU.is_gt)
                v.tensor_tensor(PQz[:, 0:NN], PQz[:, 0:NN], PRz[:, 0:NN], ALU.subtract)
                v.tensor_scalar(PQz[:, 0:NN], PQz[:, 0:NN], 127.0, -127.0, ALU.min, ALU.max)
                inc(nc.vector.drain(), "w1q", 1)
            for kt in range(NN // 128):
                @blk.tensor
                def _(te, u=u, kt=kt):
                    wait(te, "w1q")
                    if "w1ev" in sems and kt % 4 == 0:
                        wait(te, "w1ev")
                    inc(te.transpose(PS[4 + kt % 4][:, 0:128], PQz[:, kt * 128:(kt + 1) * 128], IDN[:, :]), "w1tr", 1)
                @blk.vector
                def _(v, u=u, kt=kt):
                    wait(v, "w1tr")
                    if "ag2w" in sems and kt % 4 == 0:
                        wait(v, "ag2w")
                    inc(v.tensor_copy(PSTB[:, (kt % 4) * 128:(kt % 4) * 128 + 128], PS[4 + kt % 4][:, 0:128]), "w1ev", 1)
                @blk.gpsimd
                def _(g, u=u, kt=kt):
                    wait(g, "w1ev")
                    dst = bass_rust.AP(t["ag2b_in"].tensor,
                                      off_base + kt * 128 * colstride + u * 128,
                                      [[colstride, 128], [1, 128]])
                    inc(g.dma_start(out=dst, in_=PSTB[:, (kt % 4) * 128:(kt % 4) * 128 + 128]), "ag2w", 16)
    prep_int_slice("w1_sl", 512, D, OFF_W1, 512, 0)
    prep_int_slice("w2_sl", 128, I1, OFF_W2, 128, 1)

    @blk.gpsimd
    def _(g):
        wait(g, "ag2w")
        inc(g.collective_compute("AllGather", ALU.bypass, replica_groups=[list(range(8))],
                                 ins=[t["ag2a_in"].opt()], outs=[t["ag2a_out"].opt()]), "cc2a", 1)
        inc(g.collective_compute("AllGather", ALU.bypass, replica_groups=[list(range(8))],
                                 ins=[t["ag2b_in"].opt()], outs=[t["ag2b_out"].opt()]), "cc2b", 1)

    @blk.gpsimd
    def _(g):
        wait(g, "ag2w"); wait(g, "pwld"); wait(g, "ag1w"); wait(g, "ag1r")
        wait(g, "w1q"); wait(g, "pq"); wait(g, "agc1")
        inc(nc.gpsimd.drain(), "ph2", 1)

    stp.close()

    if upto == "prep":
        @blk.gpsimd
        def _(g):
            wait(g, "cc2a"); wait(g, "cc2b")
            with nc.sbuf_tensor("ZOUT", [128, 1024], F32) as ZO:
                g.memset(ZO[:, :], 0.0)
                for i2 in range(16):
                    inc(g.dma_start(out=t["out"][i2 * 128:(i2 + 1) * 128, :], in_=ZO[:, :]), "outw", 16)
                wait(g, "outw")
                inc(nc.gpsimd.drain(), "done", 1)
        return nc, t, stack, sems

    stf = ExitStack()
    sbf = lambda name, shape, dtype=F32: stf.enter_context(nc.sbuf_tensor(name, shape, dtype))
    XF = sbf("XF", [128, 2048])
    XL2 = sbf("XL2", [128, 1024])
    FOUT = sbf("FOUT", [128, 1024])
    FT = sbf("FT", [128, 1024])
    FTI = sbf("FTI", [128, 1024], I32)
    FSGN = sbf("FSGN", [128, 1024])
    FQ2 = sbf("FQ2", [128, 1024])
    FR2 = sbf("FR2", [128, 1024])
    FP2 = sbf("FP2", [128, 1024])
    FSC = sbf("FSC", [128, 144])
    MU2 = sbf("MU2", [128, 8])
    RCB2 = sbf("RCB2", [128, 16])
    HQ = sbf("HQ", [128, 352])
    HB = sbf("HB", [128, 704])
    SIL = sbf("SIL", [128, 352])
    HS = sbf("HS", [128, 384])

    @blk.gpsimd
    def _(g):
        wait(g, "ph2")
        g.memset(FSC[0:1, 128:144], 0.0)
        g.memset(FSC[0:1, 128:129], 6.0)
        inc(nc.gpsimd.drain(), "cst", 1)
    @blk.vector
    def _(v):
        wait(v, "cst")
        inc(v.reciprocal(FSC[0:1, 129:130], FSC[0:1, 128:129]), "rcp6", 1)
    @blk.tensor
    def _(te):
        wait(te, "rcp6"); wait(te, "pev"); wait(te, "w1ev")
        inc(te.matmul(PS[1][:, 0:16], ONES1[:, :], FSC[0:1, 128:144], start=True, stop=True), "pabt", 1)
    @blk.vector
    def _(v):
        wait(v, "pabt")
        inc(v.tensor_copy(RCB2[:, :], PS[1][:, 0:16]), "rcp6", 1)

    def emit_fp4q2(N, IN, OUT, extra_wait=None):
        nb = N // 16
        @blk.vector
        def _(v):
            if extra_wait is not None:
                extra_wait(v)
            v.tensor_reduce(FSC[:, 0:nb].unsqueeze(2), IN[:, 0:N].rearrange("p (b k) -> p b k", k=16),
                            AX, ALU.max, apply_absolute_value=True)
            v.tensor_scalar(FSC[:, 0:nb], FSC[:, 0:nb], RCB2[:, 1:2], 1e-12, ALU.mult, ALU.max)
            v.reciprocal(FSC[:, 64:64 + nb], FSC[:, 0:nb])
            v.tensor_tensor(FT[:, 0:N].rearrange("p (b k) -> p b k", k=16),
                            IN[:, 0:N].rearrange("p (b k) -> p b k", k=16),
                            FSC[:, 64:64 + nb].unsqueeze(2).broadcast_to([128, nb, 16]), ALU.mult)
            inc(nc.vector.drain(), "f2q", 1)
        @blk.scalar
        def _(sc):
            wait(sc, "f2q")
            sc.activation(FSGN[:, 0:N], FT[:, 0:N], AF.Sign)
            inc(sc.activation(FT[:, 0:N], FT[:, 0:N], AF.Abs), "f2st", 1)
        @blk.vector
        def _(v):
            wait(v, "f2st")
            v.tensor_scalar(OUT[:, 0:N], FT[:, 0:N], 2.0, 0.5, ALU.mult, ALU.add)
            v.tensor_copy(FTI[:, 0:N], OUT[:, 0:N])
            v.tensor_copy(FQ2[:, 0:N], FTI[:, 0:N])
            v.tensor_tensor(FR2[:, 0:N], FQ2[:, 0:N], OUT[:, 0:N], ALU.is_gt)
            v.tensor_tensor(FQ2[:, 0:N], FQ2[:, 0:N], FR2[:, 0:N], ALU.subtract)
            v.tensor_scalar(OUT[:, 0:N], FT[:, 0:N], 1.0, 0.5, ALU.mult, ALU.add)
            v.tensor_copy(FTI[:, 0:N], OUT[:, 0:N])
            v.tensor_copy(FR2[:, 0:N], FTI[:, 0:N])
            v.tensor_tensor(FP2[:, 0:N], FR2[:, 0:N], OUT[:, 0:N], ALU.is_gt)
            v.tensor_tensor(FR2[:, 0:N], FR2[:, 0:N], FP2[:, 0:N], ALU.subtract)
            v.tensor_scalar(OUT[:, 0:N], FQ2[:, 0:N], 0.5, None, ALU.mult)
            v.tensor_single_scalar(FP2[:, 0:N], FT[:, 0:N], 2.25, ALU.is_gt)
            v.tensor_tensor(FQ2[:, 0:N], FR2[:, 0:N], OUT[:, 0:N], ALU.subtract)
            v.tensor_tensor(FP2[:, 0:N], FP2[:, 0:N], FQ2[:, 0:N], ALU.mult)
            v.tensor_tensor(OUT[:, 0:N], OUT[:, 0:N], FP2[:, 0:N], ALU.add)
            v.tensor_single_scalar(FP2[:, 0:N], FT[:, 0:N], 4.5, ALU.is_gt)
            v.tensor_single_scalar(FQ2[:, 0:N], FT[:, 0:N], 5.0, ALU.is_gt)
            v.tensor_scalar(FQ2[:, 0:N], FQ2[:, 0:N], 2.0, 4.0, ALU.mult, ALU.add)
            v.tensor_tensor(FQ2[:, 0:N], FQ2[:, 0:N], FR2[:, 0:N], ALU.subtract)
            v.tensor_tensor(FP2[:, 0:N], FP2[:, 0:N], FQ2[:, 0:N], ALU.mult)
            v.tensor_tensor(OUT[:, 0:N], OUT[:, 0:N], FP2[:, 0:N], ALU.add)
            v.tensor_tensor(OUT[:, 0:N], OUT[:, 0:N], FSGN[:, 0:N], ALU.mult)
            v.tensor_tensor(OUT[:, 0:N].rearrange("p (b k) -> p b k", k=16),
                            OUT[:, 0:N].rearrange("p (b k) -> p b k", k=16),
                            FSC[:, 0:nb].unsqueeze(2).broadcast_to([128, nb, 16]), ALU.mult)
            inc(nc.vector.drain(), "f2q", 1)

    stf1 = ExitStack()
    sbf1 = lambda name, shape, dtype=F32: stf1.enter_context(nc.sbuf_tensor(name, shape, dtype))
    QXT = sbf1("QXT", [128, 16384])
    WG = sbf1("WG", [128, 2816])
    WU = sbf1("WU", [128, 2816])

    for i in range(16):
        @blk.sync
        def _(s, i=i):
            wait(s, "ph2")
            if i >= 2:
                wait(s, "f2q", sems["f2q"] - 1)
            inc(s.dma_start(out=XF[:, (i % 2) * 1024:(i % 2 + 1) * 1024], in_=t["resid1"][i]), "x2ldA" if i % 2 == 0 else "x2ldB", 16)
        @blk.scalar
        def _(sc, i=i):
            wait(sc, "x2ldA" if i % 2 == 0 else "x2ldB", 16 * (i // 2 + 1))
            sc.activation(XL2[:, :], XF[:, (i % 2) * 1024:(i % 2 + 1) * 1024], AF.Identity, accum_out=MU2[:, 0:1])
            inc(sc.activation(XL2[:, :], XF[:, (i % 2) * 1024:(i % 2 + 1) * 1024], AF.Square, accum_out=MU2[:, 1:2]), "f2st", 1)
        @blk.vector
        def _(v, i=i):
            wait(v, "f2st")
            v.tensor_scalar(MU2[:, 2:3], MU2[:, 0:1], 1.0 / D, None, ALU.mult)
            v.tensor_scalar(MU2[:, 3:4], MU2[:, 1:2], 1.0 / D, None, ALU.mult)
            v.tensor_tensor(MU2[:, 4:5], MU2[:, 2:3], MU2[:, 2:3], ALU.mult)
            v.tensor_tensor(MU2[:, 3:4], MU2[:, 3:4], MU2[:, 4:5], ALU.subtract)
            inc(v.tensor_scalar(MU2[:, 3:4], MU2[:, 3:4], LN_EPS, None, ALU.add), "f2ln", 1)
        @blk.scalar
        def _(sc, i=i):
            wait(sc, "f2ln")
            inc(sc.activation(MU2[:, 3:4], MU2[:, 3:4], AF.Sqrt), "f2st", 1)
        @blk.vector
        def _(v, i=i):
            wait(v, "f2st")
            v.reciprocal(MU2[:, 3:4], MU2[:, 3:4])
            v.tensor_tensor(MU2[:, 4:5], MU2[:, 2:3], MU2[:, 3:4], ALU.mult)
            inc(v.tensor_scalar(MU2[:, 4:5], MU2[:, 4:5], -1.0, None, ALU.mult), "f2ln", 1)
        @blk.scalar
        def _(sc, i=i):
            wait(sc, "f2ln")
            inc(sc.activation(XL2[:, :], XF[:, (i % 2) * 1024:(i % 2 + 1) * 1024], AF.Identity,
                              bias=MU2[:, 4:5], scale=MU2[:, 3:4]), "f2st", 1)
        def ew(v, i=i):
            wait(v, "f2st")
        emit_fp4q2(1024, XL2, FOUT, extra_wait=ew)
        for jp in range(2):
            @blk.tensor
            def _(te, i=i, jp=jp):
                wait(te, "f2q")
                if "f2ev" in sems:
                    wait(te, "f2ev")
                mm = None
                for j in range(4):
                    mm = te.transpose(PS[jp][:, j * 128:(j + 1) * 128], FOUT[:, jp * 512 + j * 128:jp * 512 + (j + 1) * 128], IDN[:, :])
                inc(mm, "f2tr", 1)
            @blk.vector
            def _(v, i=i, jp=jp):
                wait(v, "f2tr")
                inc(v.tensor_copy(QXT[:, i * 1024 + jp * 512:i * 1024 + (jp + 1) * 512], PS[jp][:, :]), "f2ev", 1)

    QTICK = {}
    HTICK = {}
    PREV2 = {}
    _seq = [(c_, i_) for c_ in range(8) for i_ in range(16)]
    for _k in range(2, len(_seq)):
        PREV2[_seq[_k]] = _seq[_k - 2]
    for c in range(8):
        @blk.sync
        def _(s, c=c):
            if c > 0:
                wait(s, "gmm")
            else:
                wait(s, "cc2a")
            for j in range(8):
                src_g = bass_rust.AP(t["ag2a_out"].tensor, c * AG2N + OFF_G + j * 128 * 352, [[352, 128], [1, 352]])
                src_u = bass_rust.AP(t["ag2a_out"].tensor, c * AG2N + OFF_U + j * 128 * 352, [[352, 128], [1, 352]])
                inc(s.dma_start(out=WG[:, j * 352:(j + 1) * 352], in_=src_g), "wgldA", 16)
                inc(s.dma_start(out=WU[:, j * 352:(j + 1) * 352], in_=src_u), "wgldA", 16)
        def emit_tail(c, i):
            off = (i % 2) * 352
            @blk.tensor
            def _(te, c=c, i=i, off=off):
                wait(te, "f2q", QTICK[(c, i)])
                if "hqev" in sems:
                    wait(te, "hqev")
                te.transpose(PS[4][:, 0:128], HB[:, off:off + 128], IDN[:, :])
                te.transpose(PS[5][:, 0:128], HB[:, off + 128:off + 256], IDN[:, :])
                inc(te.transpose(PS[6][0:96, 0:128], HB[:, off + 256:off + 352], IDN[:, :]), "hqtr", 1)
            @blk.vector
            def _(v, c=c, i=i):
                wait(v, "hqtr")
                if "hqpk" in sems:
                    wait(v, "hqpk", sems["hqpk"])
                v.tensor_copy(HS[:, 0:128], PS[4][:, 0:128])
                v.tensor_copy(HS[:, 128:256], PS[5][:, 0:128])
                inc(v.tensor_copy(HS[0:96, 256:384], PS[6][0:96, 0:128]), "hqev", 1)
            @blk.gpsimd
            def _(g, c=c, i=i):
                wait(g, "hqev")
                inc(g.dma_start(out=t["hqt_park"][c, 0, :, i * 128:(i + 1) * 128], in_=HS[:, 0:128]), "hqpk", 16)
                inc(g.dma_start(out=t["hqt_park"][c, 1, :, i * 128:(i + 1) * 128], in_=HS[:, 128:256]), "hqpk", 16)
                inc(g.dma_start(out=t["hqt_park"][c, 2, 0:96, i * 128:(i + 1) * 128], in_=HS[0:96, 256:384]), "hqpk", 16)
        for i in range(16):
            k14 = c * 16 + i
            bg = 2 if k14 % 2 == 0 else 0
            bu = 3 if k14 % 2 == 0 else 1
            @blk.tensor
            def _(te, c=c, i=i, k14=k14, bg=bg, bu=bu):
                wait(te, "wgldA")
                wait(te, "f2ev")
                if k14 >= 2:
                    wait(te, "hmul", sems["hmul"] - 1)
                mm = None
                for j in range(8):
                    st_ = (j == 0); sp_ = (j == 7)
                    lhsT = QXT[:, i * 1024 + j * 128:i * 1024 + (j + 1) * 128]
                    te.matmul(PS[bg][:, 0:352], lhsT, WG[:, j * 352:(j + 1) * 352], start=st_, stop=sp_)
                    mm = te.matmul(PS[bu][:, 0:352], lhsT, WU[:, j * 352:(j + 1) * 352], start=st_, stop=sp_)
                inc(mm, "gmm", 1)
            @blk.scalar
            def _(sc, c=c, i=i, k14=k14, bg=bg):
                wait(sc, "gmm", sems["gmm"])
                inc(sc.activation(SIL[:, :], PS[bg][:, 0:352], AF.Silu), "silu", 1)
            @blk.vector
            def _(v, c=c, i=i, bu=bu):
                wait(v, "silu")
                inc(v.tensor_tensor(HQ[:, :], SIL[:, :], PS[bu][:, 0:352], ALU.mult), "hmul", 1)
            def ewh(v, c=c, i=i):
                wait(v, "hmul")
                if "hqtr" in sems and (c, i) in PREV2:
                    wait(v, "hqtr", HTICK[PREV2[(c, i)]])
            emit_fp4q2(352, HQ, HB[:, (i % 2) * 352:(i % 2) * 352 + 352], extra_wait=ewh)
            QTICK[(c, i)] = sems["f2q"]
            if i > 0:
                emit_tail(c, i - 1)
                HTICK[(c, i - 1)] = sems["hqtr"]
        emit_tail(c, 15)
        HTICK[(c, 15)] = sems["hqtr"]

    stf1.close()
    stf2 = ExitStack()
    sbf2 = lambda name, shape, dtype=F32: stf2.enter_context(nc.sbuf_tensor(name, shape, dtype))
    WD = sbf2("WD", [128, 24 * 1024])
    HQT = sbf2("HQT", [128, 2 * 3072])
    PCS = [(cc_, sub, 96 if sub == 2 else 128) for cc_ in range(8) for sub in range(3)]
    @blk.sync
    def _(s):
        wait(s, "cc2a")
        wait(s, "hqpk")
        wait(s, "gmm")
        wait(s, "hqev")
        for pi, (cc_, sub, psz) in enumerate(PCS):
            kg = cc_ * 352 + sub * 128
            for cp in range(8):
                srcw = bass_rust.AP(t["ag2a_out"].tensor, cp * AG2N + OFF_DN + kg * 128, [[128, psz], [1, 128]])
                inc(s.dma_start(out=WD[0:psz, pi * 1024 + cp * 128:pi * 1024 + (cp + 1) * 128], in_=srcw), "wdld", 16)
    for i in range(16):
        @blk.sync
        def _(s, i=i):
            if i == 0:
                wait(s, "hqpk")
            if i >= 2:
                wait(s, "dmm")
            nmh = "hqldA" if i % 2 == 0 else "hqldB"
            off = (i % 2) * 3072
            for pi, (cc_, sub, psz) in enumerate(PCS):
                inc(s.dma_start(out=HQT[0:psz, off + pi * 128:off + (pi + 1) * 128],
                                in_=t["hqt_park"][cc_, sub, 0:psz, i * 128:(i + 1) * 128]), nmh, 16)
            inc(s.dma_start(out=XF[:, (i % 2) * 1024:(i % 2 + 1) * 1024], in_=t["resid1"][i]),
                "r2ldA" if i % 2 == 0 else "r2ldB", 16)
        @blk.tensor
        def _(te, i=i):
            wait(te, "hqldA" if i % 2 == 0 else "hqldB", 24 * 16 * (i // 2 + 1))
            wait(te, "wdld")
            if i > 0:
                wait(te, "dev")
            off = (i % 2) * 3072
            mm = None
            for pi, (cc_, sub, psz) in enumerate(PCS):
                st_ = (pi == 0); sp_ = (pi == 23)
                lhsT = HQT[0:psz, off + pi * 128:off + (pi + 1) * 128]
                te.matmul(PS[2][:, :], lhsT, WD[0:psz, pi * 1024:pi * 1024 + 512], start=st_, stop=sp_)
                mm = te.matmul(PS[3][:, :], lhsT, WD[0:psz, pi * 1024 + 512:pi * 1024 + 1024], start=st_, stop=sp_)
            inc(mm, "dmm", 1)
        @blk.vector
        def _(v, i=i):
            wait(v, "dmm")
            wait(v, "r2ldA" if i % 2 == 0 else "r2ldB", 16 * (i // 2 + 1))
            if i > 0:
                wait(v, "r2w", 16 * i)
                if upto == "l0f":
                    wait(v, "o2w", 16 * i)
            v.tensor_tensor(XL2[:, 0:512], XF[:, (i % 2) * 1024:(i % 2) * 1024 + 512], PS[2][:, :], ALU.add)
            inc(v.tensor_tensor(XL2[:, 512:1024], XF[:, (i % 2) * 1024 + 512:(i % 2 + 1) * 1024], PS[3][:, :], ALU.add), "dev", 1)
        @blk.gpsimd
        def _(g, i=i):
            wait(g, "dev")
            inc(g.dma_start(out=t["resid2"][i], in_=XL2[:, :]), "r2w", 16)
            if upto == "l0f":
                inc(g.dma_start(out=t["out"][i * 128:(i + 1) * 128, :], in_=XL2[:, :]), "o2w", 16)

    if upto == "l0f":
        @blk.gpsimd
        def _(g):
            wait(g, "o2w"); wait(g, "r2w")
            inc(nc.gpsimd.drain(), "done", 1)
        stf2.close(); stf.close()
        return nc, t, stack, sems

    stf2.close(); stf.close()

    stq = ExitStack()
    sbq = lambda n_, sh, dt_=F32: stq.enter_context(nc.sbuf_tensor(n_, sh, dt_))
    NMT = sbq("NMT", [64, 1792])
    O_DTR, O_DTI, O_CRE, O_CIM = 0, 128, 256, 384
    O_PW = {8: 512, 16: 768, 32: 1024, 64: 1280, 128: 1536}
    ATB = sbq("ATB", [128, 256])
    ARB = sbq("ARB", [128, 256])
    XHB = sbq("XHB", [128, 256])
    HTB = sbq("HTB", [128, 256])
    URB = sbq("URB", [128, 128])
    UIB = sbq("UIB", [128, 128])
    TS1 = sbq("TS1", [64, 128])
    TS2 = sbq("TS2", [64, 128])
    IOT8I = stq.enter_context(nc.sbuf_tensor("IOT8I", [64, 8], I32))
    IOT8 = sbq("IOT8", [64, 8])

    def emit_cw(v, VW, KI, KF):
        v.tensor_scalar(KF, VW, INV2PI, None, ALU.mult)
        v.tensor_copy(KI, KF)
        v.tensor_copy(KF, KI)
        v.scalar_tensor_tensor(VW, KF, -CW1, VW, ALU.mult, ALU.add)
        v.scalar_tensor_tensor(VW, KF, -CW2, VW, ALU.mult, ALU.add)
        v.scalar_tensor_tensor(VW, KF, -CW3, VW, ALU.mult, ALU.add)
        v.tensor_single_scalar(KF, VW, PI, ALU.is_gt)
        v.scalar_tensor_tensor(VW, KF, -(CW1 + CW2), VW, ALU.mult, ALU.add)
        v.tensor_single_scalar(KF, VW, -PI, ALU.is_lt)
        v.scalar_tensor_tensor(VW, KF, (CW1 + CW2), VW, ALU.mult, ALU.add)
        v.tensor_scalar(VW, VW, 3.1415925, -3.1415925, ALU.min, ALU.max)

    st5 = ExitStack()
    sb5 = lambda n_, sh, dt_=F32: st5.enter_context(nc.sbuf_tensor(n_, sh, dt_))
    XF3 = sb5("XF3", [128, 2048])
    XL3 = sb5("XL3", [128, 1024])
    XT3 = sb5("XT3", [128, 1024])
    MU3 = sb5("MU3", [128, 8])
    for i in range(16):
        @blk.sync
        def _(s, i=i):
            if i == 0:
                wait(s, "r2w")
            if i >= 2:
                wait(s, "s3st", sems["s3st"] - 3)
            inc(s.dma_start(out=XF3[:, (i % 2) * 1024:(i % 2 + 1) * 1024], in_=t["resid2"][i]),
                "x3ldA" if i % 2 == 0 else "x3ldB", 16)
        @blk.scalar
        def _(sc, i=i):
            wait(sc, "x3ldA" if i % 2 == 0 else "x3ldB", sems["x3ldA" if i % 2 == 0 else "x3ldB"])
            sc.activation(XL3[:, :], XF3[:, (i % 2) * 1024:(i % 2 + 1) * 1024], AF.Identity, accum_out=MU3[:, 0:1])
            inc(sc.activation(XL3[:, :], XF3[:, (i % 2) * 1024:(i % 2 + 1) * 1024], AF.Square, accum_out=MU3[:, 1:2]), "s3st", 1)
        @blk.vector
        def _(v, i=i):
            wait(v, "s3st", sems["s3st"])
            v.tensor_scalar(MU3[:, 2:3], MU3[:, 0:1], 1.0 / D, None, ALU.mult)
            v.tensor_scalar(MU3[:, 3:4], MU3[:, 1:2], 1.0 / D, None, ALU.mult)
            v.tensor_tensor(MU3[:, 4:5], MU3[:, 2:3], MU3[:, 2:3], ALU.mult)
            v.tensor_tensor(MU3[:, 3:4], MU3[:, 3:4], MU3[:, 4:5], ALU.subtract)
            inc(v.tensor_scalar(MU3[:, 3:4], MU3[:, 3:4], LN_EPS, None, ALU.add), "s3ln", 1)
        @blk.scalar
        def _(sc, i=i):
            wait(sc, "s3ln", sems["s3ln"])
            inc(sc.activation(MU3[:, 3:4], MU3[:, 3:4], AF.Sqrt), "s3st", 1)
        @blk.vector
        def _(v, i=i):
            wait(v, "s3st", sems["s3st"])
            v.reciprocal(MU3[:, 3:4], MU3[:, 3:4])
            v.tensor_tensor(MU3[:, 4:5], MU3[:, 2:3], MU3[:, 3:4], ALU.mult)
            inc(v.tensor_scalar(MU3[:, 4:5], MU3[:, 4:5], -1.0, None, ALU.mult), "s3ln", 1)
        @blk.scalar
        def _(sc, i=i):
            wait(sc, "s3ln", sems["s3ln"])
            inc(sc.activation(XL3[:, :], XF3[:, (i % 2) * 1024:(i % 2 + 1) * 1024], AF.Identity,
                              bias=MU3[:, 4:5], scale=MU3[:, 3:4]), "s3st", 1)
        @blk.tensor
        def _(te, i=i):
            wait(te, "s3st", sems["s3st"])
            if i > 0:
                wait(te, "s3ev", sems["s3ev"])
            for j in range(4):
                te.transpose(PS[0][:, j * 128:(j + 1) * 128], XL3[:, j * 128:(j + 1) * 128], IDN[:, :])
            for j in range(3):
                te.transpose(PS[1][:, j * 128:(j + 1) * 128], XL3[:, 512 + j * 128:512 + (j + 1) * 128], IDN[:, :])
            inc(te.transpose(PS[1][:, 384:512], XL3[:, 896:1024], IDN[:, :]), "s3tr", 1)
        @blk.vector
        def _(v, i=i):
            wait(v, "s3tr", sems["s3tr"])
            if i > 0:
                wait(v, "agw4", sems["agw4"])
            v.tensor_copy(XT3[:, 0:512], PS[0][:, :])
            inc(v.tensor_copy(XT3[:, 512:1024], PS[1][:, :]), "s3ev", 1)
        @blk.sync
        def _(s, i=i):
            wait(s, "s3ev", sems["s3ev"])
            for j in range(8):
                inc(s.dma_start(out=t["ag4_in"][j * 128:(j + 1) * 128, i * 128:(i + 1) * 128],
                                in_=XT3[:, j * 128:(j + 1) * 128]), "agw4", 16)
    @blk.gpsimd
    def _(g_):
        wait(g_, "agw4")
        inc(g_.collective_compute("AllToAll", ALU.bypass, replica_groups=[list(range(8))],
                                  ins=[t["ag4_in"].opt()], outs=[t["ag4_out"].opt()]), "cc4", 1)
    @blk.gpsimd
    def _(g_):
        wait(g_, "cc4")
        for j in range(8):
            src = bass_rust.AP(t["ag4_out"].tensor, j * 128 * 2048, [[2048, 128], [1, 2048]])
            dst = bass_rust.AP(t["xch_d"].tensor, j * 2048, [[16384, 128], [1, 2048]])
            inc(g_.dma_start(out=dst, in_=src), "xchw", 16)
    st5.close()

    stcu = ExitStack()
    sbc = lambda n_, sh, dt_=F32: stcu.enter_context(nc.sbuf_tensor(n_, sh, dt_))
    LARc = sbc("LARc", [128, 64]); AIMc = sbc("AIMc", [128, 64])
    CREc = sbc("CREc", [128, 64]); CIMc = sbc("CIMc", [128, 64])
    LDTc = sbc("LDTc", [128, 1]); DSLc = sbc("DSLc", [128, 1]); DTTc = sbc("DTTc", [128, 1])
    AREc = sbc("AREc", [128, 64]); DTAR = sbc("DTAR", [128, 64]); DTAI = sbc("DTAI", [128, 64])
    ERSc = sbc("ERSc", [128, 64]); ARG2 = sbc("ARG2", [128, 128])
    KI2 = stcu.enter_context(nc.sbuf_tensor("KI2", [128, 128], I32))
    KF2 = sbc("KF2", [128, 128]); SC2 = sbc("SC2", [128, 128])
    EDRE = sbc("EDRE", [128, 64]); EDIM = sbc("EDIM", [128, 64])
    T64A = sbc("T64A", [128, 64]); T64B = sbc("T64B", [128, 64]); T64C = sbc("T64C", [128, 64])
    PRODRE = sbc("PRODRE", [128, 64]); PRODIM = sbc("PRODIM", [128, 64]); DENc = sbc("DENc", [128, 64])
    C2RE = sbc("C2RE", [128, 64]); C2IM = sbc("C2IM", [128, 64])
    PIRc = sbc("PIRc", [128, 64]); PIIc = sbc("PIIc", [128, 64])
    CBRE = sbc("CBRE", [128, 8256]); CBIM = sbc("CBIM", [128, 8256])
    TD1c = sbc("TD1c", [128, 4160]); TD2c = sbc("TD2c", [128, 4160])
    KKc = sbc("KKc", [128, 132])

    @blk.sync
    def _(s):
        wait(s, "agw4")
        for nm_, dst in (("lar_sl", LARc), ("aim_sl", AIMc), ("cre_sl", CREc), ("cim_sl", CIMc),
                         ("ldt_sl", LDTc), ("d_sl", DSLc)):
            inc(s.dma_start(out=dst[:, :], in_=t[nm_][:, :]), "prm4", 16)
    @blk.scalar
    def _(sc):
        wait(sc, "prm4", sems["prm4"])
        sc.activation(DTTc[:, :], LDTc[:, :], AF.Exp)
        inc(sc.activation(AREc[:, :], LARc[:, :], AF.Exp), "sprs", 1)
    @blk.vector
    def _(v):
        wait(v, "sprs", sems["sprs"])
        v.tensor_scalar(AREc[:, :], AREc[:, :], -1.0, None, ALU.mult)
        v.tensor_scalar(DTAR[:, :], AREc[:, :], DTTc[:, 0:1], None, ALU.mult)
        v.tensor_scalar(DTAI[:, :], AIMc[:, :], DTTc[:, 0:1], None, ALU.mult)
        v.tensor_copy(ARG2[:, 0:64], DTAI[:, :])
        v.tensor_scalar(ARG2[:, 64:128], DTAI[:, :], 1.0, PI / 2.0, ALU.mult, ALU.add)
        emit_cw(v, ARG2[:, :], KI2[:, :], KF2[:, :])
        inc(nc.vector.drain(), "sprm", 1)
    @blk.scalar
    def _(sc):
        wait(sc, "sprm", sems["sprm"])
        sc.activation(SC2[:, :], ARG2[:, :], AF.Sin)
        inc(sc.activation(ERSc[:, :], DTAR[:, :], AF.Exp), "sprs", 1)
    @blk.vector
    def _(v):
        wait(v, "sprs", sems["sprs"])
        v.tensor_tensor(EDRE[:, :], ERSc[:, :], SC2[:, 64:128], ALU.mult)
        v.tensor_tensor(EDIM[:, :], ERSc[:, :], SC2[:, 0:64], ALU.mult)
        v.tensor_scalar(T64A[:, :], EDRE[:, :], -1.0, None, ALU.add)
        v.tensor_tensor(PRODRE[:, :], CREc[:, :], T64A[:, :], ALU.mult)
        v.tensor_tensor(T64B[:, :], CIMc[:, :], EDIM[:, :], ALU.mult)
        v.tensor_tensor(PRODRE[:, :], PRODRE[:, :], T64B[:, :], ALU.subtract)
        v.tensor_tensor(PRODIM[:, :], CREc[:, :], EDIM[:, :], ALU.mult)
        v.tensor_tensor(T64B[:, :], CIMc[:, :], T64A[:, :], ALU.mult)
        v.tensor_tensor(PRODIM[:, :], PRODIM[:, :], T64B[:, :], ALU.add)
        v.tensor_tensor(DENc[:, :], AREc[:, :], AREc[:, :], ALU.mult)
        v.tensor_tensor(T64B[:, :], AIMc[:, :], AIMc[:, :], ALU.mult)
        v.tensor_tensor(DENc[:, :], DENc[:, :], T64B[:, :], ALU.add)
        v.reciprocal(DENc[:, :], DENc[:, :])
        v.tensor_scalar(DENc[:, :], DENc[:, :], 2.0, None, ALU.mult)
        v.tensor_tensor(C2RE[:, :], PRODRE[:, :], AREc[:, :], ALU.mult)
        v.tensor_tensor(T64B[:, :], PRODIM[:, :], AIMc[:, :], ALU.mult)
        v.tensor_tensor(C2RE[:, :], C2RE[:, :], T64B[:, :], ALU.add)
        v.tensor_tensor(C2RE[:, :], C2RE[:, :], DENc[:, :], ALU.mult)
        v.tensor_tensor(C2IM[:, :], PRODIM[:, :], AREc[:, :], ALU.mult)
        v.tensor_tensor(T64B[:, :], PRODRE[:, :], AIMc[:, :], ALU.mult)
        v.tensor_tensor(C2IM[:, :], C2IM[:, :], T64B[:, :], ALU.subtract)
        v.tensor_tensor(C2IM[:, :], C2IM[:, :], DENc[:, :], ALU.mult)
        v.memset(CBRE[:, 0:64], 1.0)
        v.memset(CBIM[:, 0:64], 0.0)
        v.tensor_copy(CBRE[:, 64:128], EDRE[:, :])
        v.tensor_copy(CBIM[:, 64:128], EDIM[:, :])
        v.tensor_copy(PIRc[:, :], EDRE[:, :])
        v.tensor_copy(PIIc[:, :], EDIM[:, :])
        for k in range(1, 7):
            w = 1 << k
            v.tensor_tensor(T64A[:, :], PIRc[:, :], PIIc[:, :], ALU.mult)
            v.tensor_tensor(T64B[:, :], PIRc[:, :], PIRc[:, :], ALU.mult)
            v.tensor_tensor(T64C[:, :], PIIc[:, :], PIIc[:, :], ALU.mult)
            v.tensor_tensor(PIRc[:, :], T64B[:, :], T64C[:, :], ALU.subtract)
            v.tensor_tensor(PIIc[:, :], T64A[:, :], T64A[:, :], ALU.add)
            s_re = CBRE[:, 0:w * 64].rearrange("p (l n) -> p l n", n=64)
            s_im = CBIM[:, 0:w * 64].rearrange("p (l n) -> p l n", n=64)
            d_re = CBRE[:, w * 64:2 * w * 64].rearrange("p (l n) -> p l n", n=64)
            d_im = CBIM[:, w * 64:2 * w * 64].rearrange("p (l n) -> p l n", n=64)
            pirb = PIRc[:, :].unsqueeze(1).broadcast_to([128, w, 64])
            piib = PIIc[:, :].unsqueeze(1).broadcast_to([128, w, 64])
            td1 = TD1c[:, 0:w * 64].rearrange("p (l n) -> p l n", n=64)
            td2 = TD2c[:, 0:w * 64].rearrange("p (l n) -> p l n", n=64)
            v.tensor_tensor(td1, s_re, pirb, ALU.mult)
            v.tensor_tensor(td2, s_im, piib, ALU.mult)
            v.tensor_tensor(d_re, td1, td2, ALU.subtract)
            v.tensor_tensor(td1, s_re, piib, ALU.mult)
            v.tensor_tensor(td2, s_im, pirb, ALU.mult)
            v.tensor_tensor(d_im, td1, td2, ALU.add)
        v.tensor_tensor(T64A[:, :], CBRE[:, 64 * 64:65 * 64], PIRc[:, :], ALU.mult)
        v.tensor_tensor(T64B[:, :], CBIM[:, 64 * 64:65 * 64], PIIc[:, :], ALU.mult)
        v.tensor_tensor(CBRE[:, 128 * 64:129 * 64], T64A[:, :], T64B[:, :], ALU.subtract)
        v.tensor_tensor(T64A[:, :], CBRE[:, 64 * 64:65 * 64], PIIc[:, :], ALU.mult)
        v.tensor_tensor(T64B[:, :], CBIM[:, 64 * 64:65 * 64], PIRc[:, :], ALU.mult)
        v.tensor_tensor(CBIM[:, 128 * 64:129 * 64], T64A[:, :], T64B[:, :], ALU.add)
        for c0, cn in ((0, 65), (65, 64)):
            cb3 = CBRE[:, c0 * 64:(c0 + cn) * 64].rearrange("p (l n) -> p l n", n=64)
            ci3 = CBIM[:, c0 * 64:(c0 + cn) * 64].rearrange("p (l n) -> p l n", n=64)
            c2rb = C2RE[:, :].unsqueeze(1).broadcast_to([128, cn, 64])
            c2ib = C2IM[:, :].unsqueeze(1).broadcast_to([128, cn, 64])
            td1 = TD1c[:, 0:cn * 64].rearrange("p (l n) -> p l n", n=64)
            td2 = TD2c[:, 0:cn * 64].rearrange("p (l n) -> p l n", n=64)
            v.tensor_tensor(td1, cb3, c2rb, ALU.mult)
            v.tensor_tensor(td2, ci3, c2ib, ALU.mult)
            v.tensor_tensor(td1, td1, td2, ALU.subtract)
            v.tensor_reduce(KKc[:, c0:c0 + cn].unsqueeze(2), td1, AX, ALU.add)
        v.tensor_scalar(KKc[:, 0:1], KKc[:, 0:1], DSLc[:, 0:1], None, ALU.add)
        v.memset(TD1c[:, 0:127], 0.0)
        inc(nc.vector.drain(), "kcub", 1)
    @blk.gpsimd
    def _(g_):
        wait(g_, "kcub", sems["kcub"])
        inc(g_.dma_start(out=t["khank"][:, 127:256], in_=KKc[:, 0:129]), "pd2", 16)
        inc(g_.dma_start(out=t["khank"][:, 0:127], in_=TD1c[:, 0:127]), "pd2", 16)
        inc(g_.dma_start(out=t["cub_re"][:, :], in_=CBRE[:, :]), "pd2", 16)
        inc(g_.dma_start(out=t["cub_im"][:, :], in_=CBIM[:, :]), "pd2", 16)
    TRBATCH = [[(DTAR, 0), (DTAI, 0), (C2RE, 0), (C2IM, 0)],
               [(CBRE, 8 * 64), (CBIM, 8 * 64), (CBRE, 16 * 64), (CBIM, 16 * 64)],
               [(CBRE, 32 * 64), (CBIM, 32 * 64), (CBRE, 64 * 64), (CBIM, 64 * 64)],
               [(CBRE, 128 * 64), (CBIM, 128 * 64)]]
    for bi, batch in enumerate(TRBATCH):
        @blk.tensor
        def _(te, bi=bi, batch=batch):
            wait(te, "kcub", sems["kcub"])
            if bi > 0:
                wait(te, "sprm", sems["sprm"])
            mm = None
            for q, (src, off) in enumerate(batch):
                mm = te.transpose(PS[7][0:64, q * 128:(q + 1) * 128], src[:, off:off + 64], IDN[:, :])
            inc(mm, "sptr", 1)
        @blk.vector
        def _(v, bi=bi, batch=batch):
            wait(v, "sptr", sems["sptr"])
            inc(v.tensor_copy(NMT[:, bi * 512:bi * 512 + len(batch) * 128], PS[7][0:64, 0:len(batch) * 128]), "sprm", 1)
    stcu.close()
    stq2 = ExitStack()
    SSTRE = stq2.enter_context(nc.sbuf_tensor("SSTRE", [64, 8320], F32))
    SSTIM = stq2.enter_context(nc.sbuf_tensor("SSTIM", [64, 8320], F32))
    ZBUF = stq2.enter_context(nc.sbuf_tensor("ZBUF", [128, 8192], F32))

    @blk.gpsimd
    def _(g_):
        wait(g_, "r2w")
        g_.iota(IOT8I[:, :], [[1, 8]], base=0, channel_multiplier=0)
        g_.tensor_copy(IOT8[:, :], IOT8I[:, :])
        inc(nc.gpsimd.drain(), "riot", 1)

    for gidx in range(2):
        @blk.gpsimd
        def _(g_, gidx=gidx):
            if gidx == 0:
                wait(g_, "r2w")
                wait(g_, "pd2", sems["pd2"])
                wait(g_, "sprm", sems["sprm"])
            else:
                wait(g_, "smm2", sems["smm2"])
            for b_ in range(2):
                g_.memset(bass_rust.AP(SSTRE, b_ * 4160, [[8320, 64], [65, 64]]), 0.0)
                g_.memset(bass_rust.AP(SSTIM, b_ * 4160, [[8320, 64], [65, 64]]), 0.0)
            inc(nc.gpsimd.drain(), "szro", 1)

        strt = ExitStack()
        sbr = lambda n_, sh, dt_=F32: strt.enter_context(nc.sbuf_tensor(n_ + str(gidx), sh, dt_))
        RTRE = sbr("RTRE", [64, 8192])
        RTIM = sbr("RTIM", [64, 8192])
        ARS = sbr("ARS", [64, 512]); ARC = sbr("ARC", [64, 512])
        KI8 = strt.enter_context(nc.sbuf_tensor("KI8" + str(gidx), [64, 512], I32))
        KF8 = sbr("KF8", [64, 512])
        SN8 = sbr("SN8", [64, 512]); CS8 = sbr("CS8", [64, 512])
        ER8 = sbr("ER8", [64, 512]); EX8 = sbr("EX8", [64, 512])
        TD1 = sbr("TD1", [64, 1024]); TD2 = sbr("TD2", [64, 1024])

        @blk.vector
        def _(v, gidx=gidx):
            wait(v, "riot", sems["riot"])
            wait(v, "sprm", sems["sprm"])
            if gidx == 0:
                wait(v, "pd2", sems["pd2"])
            else:
                wait(v, "rtg", sems["rtg"])
                wait(v, "smm2", sems["smm2"])
            i3 = IOT8[:, :].unsqueeze(1).broadcast_to([64, 64, 8])
            a3 = ARS[:, :].rearrange("p (c t) -> p c t", t=8)
            dti_b = NMT[:, O_DTI + gidx * 64:O_DTI + gidx * 64 + 64].unsqueeze(2).broadcast_to([64, 64, 8])
            v.tensor_tensor(a3, i3, dti_b, ALU.mult)
            v.tensor_scalar(ARC[:, :], ARS[:, :], 1.0, PI / 2.0, ALU.mult, ALU.add)
            emit_cw(v, ARS[:, :], KI8[:, :], KF8[:, :])
            emit_cw(v, ARC[:, :], KI8[:, :], KF8[:, :])
            e3 = ER8[:, :].rearrange("p (c t) -> p c t", t=8)
            dtr_b = NMT[:, O_DTR + gidx * 64:O_DTR + gidx * 64 + 64].unsqueeze(2).broadcast_to([64, 64, 8])
            v.tensor_tensor(e3, i3, dtr_b, ALU.mult)
            inc(nc.vector.drain(), "rtg", 1)
        @blk.scalar
        def _(sc, gidx=gidx):
            wait(sc, "rtg", sems["rtg"])
            sc.activation(SN8[:, :], ARS[:, :], AF.Sin)
            sc.activation(CS8[:, :], ARC[:, :], AF.Sin)
            inc(sc.activation(EX8[:, :], ER8[:, :], AF.Exp), "rts", 1)
        @blk.vector
        def _(v, gidx=gidx):
            wait(v, "rts", sems["rts"])
            v.tensor_tensor(SN8[:, :], SN8[:, :], EX8[:, :], ALU.mult)
            v.tensor_tensor(CS8[:, :], CS8[:, :], EX8[:, :], ALU.mult)
            t13 = CS8[:, :].rearrange("p (c t) -> p c t", t=8)
            t23 = SN8[:, :].rearrange("p (c t) -> p c t", t=8)
            c2r_b = NMT[:, O_CRE + gidx * 64:O_CRE + gidx * 64 + 64].unsqueeze(2).broadcast_to([64, 64, 8])
            c2i_b = NMT[:, O_CIM + gidx * 64:O_CIM + gidx * 64 + 64].unsqueeze(2).broadcast_to([64, 64, 8])
            w1 = TD1[:, 0:512].rearrange("p (c t) -> p c t", t=8)
            w2 = TD2[:, 0:512].rearrange("p (c t) -> p c t", t=8)
            rre = bass_rust.AP(RTRE, 0, [[8192, 64], [128, 64], [1, 8]])
            rim = bass_rust.AP(RTIM, 0, [[8192, 64], [128, 64], [1, 8]])
            v.tensor_tensor(w1, t13, c2r_b, ALU.mult)
            v.tensor_tensor(w2, t23, c2i_b, ALU.mult)
            v.tensor_tensor(rre, w1, w2, ALU.subtract)
            v.tensor_tensor(w1, t23, c2r_b, ALU.mult)
            v.tensor_tensor(w2, t13, c2i_b, ALU.mult)
            v.tensor_tensor(rim, w1, w2, ALU.add)
            v.tensor_scalar(rim, rim, -1.0, None, ALU.mult)
            for w in (8, 16, 32, 64):
                pc = min(64, 1024 // w)
                for c0 in range(0, 64, pc):
                    sr = bass_rust.AP(RTRE, c0 * 128, [[8192, 64], [128, pc], [1, w]])
                    si = bass_rust.AP(RTIM, c0 * 128, [[8192, 64], [128, pc], [1, w]])
                    dr = bass_rust.AP(RTRE, c0 * 128 + w, [[8192, 64], [128, pc], [1, w]])
                    di = bass_rust.AP(RTIM, c0 * 128 + w, [[8192, 64], [128, pc], [1, w]])
                    wr = NMT[:, O_PW[w] + gidx * 64 + c0:O_PW[w] + gidx * 64 + c0 + pc].unsqueeze(2).broadcast_to([64, pc, w])
                    wi = NMT[:, O_PW[w] + 128 + gidx * 64 + c0:O_PW[w] + 128 + gidx * 64 + c0 + pc].unsqueeze(2).broadcast_to([64, pc, w])
                    td1 = TD1[:, 0:pc * w].rearrange("p (c t) -> p c t", t=w)
                    td2 = TD2[:, 0:pc * w].rearrange("p (c t) -> p c t", t=w)
                    v.tensor_tensor(td1, sr, wr, ALU.mult)
                    v.tensor_tensor(td2, si, wi, ALU.mult)
                    v.tensor_tensor(dr, td1, td2, ALU.add)
                    v.tensor_tensor(td1, si, wr, ALU.mult)
                    v.tensor_tensor(td2, sr, wi, ALU.mult)
                    v.tensor_tensor(di, td1, td2, ALU.subtract)
            inc(nc.vector.drain(), "rtg", 1)

        for chl in range(64):
            ch = gidx * 64 + chl
            par = chl % 2
            nmA = "aldA" if par == 0 else "aldB"
            co = par * 128
            cu = par * 64
            @blk.sync
            def _(s, gidx=gidx, chl=chl, ch=ch, par=par, nmA=nmA, co=co, cu=cu):
                if gidx == 0 and chl == 0:
                    wait(s, "xchw")
                    wait(s, "pd2", sems["pd2"])
                if gidx > 0 or chl >= 2:
                    wait(s, "arev", sems["arev"] - 1)
                    wait(s, "smm", sems["smm"] - 1)
                inc(s.dma_start(out=ATB[:, co:co + 128],
                                in_=bass_rust.AP(t["xch_d"].tensor, ch * 16384, [[128, 128], [1, 128]])), nmA, 16)
                inc(s.dma_start(out=HTB[:, co:co + 128],
                                in_=bass_rust.AP(t["khank"].tensor, ch * 256, [[1, 128], [1, 128]])), nmA, 16)
                inc(s.dma_start(out=URB[:, cu:cu + 64],
                                in_=bass_rust.AP(t["cub_re"].tensor, ch * 8256 + 64, [[64, 128], [1, 64]])), nmA, 16)
                inc(s.dma_start(out=UIB[:, cu:cu + 64],
                                in_=bass_rust.AP(t["cub_im"].tensor, ch * 8256 + 64, [[64, 128], [1, 64]])), nmA, 16)
            @blk.vector
            def _(v, gidx=gidx, chl=chl, par=par, nmA=nmA, co=co):
                wait(v, nmA, sems[nmA])
                if gidx > 0 or chl >= 2:
                    wait(v, "atr", sems["atr"] - 1)
                inc(v.tensor_copy(ARB[:, co:co + 128],
                                  bass_rust.AP(ATB, co + 127, [[256, 128], [-1, 128]])), "arev", 1)
            @blk.tensor
            def _(te, gidx=gidx, chl=chl, par=par, co=co):
                wait(te, "arev", sems["arev"])
                if gidx > 0 or chl >= 2:
                    wait(te, "aev", sems["aev"] - 1)
                inc(te.transpose(PS[par][:, 0:128], ARB[:, co:co + 128], IDN[:, :]), "atr", 1)
            @blk.vector
            def _(v, gidx=gidx, chl=chl, par=par, co=co):
                wait(v, "atr", sems["atr"])
                if gidx > 0 or chl >= 2:
                    wait(v, "smm", sems["smm"] - 1)
                inc(v.tensor_copy(XHB[:, co:co + 128], PS[par][:, 0:128]), "aev", 1)
            @blk.tensor
            def _(te, gidx=gidx, chl=chl, par=par, nmA=nmA, co=co, cu=cu):
                wait(te, "aev", sems["aev"])
                wait(te, nmA, sems[nmA])
                if gidx > 0 or chl >= 2:
                    wait(te, "sev", sems["sev"] - 1)
                te.matmul(PS[2 + par][:, 0:128], HTB[:, co:co + 128], XHB[:, co:co + 128], start=True, stop=True)
                te.matmul(PS[4 + par][0:64, 0:128], URB[:, cu:cu + 64], XHB[:, co:co + 128], start=True, stop=True)
                inc(te.matmul(PS[4 + par][0:64, 128:256], UIB[:, cu:cu + 64], XHB[:, co:co + 128], start=True, stop=True), "smm", 1)
            @blk.vector
            def _(v, gidx=gidx, chl=chl, par=par):
                wait(v, "smm", sems["smm"])
                if gidx == 1 and chl == 0:
                    wait(v, "agr4", 32)
                    wait(v, "smm2", sems["smm2"])
                if gidx == 0 and chl == 0:
                    wait(v, "szro", sems["szro"])
                zv = bass_rust.AP(ZBUF, chl, [[8192, 128], [4096, 2], [64, 64]])
                v.tensor_copy(zv, PS[2 + par][:, 0:128].rearrange("p (b k) -> p b k", b=2))
                sre = bass_rust.AP(SSTRE, chl * 65 + 1, [[8320, 64], [4160, 2], [1, 64]])
                v.tensor_copy(sre, PS[4 + par][0:64, 0:128].rearrange("p (b k) -> p b k", b=2))
                sIm = bass_rust.AP(SSTIM, chl * 65 + 1, [[8320, 64], [4160, 2], [1, 64]])
                v.tensor_copy(sIm, PS[4 + par][0:64, 128:256].rearrange("p (b k) -> p b k", b=2))
                inc(nc.vector.drain(), "sev", 1)

        @blk.vector
        def _(v, gidx=gidx):
            wait(v, "sev", sems["sev"])
            wait(v, "szro", sems["szro"])
            pr = NMT[:, O_PW[128] + gidx * 64:O_PW[128] + gidx * 64 + 64].unsqueeze(1).broadcast_to([64, 2, 64])
            pi = NMT[:, O_PW[128] + 128 + gidx * 64:O_PW[128] + 128 + gidx * 64 + 64].unsqueeze(1).broadcast_to([64, 2, 64])
            t1v = TS1[:, :].rearrange("p (b c) -> p b c", b=2)
            t2v = TS2[:, :].rearrange("p (b c) -> p b c", b=2)
            for s_ in range(1, 64):
                pre_ = bass_rust.AP(SSTRE, s_ - 1, [[8320, 64], [4160, 2], [65, 64]])
                cre_ = bass_rust.AP(SSTRE, s_, [[8320, 64], [4160, 2], [65, 64]])
                pim_ = bass_rust.AP(SSTIM, s_ - 1, [[8320, 64], [4160, 2], [65, 64]])
                cim_ = bass_rust.AP(SSTIM, s_, [[8320, 64], [4160, 2], [65, 64]])
                v.tensor_tensor(t1v, pre_, pr, ALU.mult)
                v.tensor_tensor(t2v, pim_, pi, ALU.mult)
                v.tensor_tensor(cre_, cre_, t1v, ALU.add)
                v.tensor_tensor(cre_, cre_, t2v, ALU.subtract)
                v.tensor_tensor(t1v, pim_, pr, ALU.mult)
                v.tensor_tensor(t2v, pre_, pi, ALU.mult)
                v.tensor_tensor(cim_, cim_, t1v, ALU.add)
                v.tensor_tensor(cim_, cim_, t2v, ALU.add)
            inc(nc.vector.drain(), "scan", 1)

        for chl in range(64):
            par = chl % 2
            @blk.tensor
            def _(te, gidx=gidx, chl=chl, par=par):
                if chl == 0:
                    wait(te, "scan", sems["scan"])
                    wait(te, "rtg", sems["rtg"])
                if chl >= 2:
                    wait(te, "sev2", sems["sev2"] - 1)
                rhs_re = SSTRE[:, :].rearrange("p (b c s) -> p b c s", b=2, c=64, s=65)[:, :, chl:chl + 1, 0:64]
                rhs_im = SSTIM[:, :].rearrange("p (b c s) -> p b c s", b=2, c=64, s=65)[:, :, chl:chl + 1, 0:64]
                te.matmul(PS[2 + par][:, 0:128], RTRE[:, chl * 128:(chl + 1) * 128], rhs_re, start=True, stop=False)
                inc(te.matmul(PS[2 + par][:, 0:128], RTIM[:, chl * 128:(chl + 1) * 128], rhs_im, start=False, stop=True), "smm2", 1)
            @blk.vector
            def _(v, gidx=gidx, chl=chl, par=par):
                wait(v, "smm2", sems["smm2"])
                zv = bass_rust.AP(ZBUF, chl, [[8192, 128], [4096, 2], [64, 64]])
                v.tensor_tensor(zv, zv, PS[2 + par][:, 0:128].rearrange("p (b k) -> p b k", b=2), ALU.add)
                inc(nc.vector.drain(), "sev2", 1)
        @blk.gpsimd
        def _(g_, gidx=gidx):
            wait(g_, "sev2", sems["sev2"])
            for b_ in range(2):
                dst = bass_rust.AP(t["az_in"].tensor, b_ * 1048576 + gidx * 64, [[128, 128], [16384, 64], [1, 64]])
                inc(g_.dma_start(out=dst, in_=ZBUF[:, b_ * 4096:(b_ + 1) * 4096]), "agr4", 16)
        strt.close()

    stq2.close()
    @blk.gpsimd
    def _(g_):
        wait(g_, "agr4", 64)
        inc(g_.collective_compute("AllToAll", ALU.bypass, replica_groups=[list(range(8))],
                                  ins=[t["az_in"].opt()], outs=[t["az_out"].opt()]), "cc5", 1)
    stq.close()

    str_ = ExitStack()
    sbz = lambda n_, sh, dt_=F32: str_.enter_context(nc.sbuf_tensor(n_, sh, dt_))
    XF3r = sbz("XF3r", [128, 2048])
    ZTr = sbz("ZTr", [128, 2048])
    XL3r = sbz("XL3r", [128, 1024])
    RRr = sbz("RRr", [128, 1024])
    OTr = sbz("OTr", [128, 1024])
    LNWr = sbz("LNWr", [128, 1024])
    LNBr = sbz("LNBr", [128, 1024])
    LROWr = sbz("LROWr", [1, 2048])
    MU3r = sbz("MU3r", [128, 8])
    @blk.sync
    def _(s):
        wait(s, "agr4", 64)
        inc(s.dma_start(out=LROWr[:, 0:1024], in_=t["L1_attn_ln_w"][None, :]), "prm4", 16)
        inc(s.dma_start(out=LROWr[:, 1024:2048], in_=t["L1_attn_ln_b"][None, :]), "prm4", 16)
    @blk.tensor
    def _(te):
        wait(te, "prm4", sems["prm4"])
        wait(te, "sev2", sems["sev2"])
        te.matmul(PS[6][:, 0:512], ONES1[:, :], LROWr[0:1, 0:512], start=True, stop=True)
        te.matmul(PS[7][:, 0:512], ONES1[:, :], LROWr[0:1, 512:1024], start=True, stop=True)
        inc(te.matmul(PS[4][:, 0:512], ONES1[:, :], LROWr[0:1, 1024:1536], start=True, stop=True), "glt", 1)
        inc(te.matmul(PS[5][:, 0:512], ONES1[:, :], LROWr[0:1, 1536:2048], start=True, stop=True), "glt", 1)
    @blk.vector
    def _(v):
        wait(v, "glt", sems["glt"])
        wait(v, "agr4", 64)
        v.tensor_copy(LNWr[:, 0:512], PS[6][:, :])
        v.tensor_copy(LNWr[:, 512:1024], PS[7][:, :])
        v.tensor_copy(LNBr[:, 0:512], PS[4][:, :])
        inc(v.tensor_copy(LNBr[:, 512:1024], PS[5][:, :]), "gln", 1)
    for i in range(16):
        par = i % 2
        nmx = "x3ldA" if par == 0 else "x3ldB"
        nmz = "zrd" if par == 0 else "zrdB"
        @blk.sync
        def _(s, i=i, par=par, nmx=nmx, nmz=nmz):
            if i == 0:
                wait(s, "cc5", sems["cc5"])
            if i >= 2:
                wait(s, "rfin", sems["rfin"] - 1)
            inc(s.dma_start(out=XF3r[:, par * 1024:(par + 1) * 1024], in_=t["resid2"][i]), nmx, 16)
            for j in range(8):
                src = bass_rust.AP(t["az_out"].tensor, (j * 2048 + i * 128) * 128, [[128, 128], [1, 128]])
                inc(s.dma_start(out=ZTr[:, par * 1024 + j * 128:par * 1024 + (j + 1) * 128], in_=src), nmz, 16)
        @blk.scalar
        def _(sc, i=i, par=par, nmx=nmx):
            wait(sc, nmx, sems[nmx])
            if i > 0:
                wait(sc, "rfin", sems["rfin"])
            sc.activation(XL3r[:, :], XF3r[:, par * 1024:(par + 1) * 1024], AF.Identity, accum_out=MU3r[:, 0:1])
            inc(sc.activation(XL3r[:, :], XF3r[:, par * 1024:(par + 1) * 1024], AF.Square, accum_out=MU3r[:, 1:2]), "gls", 1)
        @blk.vector
        def _(v, i=i):
            wait(v, "gls", sems["gls"])
            v.tensor_scalar(MU3r[:, 2:3], MU3r[:, 0:1], 1.0 / D, None, ALU.mult)
            v.tensor_scalar(MU3r[:, 3:4], MU3r[:, 1:2], 1.0 / D, None, ALU.mult)
            v.tensor_tensor(MU3r[:, 4:5], MU3r[:, 2:3], MU3r[:, 2:3], ALU.mult)
            v.tensor_tensor(MU3r[:, 3:4], MU3r[:, 3:4], MU3r[:, 4:5], ALU.subtract)
            inc(v.tensor_scalar(MU3r[:, 3:4], MU3r[:, 3:4], LN_EPS, None, ALU.add), "gln", 1)
        @blk.scalar
        def _(sc, i=i):
            wait(sc, "gln", sems["gln"])
            inc(sc.activation(MU3r[:, 3:4], MU3r[:, 3:4], AF.Sqrt), "gls", 1)
        @blk.vector
        def _(v, i=i):
            wait(v, "gls", sems["gls"])
            v.reciprocal(MU3r[:, 3:4], MU3r[:, 3:4])
            v.tensor_tensor(MU3r[:, 4:5], MU3r[:, 2:3], MU3r[:, 3:4], ALU.mult)
            inc(v.tensor_scalar(MU3r[:, 4:5], MU3r[:, 4:5], -1.0, None, ALU.mult), "gln", 1)
        @blk.scalar
        def _(sc, i=i, par=par):
            wait(sc, "gln", sems["gln"])
            inc(sc.activation(XL3r[:, :], XF3r[:, par * 1024:(par + 1) * 1024], AF.Identity,
                              bias=MU3r[:, 4:5], scale=MU3r[:, 3:4]), "gls", 1)
        @blk.vector
        def _(v, i=i, par=par, nmz=nmz):
            wait(v, "gls", sems["gls"])
            wait(v, nmz, sems[nmz])
            inc(v.tensor_tensor(RRr[:, :], XL3r[:, :], ZTr[:, par * 1024:(par + 1) * 1024], ALU.add), "gln", 1)
        @blk.scalar
        def _(sc, i=i):
            wait(sc, "gln", sems["gln"])
            sc.activation(XL3r[:, :], RRr[:, :], AF.Identity, accum_out=MU3r[:, 0:1])
            inc(sc.activation(XL3r[:, :], RRr[:, :], AF.Square, accum_out=MU3r[:, 1:2]), "gls", 1)
        @blk.vector
        def _(v, i=i):
            wait(v, "gls", sems["gls"])
            v.tensor_scalar(MU3r[:, 2:3], MU3r[:, 0:1], 1.0 / D, None, ALU.mult)
            v.tensor_scalar(MU3r[:, 3:4], MU3r[:, 1:2], 1.0 / D, None, ALU.mult)
            v.tensor_tensor(MU3r[:, 4:5], MU3r[:, 2:3], MU3r[:, 2:3], ALU.mult)
            v.tensor_tensor(MU3r[:, 3:4], MU3r[:, 3:4], MU3r[:, 4:5], ALU.subtract)
            inc(v.tensor_scalar(MU3r[:, 3:4], MU3r[:, 3:4], LN_EPS, None, ALU.add), "gln", 1)
        @blk.scalar
        def _(sc, i=i):
            wait(sc, "gln", sems["gln"])
            inc(sc.activation(MU3r[:, 3:4], MU3r[:, 3:4], AF.Sqrt), "gls", 1)
        @blk.vector
        def _(v, i=i):
            wait(v, "gls", sems["gls"])
            v.reciprocal(MU3r[:, 3:4], MU3r[:, 3:4])
            v.tensor_tensor(MU3r[:, 4:5], MU3r[:, 2:3], MU3r[:, 3:4], ALU.mult)
            inc(v.tensor_scalar(MU3r[:, 4:5], MU3r[:, 4:5], -1.0, None, ALU.mult), "gln", 1)
        @blk.scalar
        def _(sc, i=i):
            wait(sc, "gln", sems["gln"])
            inc(sc.activation(RRr[:, :], RRr[:, :], AF.Identity, bias=MU3r[:, 4:5], scale=MU3r[:, 3:4]), "gls", 1)
        @blk.vector
        def _(v, i=i, par=par):
            wait(v, "gls", sems["gls"])
            wait(v, "gln", sems["gln"])
            if i > 0:
                wait(v, "r3w", sems["r3w"])
                if upto == "ssm":
                    wait(v, "o3w", sems["o3w"])
            v.tensor_tensor(OTr[:, :], RRr[:, :], LNWr[:, :], ALU.mult)
            v.tensor_tensor(OTr[:, :], OTr[:, :], LNBr[:, :], ALU.add)
            inc(v.tensor_tensor(OTr[:, :], OTr[:, :], XF3r[:, par * 1024:(par + 1) * 1024], ALU.add), "rfin", 1)
        @blk.gpsimd
        def _(g_, i=i):
            wait(g_, "rfin", sems["rfin"])
            inc(g_.dma_start(out=t["resid3"][i], in_=OTr[:, :]), "r3w", 16)
            if upto == "ssm":
                inc(g_.dma_start(out=t["out"][i * 128:(i + 1) * 128, :], in_=OTr[:, :]), "o3w", 16)

    if upto == "ssm":
        @blk.gpsimd
        def _(g_):
            wait(g_, "o3w"); wait(g_, "r3w")
            inc(nc.gpsimd.drain(), "done", 1)
        str_.close()
        return nc, t, stack, sems
    str_.close()

    stv4 = ExitStack()
    sv4 = lambda n_, sh, dt_=F32: stv4.enter_context(nc.sbuf_tensor(n_, sh, dt_))
    QC4A = sv4("QC4A", [128, 16])
    QC4B = sv4("QC4B", [128, 16])
    R4 = sv4("R4", [1, 16])
    RI4 = sv4("RI4", [1, 4])
    KIQ = stv4.enter_context(nc.sbuf_tensor("KIQ", [1, 4], I32))
    MMX4 = sv4("MMX4", [128, 4])
    T41 = sv4("T41", [128, 2])
    AM8 = sv4("AM8", [8, 4])
    MU4ALL = sv4("MU4ALL", [128, 32])
    B2B = sv4("B2B", [128, 1024])
    BR2 = sv4("BR2", [1, 1024])

    def round_safe_v(eng, VV, QI, Q0, TB, FL):
        eng.tensor_copy(QI, VV)
        eng.tensor_copy(Q0, QI)
        eng.tensor_tensor(TB, VV, Q0, ALU.subtract)
        eng.tensor_single_scalar(FL, TB, 0.5, ALU.is_gt)
        eng.tensor_tensor(Q0, Q0, FL, ALU.add)
        eng.tensor_single_scalar(FL, TB, -0.5, ALU.is_lt)
        eng.tensor_tensor(Q0, Q0, FL, ALU.subtract)

    @blk.gpsimd
    def _(g_):
        wait(g_, "r3w")
        g_.memset(R4[:, :], 0.0)
        g_.memset(RI4[:, 0:1], 127.0)
        g_.memset(RI4[:, 1:2], 255.0)
        inc(nc.gpsimd.drain(), "h4q", 1)
    @blk.vector
    def _(v):
        wait(v, "h4q", sems["h4q"])
        wait(v, "r3w")
        inc(v.reciprocal(RI4[:, 2:4], RI4[:, 0:2]), "w1c", 1)
    @blk.sync
    def _(s):
        wait(s, "r3w")
        inc(s.dma_start(out=AM8[:, 0:2], in_=t["ag1_out"][:, 0:2]), "a1r4", 16)
        inc(s.dma_start(out=BR2[:, 0:1024], in_=t["L1_ffn_b2"][None, :]), "b14", 16)
    @blk.tensor
    def _(te):
        wait(te, "a1r4", sems["a1r4"])
        wait(te, "w1c", sems["w1c"])
        inc(te.transpose(PS[7][0:2, 0:8], AM8[0:8, 0:2], IDN[0:8, 0:8]), "f4tr", 1)
    @blk.vector
    def _(v):
        wait(v, "f4tr", sems["f4tr"])
        inc(v.tensor_reduce(MMX4[0:2, 2:3].unsqueeze(2), PS[7][0:2, 0:8].unsqueeze(1), AX, ALU.max), "f4mx", 1)
    @blk.tensor
    def _(te):
        wait(te, "f4mx", sems["f4mx"])
        inc(te.transpose(PS[6][0:1, 0:2], MMX4[0:2, 2:3], IDN[0:2, 0:2]), "f4tr", 1)
    @blk.vector
    def _(v):
        wait(v, "f4tr", sems["f4tr"])
        v.tensor_copy(R4[0:1, 2:4], PS[6][0:1, 0:2])
        inc(v.tensor_scalar(R4[0:1, 2:4], R4[0:1, 2:4], RI4[0:1, 2:3], 1e-8, ALU.mult, ALU.max), "f4mx", 1)
    st41 = ExitStack()
    s41 = lambda n_, sh, dt_=F32: st41.enter_context(nc.sbuf_tensor(n_, sh, dt_))
    XF4 = s41("XF4", [128, 2048])
    XL4 = s41("XL4", [128, 1024])
    XSC4 = s41("XSC4", [128, 1024])
    MU4 = s41("MU4", [128, 8])
    for i in range(16):
        par = i % 2
        nmx = "x4ldA" if par == 0 else "x4ldB"
        @blk.sync
        def _(s, i=i, par=par, nmx=nmx):
            if i == 0:
                wait(s, "r3w")
            if i >= 2:
                wait(s, "f4st", sems["f4st"] - 3)
            inc(s.dma_start(out=XF4[:, par * 1024:(par + 1) * 1024], in_=t["resid3"][i]), nmx, 16)
        @blk.scalar
        def _(sc, i=i, par=par, nmx=nmx):
            wait(sc, nmx, sems[nmx])
            sc.activation(XSC4[:, :], XF4[:, par * 1024:(par + 1) * 1024], AF.Identity, accum_out=MU4[:, 0:1])
            inc(sc.activation(XSC4[:, :], XF4[:, par * 1024:(par + 1) * 1024], AF.Square, accum_out=MU4[:, 1:2]), "f4st", 1)
        @blk.vector
        def _(v, i=i):
            wait(v, "f4st", sems["f4st"])
            v.tensor_scalar(MU4[:, 2:3], MU4[:, 0:1], 1.0 / D, None, ALU.mult)
            v.tensor_scalar(MU4[:, 3:4], MU4[:, 1:2], 1.0 / D, None, ALU.mult)
            v.tensor_tensor(MU4[:, 4:5], MU4[:, 2:3], MU4[:, 2:3], ALU.mult)
            v.tensor_tensor(MU4[:, 3:4], MU4[:, 3:4], MU4[:, 4:5], ALU.subtract)
            inc(v.tensor_scalar(MU4[:, 3:4], MU4[:, 3:4], LN_EPS, None, ALU.add), "f4ln", 1)
        @blk.scalar
        def _(sc, i=i):
            wait(sc, "f4ln", sems["f4ln"])
            inc(sc.activation(MU4[:, 3:4], MU4[:, 3:4], AF.Sqrt), "f4st", 1)
        @blk.vector
        def _(v, i=i):
            wait(v, "f4st", sems["f4st"])
            v.reciprocal(MU4ALL[:, 2 * i:2 * i + 1], MU4[:, 3:4])
            v.tensor_tensor(MU4ALL[:, 2 * i + 1:2 * i + 2], MU4[:, 2:3], MU4ALL[:, 2 * i:2 * i + 1], ALU.mult)
            inc(v.tensor_scalar(MU4ALL[:, 2 * i + 1:2 * i + 2], MU4ALL[:, 2 * i + 1:2 * i + 2], -1.0, None, ALU.mult), "f4ln", 1)
        @blk.scalar
        def _(sc, i=i, par=par):
            wait(sc, "f4ln", sems["f4ln"])
            if i > 0:
                wait(sc, "f4mx", sems["f4mx"])
            inc(sc.activation(XL4[:, :], XF4[:, par * 1024:(par + 1) * 1024], AF.Identity,
                              bias=MU4ALL[:, 2 * i + 1:2 * i + 2], scale=MU4ALL[:, 2 * i:2 * i + 1]), "f4st", 1)
        @blk.vector
        def _(v, i=i):
            wait(v, "f4st", sems["f4st"])
            v.tensor_reduce(T41[:, 0:1].unsqueeze(2), XL4[:, :].unsqueeze(1), AX, ALU.max)
            v.tensor_reduce(T41[:, 1:2].unsqueeze(2), XL4[:, :].unsqueeze(1), AX, ALU.min)
            if i == 0:
                v.tensor_copy(MMX4[:, 0:2], T41[:, 0:2])
            else:
                v.tensor_tensor(MMX4[:, 0:1], MMX4[:, 0:1], T41[:, 0:1], ALU.max)
                v.tensor_tensor(MMX4[:, 1:2], MMX4[:, 1:2], T41[:, 1:2], ALU.min)
            inc(nc.vector.drain(), "f4mx", 1)

    def emit_armax(col_src, ar_in, ar_out, cc_nm, arw_nm, arr_nm, rdst):
        @blk.vector
        def _(v):
            wait(v, "f4mx", sems["f4mx"])
            inc(v.tensor_scalar(MMX4[:, col_src + 1:col_src + 2], MMX4[:, col_src + 1:col_src + 2], -1.0, None, ALU.mult), "f4mx", 1)
        @blk.tensor
        def _(te):
            wait(te, "f4mx", sems["f4mx"])
            inc(te.transpose(PS[7][0:2, 0:128], MMX4[:, col_src:col_src + 2], IDN[:, :]), "f4tr", 1)
        @blk.vector
        def _(v):
            wait(v, "f4tr", sems["f4tr"])
            inc(v.tensor_reduce(MMX4[0:2, 2:3].unsqueeze(2), PS[7][0:2, 0:128].unsqueeze(1), AX, ALU.max), "f4mx", 1)
        @blk.tensor
        def _(te):
            wait(te, "f4mx", sems["f4mx"])
            inc(te.transpose(PS[6][0:1, 0:2], MMX4[0:2, 2:3], IDN[0:2, 0:2]), "f4tr", 1)
        @blk.vector
        def _(v):
            wait(v, "f4tr", sems["f4tr"])
            inc(v.tensor_copy(R4[0:1, 0:2], PS[6][0:1, 0:2]), "f4mx", 1)
        @blk.gpsimd
        def _(g_):
            wait(g_, "f4mx", sems["f4mx"])
            inc(g_.dma_start(out=t[ar_in][:, :], in_=R4[0:1, 0:2]), arw_nm, 16)
            wait(g_, arw_nm, sems[arw_nm])
            inc(g_.collective_compute("AllReduce", ALU.max, replica_groups=[list(range(8))],
                                      ins=[t[ar_in].opt()], outs=[t[ar_out].opt()]), cc_nm, 1)
        @blk.sync
        def _(s):
            wait(s, cc_nm, sems[cc_nm])
            inc(s.dma_start(out=R4[0:1, rdst:rdst + 2], in_=t[ar_out][:, :]), arr_nm, 16)

    def emit_scale_math(rdst, sw_col, qc_dst_getter):
        @blk.vector
        def _(v):
            wait(v, qc_dst_getter, sems[qc_dst_getter])
            v.tensor_tensor(R4[0:1, 6:7], R4[0:1, rdst:rdst + 1], R4[0:1, rdst + 1:rdst + 2], ALU.add)
            v.tensor_scalar(R4[0:1, 6:7], R4[0:1, 6:7], RI4[0:1, 3:4], 1e-8, ALU.mult, ALU.max)
            v.reciprocal(R4[0:1, 7:8], R4[0:1, 6:7])
            v.tensor_tensor(R4[0:1, 8:9], R4[0:1, rdst + 1:rdst + 2], R4[0:1, 7:8], ALU.mult)
            round_safe_v(v, R4[0:1, 8:9], KIQ[0:1, 0:1], R4[0:1, 9:10], R4[0:1, 10:11], R4[0:1, 11:12])
            v.tensor_scalar(R4[0:1, 9:10], R4[0:1, 9:10], -128.0, None, ALU.add)
            v.tensor_single_scalar(R4[0:1, 9:10], R4[0:1, 9:10], -128.0, ALU.max)
            v.tensor_single_scalar(R4[0:1, 9:10], R4[0:1, 9:10], 127.0, ALU.min)
            v.tensor_scalar(R4[0:1, 12:13], R4[0:1, 9:10], -1.0, -128.0, ALU.mult, ALU.add)
            v.tensor_scalar(R4[0:1, 13:14], R4[0:1, 9:10], -1.0, 127.0, ALU.mult, ALU.add)
            v.tensor_tensor(R4[0:1, 14:15], R4[0:1, 6:7], R4[0:1, sw_col:sw_col + 1], ALU.mult)
            inc(nc.vector.drain(), "f4mx", 1)

    def emit_qc_bcast(QCdst):
        @blk.tensor
        def _(te):
            wait(te, "f4mx", sems["f4mx"])
            inc(te.matmul(PS[6][:, 0:16], ONES1[:, :], R4[0:1, 0:16], start=True, stop=True), "f4tr", 1)
        @blk.vector
        def _(v):
            wait(v, "f4tr", sems["f4tr"])
            inc(v.tensor_copy(QCdst[:, :], PS[6][:, 0:16]), "q4", 1)

    emit_armax(0, "ar6_in", "ar6_out", "cc6", "ar6w", "ar6r", 4)
    emit_scale_math(4, 2, "ar6r")
    emit_qc_bcast(QC4A)

    st42 = ExitStack()
    s42 = lambda n_, sh, dt_=F32: st42.enter_context(nc.sbuf_tensor(n_, sh, dt_))
    XQT = st42.enter_context(nc.sbuf_tensor("XQT", [128, 16384], BF16))
    W1TB = st42.enter_context(nc.sbuf_tensor("W1TB", [128, 8192], BF16))
    BR4 = st42.enter_context(nc.sbuf_tensor("BR4", [1, 4096], F32))
    B1B = st42.enter_context(nc.sbuf_tensor("B1B", [128, 4096], F32))
    XQ4 = s42("XQ4", [128, 1024])
    @blk.sync
    def _(s):
        wait(s, "f4st", sems["f4st"])
        inc(s.dma_start(out=BR4[:, 0:4096], in_=t["L1_ffn_b1"][None, :]), "b14", 16)
    for r in range(5):
        @blk.tensor
        def _(te, r=r):
            wait(te, "b14", sems["b14"])
            if r > 0:
                wait(te, "w1c", sems["w1c"])
            else:
                wait(te, "q4", sems["q4"])
            srcrow = BR4[0:1, r * 1024:r * 1024 + 512] if r < 4 else BR2[0:1, 0:512]
            srcrow2 = BR4[0:1, r * 1024 + 512:(r + 1) * 1024] if r < 4 else BR2[0:1, 512:1024]
            te.matmul(PS[6][:, 0:512], ONES1[:, :], srcrow, start=True, stop=True)
            inc(te.matmul(PS[7][:, 0:512], ONES1[:, :], srcrow2, start=True, stop=True), "arbc", 1)
        @blk.vector
        def _(v, r=r):
            wait(v, "arbc", sems["arbc"])
            DSTB = B1B if r < 4 else B2B
            c0 = r * 1024 if r < 4 else 0
            v.tensor_copy(DSTB[:, c0:c0 + 512], PS[6][:, :])
            inc(v.tensor_copy(DSTB[:, c0 + 512:c0 + 1024], PS[7][:, :]), "w1c", 1)

    QI4a = st42.enter_context(nc.sbuf_tensor("QI4a", [128, 1024], I32))
    Q04a = s42("Q04a", [128, 1024])
    TB4a = s42("TB4a", [128, 1024])
    FL4a = s42("FL4a", [128, 1024])
    HV4 = s42("HV4", [128, 1024])
    HG4 = s42("HG4", [128, 1024])
    for i in range(16):
        par = i % 2
        nmx = "x4ldA" if par == 0 else "x4ldB"
        @blk.sync
        def _(s, i=i, par=par, nmx=nmx):
            if i >= 2:
                wait(s, "q4s", sems["q4s"] - 1)
            else:
                wait(s, "f4st", sems["f4st"])
            inc(s.dma_start(out=XF4[:, par * 1024:(par + 1) * 1024], in_=t["resid3"][i]), nmx, 16)
        @blk.scalar
        def _(sc, i=i, par=par, nmx=nmx):
            wait(sc, nmx, sems[nmx])
            if i > 0:
                wait(sc, "q4", sems["q4"])
            else:
                wait(sc, "f4mx", sems["f4mx"])
            inc(sc.activation(XL4[:, :], XF4[:, par * 1024:(par + 1) * 1024], AF.Identity,
                              bias=MU4ALL[:, 2 * i + 1:2 * i + 2], scale=MU4ALL[:, 2 * i:2 * i + 1]), "q4s", 1)
        @blk.vector
        def _(v, i=i):
            wait(v, "q4s", sems["q4s"])
            if i > 0:
                wait(v, "f4tr2", sems["f4tr2"])
            v.tensor_scalar(XQ4[:, :], XL4[:, :], QC4A[:, 7:8], None, ALU.mult)
            round_safe_v(v, XQ4[:, :], QI4a[:, :], Q04a[:, :], TB4a[:, :], FL4a[:, :])
            v.tensor_scalar(Q04a[:, :], Q04a[:, :], QC4A[:, 12:13], None, ALU.max)
            inc(v.tensor_scalar(Q04a[:, :], Q04a[:, :], QC4A[:, 13:14], None, ALU.min), "q4", 1)
        @blk.tensor
        def _(te, i=i):
            wait(te, "q4", sems["q4"])
            if i > 0:
                wait(te, "f4ev", sems["f4ev"])
            mm = None
            for j in range(4):
                te.transpose(PS[0][:, j * 128:(j + 1) * 128], Q04a[:, j * 128:(j + 1) * 128], IDN[:, :])
                mm = te.transpose(PS[1][:, j * 128:(j + 1) * 128], Q04a[:, 512 + j * 128:512 + (j + 1) * 128], IDN[:, :])
            inc(mm, "f4tr2", 1)
        @blk.vector
        def _(v, i=i):
            wait(v, "f4tr2", sems["f4tr2"])
            v.tensor_copy(XQT[:, i * 1024:i * 1024 + 512], PS[0][:, :])
            inc(v.tensor_copy(XQT[:, i * 1024 + 512:(i + 1) * 1024], PS[1][:, :]), "f4ev", 1)
    for ob in range(8):
        parw = ob % 2
        nmw = "w1ld4A" if parw == 0 else "w1ld4B"
        @blk.sync
        def _(s, ob=ob, parw=parw, nmw=nmw):
            if ob >= 2:
                wait(s, "g4mm", sems["g4mm"] - 16)
            else:
                wait(s, "cc2b", sems["cc2b"])
            for hb in range(8):
                src = bass_rust.AP(t["ag2b_out"].tensor, ob * AG2BN + OFF_W1 + hb * 128 * 512, [[512, 128], [1, 512]])
                inc(s.dma_start(out=W1TB[:, parw * 4096 + hb * 512:parw * 4096 + (hb + 1) * 512], in_=src), nmw, 16)
        for i in range(16):
            k = ob * 16 + i
            par2 = k % 2
            @blk.tensor
            def _(te, ob=ob, i=i, k=k, par2=par2, parw=parw, nmw=nmw):
                wait(te, nmw, sems[nmw])
                if ob == 0 and i == 0:
                    wait(te, "f4ev", sems["f4ev"])
                if k >= 2:
                    wait(te, "g4e", sems["g4e"] - 1)
                mm = None
                for hb in range(8):
                    mm = te.matmul(PS[2 + par2][:, 0:512], XQT[:, i * 1024 + hb * 128:i * 1024 + (hb + 1) * 128],
                                   W1TB[:, parw * 4096 + hb * 512:parw * 4096 + (hb + 1) * 512],
                                   start=(hb == 0), stop=(hb == 7))
                inc(mm, "g4mm", 1)
            @blk.vector
            def _(v, ob=ob, i=i, k=k, par2=par2):
                wait(v, "g4mm", sems["g4mm"])
                if k >= 2:
                    wait(v, "gel4", sems["gel4"] - 1)
                v.tensor_scalar(HV4[:, par2 * 512:(par2 + 1) * 512], PS[2 + par2][:, 0:512], QC4A[:, 14:15], None, ALU.mult)
                inc(v.tensor_tensor(HV4[:, par2 * 512:(par2 + 1) * 512], HV4[:, par2 * 512:(par2 + 1) * 512],
                                    B1B[:, ob * 512:(ob + 1) * 512], ALU.add), "g4e", 1)
            @blk.scalar
            def _(sc, ob=ob, i=i, k=k, par2=par2):
                wait(sc, "g4e", sems["g4e"])
                if k >= 2:
                    wait(sc, "hpwA" if par2 == 0 else "hpwB", sems["hpwA" if par2 == 0 else "hpwB"])
                inc(sc.activation(HG4[:, par2 * 512:(par2 + 1) * 512], HV4[:, par2 * 512:(par2 + 1) * 512], AF.Gelu), "gel4", 1)
            @blk.vector
            def _(v, ob=ob, i=i, k=k, par2=par2):
                wait(v, "gel4", sems["gel4"])
                v.tensor_reduce(T41[:, 0:1].unsqueeze(2), HG4[:, par2 * 512:(par2 + 1) * 512].unsqueeze(1), AX, ALU.max)
                v.tensor_reduce(T41[:, 1:2].unsqueeze(2), HG4[:, par2 * 512:(par2 + 1) * 512].unsqueeze(1), AX, ALU.min)
                if k == 0:
                    v.tensor_copy(MMX4[:, 2:4], T41[:, 0:2])
                else:
                    v.tensor_tensor(MMX4[:, 2:3], MMX4[:, 2:3], T41[:, 0:1], ALU.max)
                    v.tensor_tensor(MMX4[:, 3:4], MMX4[:, 3:4], T41[:, 1:2], ALU.min)
                inc(nc.vector.drain(), "f4mx2", 1)
            @blk.gpsimd
            def _(g_, ob=ob, i=i, k=k, par2=par2):
                wait(g_, "f4mx2", sems["f4mx2"])
                inc(g_.dma_start(out=t["hpark"][i, :, ob * 512:(ob + 1) * 512],
                                 in_=HG4[:, par2 * 512:(par2 + 1) * 512]), "hpwA" if par2 == 0 else "hpwB", 16)
    st42.close()

    @blk.vector
    def _(v):
        wait(v, "f4mx2", sems["f4mx2"])
        v.tensor_copy(MMX4[:, 0:2], MMX4[:, 2:4])
        inc(nc.vector.drain(), "f4mx", 1)
    emit_armax(0, "ar7_in", "ar7_out", "cc7", "ar7w", "ar7r", 4)
    emit_scale_math(4, 3, "ar7r")
    emit_qc_bcast(QC4B)

    st43 = ExitStack()
    s43 = lambda n_, sh, dt_=F32: st43.enter_context(nc.sbuf_tensor(n_, sh, dt_))
    W2TB = st43.enter_context(nc.sbuf_tensor("W2TB", [128, 32768], BF16))
    HP4 = s43("HP4", [128, 8192])
    QI4b = st43.enter_context(nc.sbuf_tensor("QI4b", [128, 4096], I32))
    Q04b = s43("Q04b", [128, 4096])
    TB4b = s43("TB4b", [128, 4096])
    HQT4 = st43.enter_context(nc.sbuf_tensor("HQT4", [128, 4096], BF16))
    OUT4 = s43("OUT4", [128, 1024])
    @blk.sync
    def _(s):
        wait(s, "hpwA"); wait(s, "hpwB")
        for oq in range(32):
            for cp in range(8):
                src = bass_rust.AP(t["ag2b_out"].tensor, cp * AG2BN + OFF_W2 + oq * 128 * 128, [[128, 128], [1, 128]])
                inc(s.dma_start(out=W2TB[:, oq * 1024 + cp * 128:oq * 1024 + (cp + 1) * 128], in_=src), "w2ld4", 16)
    for i in range(16):
        par = i % 2
        nmh = "hpldA" if par == 0 else "hpldB"
        nmx = "x4ldA" if par == 0 else "x4ldB"
        @blk.sync
        def _(s, i=i, par=par, nmh=nmh, nmx=nmx):
            if i >= 2:
                wait(s, "h4qv", sems["h4qv"] - 1)
                wait(s, "d4e", sems["d4e"] - 1)
            else:
                wait(s, "q4s", sems["q4s"])
            inc(s.dma_start(out=HP4[:, par * 4096:(par + 1) * 4096], in_=t["hpark"][i]), nmh, 16)
            inc(s.dma_start(out=XF4[:, par * 1024:(par + 1) * 1024], in_=t["resid3"][i]), nmx, 16)
        @blk.vector
        def _(v, i=i, par=par, nmh=nmh):
            wait(v, nmh, sems[nmh])
            if i > 0:
                wait(v, "h4tr", sems["h4tr"])
            hp = HP4[:, par * 4096:(par + 1) * 4096]
            v.tensor_scalar(hp, hp, QC4B[:, 7:8], None, ALU.mult)
            v.tensor_copy(QI4b[:, :], hp)
            v.tensor_copy(Q04b[:, :], QI4b[:, :])
            v.tensor_tensor(TB4b[:, :], hp, Q04b[:, :], ALU.subtract)
            v.tensor_single_scalar(hp, TB4b[:, :], 0.5, ALU.is_gt)
            v.tensor_tensor(Q04b[:, :], Q04b[:, :], hp, ALU.add)
            v.tensor_single_scalar(hp, TB4b[:, :], -0.5, ALU.is_lt)
            v.tensor_tensor(Q04b[:, :], Q04b[:, :], hp, ALU.subtract)
            v.tensor_scalar(Q04b[:, :], Q04b[:, :], QC4B[:, 12:13], None, ALU.max)
            v.tensor_scalar(Q04b[:, :], Q04b[:, :], QC4B[:, 13:14], None, ALU.min)
            inc(nc.vector.drain(), "h4qv", 1)
        for tb in range(8):
            @blk.tensor
            def _(te, i=i, tb=tb):
                wait(te, "h4qv", sems["h4qv"])
                if i > 0 or tb >= 2:
                    wait(te, "h4ev", sems["h4ev"] - 1)
                mm = None
                for q in range(4):
                    mm = te.transpose(PS[tb % 2][:, q * 128:(q + 1) * 128],
                                      Q04b[:, tb * 512 + q * 128:tb * 512 + (q + 1) * 128], IDN[:, :])
                inc(mm, "h4tr", 1)
            @blk.vector
            def _(v, i=i, tb=tb):
                wait(v, "h4tr", sems["h4tr"])
                if i > 0 and tb == 0:
                    wait(v, "d4mm", sems["d4mm"])
                inc(v.tensor_copy(HQT4[:, tb * 512:(tb + 1) * 512], PS[tb % 2][:, :]), "h4ev", 1)
        b4a = 4 if par == 0 else 6
        b4b = 5 if par == 0 else 7
        @blk.tensor
        def _(te, i=i, b4a=b4a, b4b=b4b):
            wait(te, "h4ev", sems["h4ev"])
            wait(te, "w2ld4", sems["w2ld4"])
            if i == 1:
                wait(te, "q4", sems["q4"])
            if i >= 2:
                wait(te, "d4e", sems["d4e"] - 1)
            mm = None
            for oq in range(32):
                te.matmul(PS[b4a][:, 0:512], HQT4[:, oq * 128:(oq + 1) * 128],
                          W2TB[:, oq * 1024:oq * 1024 + 512], start=(oq == 0), stop=(oq == 31))
                mm = te.matmul(PS[b4b][:, 0:512], HQT4[:, oq * 128:(oq + 1) * 128],
                               W2TB[:, oq * 1024 + 512:(oq + 1) * 1024], start=(oq == 0), stop=(oq == 31))
            inc(mm, "d4mm", 1)
        @blk.vector
        def _(v, i=i, par=par, nmx=nmx, b4a=b4a, b4b=b4b):
            wait(v, "d4mm", sems["d4mm"])
            wait(v, nmx, sems[nmx])
            if i > 0:
                wait(v, "o4w", sems["o4w"])
            v.tensor_scalar(OUT4[:, 0:512], PS[b4a][:, 0:512], QC4B[:, 14:15], None, ALU.mult)
            v.tensor_scalar(OUT4[:, 512:1024], PS[b4b][:, 0:512], QC4B[:, 14:15], None, ALU.mult)
            v.tensor_tensor(OUT4[:, :], OUT4[:, :], B2B[:, :], ALU.add)
            inc(v.tensor_tensor(OUT4[:, :], OUT4[:, :], XF4[:, par * 1024:(par + 1) * 1024], ALU.add), "d4e", 1)
        @blk.gpsimd
        def _(g_, i=i):
            wait(g_, "d4e", sems["d4e"])
            inc(g_.dma_start(out=t["out"][i * 128:(i + 1) * 128, :], in_=OUT4[:, :]), "o4w", 16)
    @blk.gpsimd
    def _(g_):
        wait(g_, "o4w")
        inc(nc.gpsimd.drain(), "done", 1)
    st43.close()
    st41.close()
    stv4.close()
    return nc, t, stack, sems




from concourse.bass_utils import run_bass_kernel_spmd


def kernel(**inputs):
    nc, t, stack, sems = build(upto="full")
    in_maps = host_inputs(inputs)
    res = run_bass_kernel_spmd(nc, in_maps, core_ids=list(range(NC)))
    outs = [np.asarray(res.results[c]["out"]) for c in range(NC)]
    full = np.empty((B, S, D), dtype=np.float32)
    for c in range(NC):
        b, qd = c // 4, c % 4
        full[b, qd * TPC:(qd + 1) * TPC, :] = outs[c]
    stack.close()
    return full


# version 05
# speedup vs baseline: 1.0210x; 1.0210x over previous
import os
os.environ.setdefault("TRN_ALLOW_ICI_ALL_TO_ALL", "1")
import numpy as np
import bass_rust
import concourse.bass as bass
import concourse.mybir as mybir
from contextlib import ExitStack

F32 = mybir.dt.float32
BF16 = mybir.dt.bfloat16
I32 = mybir.dt.int32
AF = mybir.ActivationFunctionType
ALU = mybir.AluOpType

B, S, D = 2, 8192, 1024
NC, TPC, NT = 8, 2048, 16
H, DH = 16, 64
I0, I1 = 2816, 4096
EPS_DN, LN_EPS = 1e-6, 1e-5
PI = float(np.pi)

def _cw():
    tp = 2.0 * np.float64(np.pi)
    def short(v):
        f = np.float32(v)
        return (f.view(np.uint32) & np.uint32(0xFFFFF000)).view(np.float32)
    C1 = short(tp); C2 = short(tp - np.float64(C1))
    C3 = np.float32(tp - np.float64(C1) - np.float64(C2))
    return float(C1), float(C2), float(C3)
CW1, CW2, CW3 = _cw()
INV2PI = float(1.0 / (2.0 * np.pi))

def num_slot(h):
    if h < 7:
        return 0, h * 65
    if h < 14:
        return 1, (h - 7) * 65
    return 2, (h - 14) * 65
SPANS = [(0, 0, 455), (1, 455, 455), (2, 910, 130)]


def host_consts():
    c = {}
    idn = np.eye(128, dtype=np.float32)
    c["IDN"] = idn
    pp, ff = np.mgrid[0:128, 0:128]
    c["CMASK"] = (pp <= ff).astype(np.float32)
    hc, p = np.mgrid[0:32, 0:128]
    c["CPAT"] = ((p // 64) == (hc % 2)).astype(np.float32)
    hc2, f = np.mgrid[0:32, 0:1040]
    c["HSEL"] = ((f // 65) == (hc2 // 2)).astype(np.float32)
    return c


def host_inputs(d):
    x = np.ascontiguousarray(np.asarray(d["L0_attn_x"], np.float32))
    cst = host_consts()
    in_maps = []
    for c in range(NC):
        b, qd = c // 4, c % 4
        m = {}
        m["x_in"] = np.ascontiguousarray(x[b, qd * TPC:(qd + 1) * TPC, :])
        for k in d:
            if k != "L0_attn_x":
                m[k] = np.ascontiguousarray(np.asarray(d[k], np.float32))
        m["wg_sl"] = np.ascontiguousarray(np.asarray(d["L0_ffn_w_gate"], np.float32)[c * 352:(c + 1) * 352])
        m["wu_sl"] = np.ascontiguousarray(np.asarray(d["L0_ffn_w_up"], np.float32)[c * 352:(c + 1) * 352])
        m["wd_sl"] = np.ascontiguousarray(np.asarray(d["L0_ffn_w_down"], np.float32)[c * 128:(c + 1) * 128])
        m["w1_sl"] = np.ascontiguousarray(np.asarray(d["L1_ffn_W1"], np.float32)[c * 512:(c + 1) * 512])
        for pn, key in (("lar_sl", "L1_attn_log_A_real"), ("aim_sl", "L1_attn_A_imag"),
                        ("cre_sl", "L1_attn_C_re"), ("cim_sl", "L1_attn_C_im")):
            m[pn] = np.ascontiguousarray(np.asarray(d[key], np.float32)[c * 128:(c + 1) * 128])
        m["ldt_sl"] = np.ascontiguousarray(np.asarray(d["L1_attn_log_dt"], np.float32)[c * 128:(c + 1) * 128, None])
        m["d_sl"] = np.ascontiguousarray(np.asarray(d["L1_attn_D"], np.float32)[c * 128:(c + 1) * 128, None])
        m["w2_sl"] = np.ascontiguousarray(np.asarray(d["L1_ffn_W2"], np.float32)[c * 128:(c + 1) * 128])
        m["pos"] = np.arange(qd * TPC, (qd + 1) * TPC, dtype=np.float32).reshape(16, 128)
        pred = np.zeros((8, 1), np.float32)
        betwT = np.zeros((8, 8), np.float32)
        for i in range(NC):
            if i // 4 == b and i < c:
                pred[i, 0] = 1.0
                for k2 in range(i + 1, c):
                    betwT[k2, i] = 1.0
        m["chain_pred"] = pred
        m["chain_betwT"] = np.ascontiguousarray(betwT)
        for k, v in cst.items():
            m[k] = v
        in_maps.append(m)
    return in_maps


def build(upto="l0a", n_loc=16, do_ag=True, do_fin=True, n_fin=16, fstage=99):
    nc = bass.Bass("TRN2", target_bir_lowering=False)
    stack = ExitStack()
    t = {}
    def din(name, shape, dtype=F32):
        t[name] = nc.dram_tensor(name, shape, dtype, kind="ExternalInput").ap()
    def dout(name, shape, dtype=F32):
        t[name] = nc.dram_tensor(name, shape, dtype, kind="ExternalOutput").ap()
    def dint(name, shape, dtype=F32):
        t[name] = nc.dram_tensor(name, shape, dtype).ap()

    din("x_in", [TPC, D])
    for n, sh in [("L0_attn_Wq", [D, D]), ("L0_attn_Wk", [D, D]), ("L0_attn_Wv", [D, D]),
                  ("L0_attn_Wo", [D, D]), ("L0_attn_Wbeta", [D, 32]),
                  ("L0_attn_bbeta", [32]), ("L0_attn_bo", [D]),
                  ("L0_attn_ln_gamma", [D]), ("L0_attn_ln_beta", [D]),
                  ("L0_ffn_w_gate", [I0, D]), ("L0_ffn_w_up", [I0, D]), ("L0_ffn_w_down", [D, I0]),
                  ("L1_attn_log_A_real", [D, 64]), ("L1_attn_A_imag", [D, 64]),
                  ("L1_attn_C_re", [D, 64]), ("L1_attn_C_im", [D, 64]),
                  ("L1_attn_log_dt", [D]), ("L1_attn_D", [D]),
                  ("L1_attn_ln_w", [D]), ("L1_attn_ln_b", [D]),
                  ("L1_ffn_W1", [I1, D]), ("L1_ffn_b1", [I1]),
                  ("L1_ffn_W2", [D, I1]), ("L1_ffn_b2", [D]),
                  ("lar_sl", [128, 64]), ("aim_sl", [128, 64]), ("cre_sl", [128, 64]),
                  ("cim_sl", [128, 64]), ("ldt_sl", [128, 1]), ("d_sl", [128, 1]),
                  ("wg_sl", [352, D]), ("wu_sl", [352, D]), ("wd_sl", [128, I0]),
                  ("w1_sl", [512, D]), ("w2_sl", [128, I1]),
                  ("pos", [16, 128]), ("chain_pred", [8, 1]), ("chain_betwT", [8, 8]),
                  ("IDN", [128, 128]), ("CMASK", [128, 128]), ("CPAT", [32, 128]),
                  ("HSEL", [32, 1040])]:
        din(n, sh)
    dout("out", [TPC, D])

    dint("q_park", [NT, 128, D])
    dint("num_park", [NT, 128, 1040])
    dint("ag3_in", [128, 1072])
    dint("ag3_out", [NC * 128, 1072])
    dint("resid1", [NT, 128, D])
    dint("resid2", [NT, 128, D])
    dint("hqt_park", [8, 3, 128, 2048])
    SZ_GU = D * 352
    SZ_DN = I0 * 128
    AG2N = 2 * SZ_GU + SZ_DN
    OFF_G, OFF_U, OFF_DN = 0, SZ_GU, 2 * SZ_GU
    dint("ag1_in", [1, 16])
    dint("ag1_out", [NC, 16])
    dint("ag2a_in", [1, AG2N])
    dint("ag2a_out", [NC, AG2N])
    AG2BN = D * 512 + I1 * 128
    OFF_W1, OFF_W2 = 0, D * 512
    dint("ag2b_in", [1, AG2BN], BF16)
    dint("ag2b_out", [NC, AG2BN], BF16)
    dint("ag4_in", [D, TPC])
    dint("ag4_out", [D, TPC])
    dint("xch_d", [128, 16384])
    dint("khank", [128, 256])
    dint("cub_re", [128, 8256])
    dint("cub_im", [128, 8256])
    dint("az_in", [16384, 128])
    dint("az_out", [16384, 128])
    dint("resid3", [NT, 128, D])
    dint("hpark", [NT, 128, I1])
    dint("ar6_in", [1, 2])
    dint("ar6_out", [1, 2])
    dint("ar7_in", [1, 2])
    dint("ar7_out", [1, 2])

    DMA_NAMES = ["cld", "wld", "posld", "xldA", "xldB", "xld2A", "xld2B", "agr", "msl",
                 "wold", "gld", "bld", "cqldA", "cqldB", "qpk", "numpk", "stupd", "agw",
                 "mged", "r1w", "outw", "cc3",
                 "pwld", "ag1w", "cc1", "ag1r", "ag2w", "cc2a", "cc2b",
                 "wgldA", "wgldB", "r2ldA", "r2ldB", "hqpk", "wdld", "hqldA", "hqldB",
                 "r2w", "o2w", "x2ldA", "x2ldB",
                 "pd2", "xchw", "agw4", "cc4", "sd2", "agr4", "cc5", "aldA", "aldB",
                 "hld", "zrd", "zrdB", "r3w", "o3w", "x3ldA", "x3ldB", "prm4",
                 "a1r4", "b14", "ar6w", "ar6r", "cc6", "ar7w", "ar7r", "cc7", "hpwA", "hpwB",
                 "hpldA", "hpldB", "w1ld4A", "w1ld4B", "w2ld4", "x4ldA", "x4ldB", "o4w"]
    KINDS = ["te", "v", "s", "p"] + DMA_NAMES
    PHYS = {k: stack.enter_context(nc.semaphore("sem_" + k)) for k in KINDS}
    NAME2KIND = {
        **{n: n for n in DMA_NAMES},
        "cst": "p", "g0": "p", "g1": "p", "vone": "p", "mgz": "p", "stup": "p", "z1": "p", "ph1": "p", "ph2": "p",
        "fin": "p", "done": "p",
        "te": "te", "frq": "te", "xtr": "te", "proj": "te", "btr": "te", "etr": "te",
        "etr2": "te", "qktr": "te", "scor": "te", "inj": "te", "edec": "te",
        "lttr": "te", "cmb0": "te", "ctr": "te", "mgd": "te", "boc0": "te",
        "gbc0": "te", "bbc0": "te", "cqtr": "te", "corr": "te", "ytr": "te", "oproj": "te",
        "bbc": "v", "frqe": "v", "frq2": "v", "var": "v", "ln1": "v", "b0": "v",
        "b2": "v", "btr2": "v", "lam": "v", "lrel": "v", "ett": "v", "ett2": "v",
        "scal": "v", "ksc": "v", "qkev": "v", "ssev": "v", "numev": "v", "stupv": "v",
        "wpat": "v", "lttc": "v", "cmb2": "v", "wpc": "v", "mgv": "v", "mgj": "v",
        "mge": "p", "boc": "v", "gbc": "v", "bb2": "v", "cscal": "v", "cqev": "v",
        "cnume": "v", "yt": "v", "ytev": "v", "zt": "v", "zv": "v", "zr": "v",
        "x1": "v", "xtev": "v",
        "pq": "v", "pqs": "s", "ptr": "te", "pev": "v", "pabs": "v", "pabt": "te",
        "pabe": "v", "pabe2": "v", "agc1": "v", "w1q": "v", "w1tr": "te", "w1ev": "v",
        "s3st": "s", "s3ln": "v", "s3tr": "te", "s3ev": "v", "sprm": "v", "sprs": "s",
        "sptr": "te", "kcub": "v", "kcs": "s", "kfin": "v", "gln": "v", "gls": "s",
        "glt": "te", "arev": "v", "atr": "te", "aev": "v", "smm": "te", "sev": "v",
        "scan": "v", "zas": "v", "zw": "pd2", "rfin": "v", "ph3": "p", "ph4": "p",
        "szro": "p", "riot": "p", "rtg": "v", "rts": "s", "sev2": "v", "smm2": "te",
        "f4st": "s", "f4ln": "v", "f4mx": "v", "f4mx2": "v", "f4tr": "te", "f4tr2": "te",
        "f4ev": "v", "q4": "v", "q4s": "s", "g4mm": "te", "g4e": "v", "gel4": "s",
        "h4q": "p", "h4qv": "v", "h4tr": "te", "h4ev": "v", "d4mm": "te", "d4e": "v", "arbc": "te", "w1c": "v",
        "xchw": "xchw", "agw4": "agw4", "cc4": "cc4", "agr4": "agr4", "cc5": "cc5", "prm4": "prm4",
        "aldA": "aldA", "aldB": "aldB", "hld": "hld", "zrd": "zrd", "zrdB": "zrdB", "r3w": "r3w",
        "o3w": "o3w", "x3ldA": "x3ldA", "x3ldB": "x3ldB",
        "f2st": "s", "f2ln": "s", "f2q": "v", "f2tr": "te", "f2ev": "v",
        "gmm": "te", "silu": "s", "hmul": "v", "hq": "v", "hqtr": "te", "hqev": "v",
        "dmm": "te", "dev": "v", "r2x": "v", "rcp6": "v",
        "ivf": "s", "tab": "s", "stats": "s", "std": "s", "xuse": "s", "phi0": "v",
        "phi1": "s", "phi": "v", "rope": "v", "b1": "s", "b3": "s", "erow": "s",
        "erow2": "s", "wp1": "s", "vev": "s", "zst": "s", "zs2": "s", "zn": "s",
        "cxln": "s", "xtevs": "s", "qkevs": "s", "cmb1": "s",
    }
    phys_count = {k: 0 for k in KINDS}
    name_val = {}
    sems = {}
    def inc(instr, name, k=1):
        kind = NAME2KIND[name]
        instr.then_inc(PHYS[kind], k)
        phys_count[kind] += k
        sems[name] = sems.get(name, 0) + k
        name_val.setdefault(name, {})[sems[name]] = (kind, phys_count[kind])
        return sems[name]
    def wait(eng, name, v=None):
        kind = NAME2KIND[name]
        if v is None:
            assert name in sems, ("wait before any inc", name)
            pv = phys_count[kind]
        else:
            assert v in name_val.get(name, {}), ("wait on unknown ticket", name, v, sems.get(name))
            pv = name_val[name][v][1]
        eng.wait_ge(PHYS[kind], pv)

    sbt = lambda name, shape, dtype=F32: stack.enter_context(nc.sbuf_tensor(name, shape, dtype))

    _DRAIN_OPS = {"tensor_copy", "tensor_tensor", "tensor_scalar", "scalar_tensor_tensor",
                  "tensor_single_scalar", "reciprocal", "tensor_tensor_scan", "activation",
                  "copy", "add", "mul", "memset", "iota", "tensor_reduce", "reduce_sum",
                  "reduce_max", "dma_start", "collective_compute", "tensor_sub", "tensor_add",
                  "tensor_mul", "select", "transpose", "stream_shuffle"}

    class _DrainWrap:
        def __init__(self, eng):
            self._e = eng
            self._n = 0
        def __getattr__(self, k):
            f = getattr(self._e, k)
            if k in _DRAIN_OPS:
                def g(*a, **kw):
                    if self._n > 0:
                        self._e.drain()
                    self._n += 1
                    return f(*a, **kw)
                return g
            return f

    _WV = _DrainWrap(nc.vector)
    _WS = _DrainWrap(nc.scalar)
    _WG = _DrainWrap(nc.gpsimd)

    class _WrapBlk:
        def __init__(self, b):
            self._b = b
        def vector(self, fn):
            return self._b.vector(lambda e: fn(_WV))
        def scalar(self, fn):
            return self._b.scalar(lambda e: fn(_WS))
        def gpsimd(self, fn):
            return self._b.gpsimd(lambda e: fn(_WG))
        def tensor(self, fn):
            return self._b.tensor(fn)
        def sync(self, fn):
            return self._b.sync(fn)

    IDN = sbt("IDNs", [128, 128])
    CMASK = sbt("CMASKs", [128, 128])
    CPAT = sbt("CPATs", [32, 128])
    HSEL = sbt("HSELs", [32, 1040])
    ONES1 = sbt("ONES1", [1, 128])
    PS = [stack.enter_context(nc.psum_tensor("psb%d" % _i, [128, 512], F32)) for _i in range(8)]
    blk = _WrapBlk(stack.enter_context(nc.Block()))

    @blk.sync
    def _(s):
        inc(s.dma_start(out=IDN[:, :], in_=t["IDN"][:, :]), "cld", 16)
        inc(s.dma_start(out=CMASK[:, :], in_=t["CMASK"][:, :]), "cld", 16)
        inc(s.dma_start(out=CPAT[:, :], in_=t["CPAT"][:, :]), "cld", 16)
        inc(s.dma_start(out=HSEL[:, :], in_=t["HSEL"][:, :]), "cld", 16)
    @blk.gpsimd
    def _(g):
        inc(g.memset(ONES1[:, :], 1.0), "cst", 1)

    st2 = ExitStack()
    sb2 = lambda name, shape, dtype=F32: st2.enter_context(nc.sbuf_tensor(name, shape, dtype))
    XIN = sb2("XIN", [128, 2048])
    XLN = sb2("XLN", [128, 1024])
    XT = sb2("XT", [128, 1024])
    WQ = sb2("WQ", [128, 8 * 1024])
    QT = sb2("QT", [128, 1024])
    TMPA = sb2("TMPA", [128, 1024])
    QS = sb2("QS", [128, 2048])
    QTT = sb2("QTT", [128, 2048])
    MU = sb2("MU", [128, 96])
    NUMT = sb2("NUMT", [128, 1040])
    RD = sb2("RD", [128, 16])
    ECORR = sb2("ECORR", [128, 512])
    LBT = sb2("LBT", [32, 128])
    st3 = ExitStack()
    sb3 = lambda name, shape, dtype=F32: st3.enter_context(nc.sbuf_tensor(name, shape, dtype))
    WK = sb3("WK", [128, 8 * 1024])
    WV = sb3("WV", [128, 8 * 1024])
    WBt = sb3("WBt", [128, 8 * 32])
    KT = sb3("KT", [128, 1024])
    VB = sb3("VB", [128, 1040])
    KH = sb3("KH", [128, 2048])
    KTT = sb3("KTT", [128, 2048])
    SS = sb3("SS", [128, 256])
    BBC = sb3("BBC", [128, 32])
    LB = sb3("LB", [128, 32])
    LTT = sb3("LTT", [128, 128])
    LROWS = sb3("LROWS", [32, 128])
    STM = sb3("STM", [128, 1040])
    STM2 = sb3("STM2", [128, 1040])
    TAB = sb3("TAB", [128, 1024])
    LAM = sb3("LAM", [32, 2048])
    FRQ = sb3("FRQ", [128, 64])
    RK = sb3("RK", [128, 64])
    RKI = sb3("RKI", [128, 64], I32)
    IVF = sb3("IVF", [1, 32])
    IVI = sb3("IVI", [1, 32], I32)
    BRO = sb3("BRO", [1, 32])

    @blk.sync
    def _(s):
        for j in range(8):
            inc(s.dma_start(out=WQ[:, j * 1024:(j + 1) * 1024], in_=t["L0_attn_Wq"][j * 128:(j + 1) * 128, :]), "wld", 16)
            inc(s.dma_start(out=WK[:, j * 1024:(j + 1) * 1024], in_=t["L0_attn_Wk"][j * 128:(j + 1) * 128, :]), "wld", 16)
            inc(s.dma_start(out=WV[:, j * 1024:(j + 1) * 1024], in_=t["L0_attn_Wv"][j * 128:(j + 1) * 128, :]), "wld", 16)
            inc(s.dma_start(out=WBt[:, j * 32:(j + 1) * 32], in_=t["L0_attn_Wbeta"][j * 128:(j + 1) * 128, :]), "wld", 16)
        inc(s.dma_start(out=BRO[:, :], in_=t["L0_attn_bbeta"][None, :]), "wld", 16)

    @blk.tensor
    def _(te):
        wait(te, "wld"); wait(te, "cld"); wait(te, "cst")
        inc(te.matmul(PS[7][:, 0:32], ONES1[:, :], BRO[:, :], start=True, stop=True), "te", 1)
    @blk.vector
    def _(v):
        wait(v, "te")
        inc(v.tensor_copy(BBC[:, :], PS[7][:, 0:32]), "bbc", 1)

    @blk.gpsimd
    def _(g):
        inc(g.iota(IVI[:, :], [[1, 32]], base=0, channel_multiplier=0), "g0", 1)
        wait(g, "g0")
        inc(g.tensor_copy(IVF[:, :], IVI[:, :]), "g1", 1)
    @blk.scalar
    def _(sc):
        wait(sc, "g1")
        inc(sc.activation(IVF[:, :], IVF[:, :], AF.Exp, scale=float(-np.log(10000.0) / 32.0)), "ivf", 1)
    with nc.sbuf_tensor("POSS", [1, 2048], F32) as POSS:
      @blk.sync
      def _(s):
        inc(s.dma_start(out=POSS[:, :], in_=t["pos"].rearrange("a b -> (a b)").unsqueeze(0)), "posld", 16)
      for i in range(16):
        @blk.tensor
        def _(te, i=i):
            wait(te, "ivf"); wait(te, "bbc"); wait(te, "posld")
            if i > 0:
                wait(te, "frqe", i)
            inc(te.matmul(PS[7][:, 0:32], POSS[:, i * 128:(i + 1) * 128], IVF[:, :], start=True, stop=True), "frq", 1)
        @blk.vector
        def _(v, i=i):
            wait(v, "frq")
            v.tensor_copy(FRQ[:, 0:32], PS[7][:, 0:32])
            inc(v.tensor_scalar(FRQ[:, 32:64], FRQ[:, 0:32], 1.0, PI / 2.0, ALU.mult, ALU.add), "frqe", 1)
            v.tensor_scalar(RK[:, :], FRQ[:, :], INV2PI, None, ALU.mult)
            v.tensor_copy(RKI[:, :], RK[:, :])
            v.tensor_copy(RK[:, :], RKI[:, :])
            v.scalar_tensor_tensor(FRQ[:, :], RK[:, :], -CW1, FRQ[:, :], ALU.mult, ALU.add)
            v.scalar_tensor_tensor(FRQ[:, :], RK[:, :], -CW2, FRQ[:, :], ALU.mult, ALU.add)
            v.scalar_tensor_tensor(FRQ[:, :], RK[:, :], -CW3, FRQ[:, :], ALU.mult, ALU.add)
            v.tensor_single_scalar(RK[:, :], FRQ[:, :], PI, ALU.is_gt)
            v.scalar_tensor_tensor(FRQ[:, :], RK[:, :], -(CW1 + CW2), FRQ[:, :], ALU.mult, ALU.add)
            v.tensor_single_scalar(RK[:, :], FRQ[:, :], -PI, ALU.is_lt)
            v.scalar_tensor_tensor(FRQ[:, :], RK[:, :], (CW1 + CW2), FRQ[:, :], ALU.mult, ALU.add)
            inc(v.tensor_scalar(FRQ[:, :], FRQ[:, :], 3.1415925, -3.1415925, ALU.min, ALU.max), "frq2", 1)
        @blk.scalar
        def _(sc, i=i):
            wait(sc, "frq2")
            sc.activation(TAB[:, i * 64 + 32:i * 64 + 64], FRQ[:, 0:32], AF.Sin)
            inc(sc.activation(TAB[:, i * 64:i * 64 + 32], FRQ[:, 32:64], AF.Sin), "tab", 1)

    for i in range(16):
        @blk.sync
        def _(s, i=i):
            if i >= 2:
                wait(s, "stats", i - 1)
            inc(s.dma_start(out=XIN[:, (i % 2) * 1024:(i % 2) * 1024 + 1024], in_=t["x_in"][i * 128:(i + 1) * 128, :]), "xldA" if i % 2 == 0 else "xldB", 16)
        @blk.scalar
        def _(sc, i=i):
            wait(sc, "xldA" if i % 2 == 0 else "xldB", (i // 2 + 1) * 16)
            sc.activation(XLN[:, :], XIN[:, (i % 2) * 1024:(i % 2) * 1024 + 1024], AF.Identity, accum_out=MU[:, i:i + 1])
            inc(sc.activation(XLN[:, :], XIN[:, (i % 2) * 1024:(i % 2) * 1024 + 1024], AF.Square, accum_out=MU[:, 16 + i:17 + i]), "stats", 1)
    @blk.vector
    def _(v):
        wait(v, "stats")
        v.tensor_scalar(MU[:, 32:48], MU[:, 0:16], 1.0 / D, None, ALU.mult)
        v.tensor_scalar(MU[:, 48:64], MU[:, 16:32], 1.0 / D, None, ALU.mult)
        v.tensor_tensor(MU[:, 80:96], MU[:, 32:48], MU[:, 32:48], ALU.mult)
        v.tensor_tensor(MU[:, 48:64], MU[:, 48:64], MU[:, 80:96], ALU.subtract)
        inc(v.tensor_scalar(MU[:, 48:64], MU[:, 48:64], LN_EPS, None, ALU.add), "var", 1)
    @blk.scalar
    def _(sc):
        wait(sc, "var")
        inc(sc.activation(MU[:, 48:64], MU[:, 48:64], AF.Sqrt), "std", 1)
    @blk.vector
    def _(v):
        wait(v, "std")
        v.reciprocal(MU[:, 48:64], MU[:, 48:64])
        v.tensor_tensor(MU[:, 64:80], MU[:, 32:48], MU[:, 48:64], ALU.mult)
        inc(v.tensor_scalar(MU[:, 64:80], MU[:, 64:80], -1.0, None, ALU.mult), "ln1", 1)

    @blk.gpsimd
    def _(g):
        g.memset(STM[:, :], 0.0)
        inc(g.memset(STM2[:, :], 0.0), "stup", 1)

    for i in range(n_loc):
        @blk.sync
        def _(s, i=i):
            wait(s, "ln1")
            if i >= 2:
                wait(s, "xuse", i - 1)
            inc(s.dma_start(out=XIN[:, (i % 2) * 1024:(i % 2) * 1024 + 1024], in_=t["x_in"][i * 128:(i + 1) * 128, :]), "xld2A" if i % 2 == 0 else "xld2B", 16)
        @blk.scalar
        def _(sc, i=i):
            wait(sc, "xld2A" if i % 2 == 0 else "xld2B", (i // 2 + 1) * 16)
            inc(sc.activation(XLN[:, :], XIN[:, (i % 2) * 1024:(i % 2) * 1024 + 1024], AF.Identity,
                              bias=MU[:, 64 + i:65 + i], scale=MU[:, 48 + i:49 + i]), "xuse", 1)
        @blk.tensor
        def _(te, i=i):
            wait(te, "xuse")
            wait(te, "stup", i + 1)
            for j in range(4):
                te.transpose(PS[0][:, j * 128:(j + 1) * 128], XLN[:, j * 128:(j + 1) * 128], IDN[:, :])
            for j in range(3):
                te.transpose(PS[1][:, j * 128:(j + 1) * 128], XLN[:, 512 + j * 128:512 + (j + 1) * 128], IDN[:, :])
            inc(te.transpose(PS[1][:, 384:512], XLN[:, 896:1024], IDN[:, :]), "xtr", 1)
        @blk.vector
        def _(v, i=i):
            wait(v, "xtr")
            inc(v.tensor_copy(XT[:, 0:512], PS[0][:, :]), "xtev", 1)
        @blk.scalar
        def _(sc, i=i):
            wait(sc, "xtr")
            inc(sc.copy(XT[:, 512:1024], PS[1][:, :]), "xtevs", 1)
        @blk.tensor
        def _(te, i=i):
            wait(te, "xtev"); wait(te, "xtevs")
            mm = None
            for j in range(8):
                st = (j == 0); sp = (j == 7)
                lhsT = XT[:, j * 128:(j + 1) * 128]
                te.matmul(PS[2][:, :], lhsT, WQ[:, j * 1024:j * 1024 + 512], start=st, stop=sp)
                te.matmul(PS[3][:, :], lhsT, WQ[:, j * 1024 + 512:j * 1024 + 1024], start=st, stop=sp)
                te.matmul(PS[4][:, :], lhsT, WK[:, j * 1024:j * 1024 + 512], start=st, stop=sp)
                te.matmul(PS[5][:, :], lhsT, WK[:, j * 1024 + 512:j * 1024 + 1024], start=st, stop=sp)
                te.matmul(PS[6][:, :], lhsT, WV[:, j * 1024:j * 1024 + 512], start=st, stop=sp)
                te.matmul(PS[7][:, :], lhsT, WV[:, j * 1024 + 512:j * 1024 + 1024], start=st, stop=sp)
                mm = te.matmul(PS[0][:, 0:32], lhsT, WBt[:, j * 32:(j + 1) * 32], start=st, stop=sp)
            inc(mm, "proj", 1)
        @blk.vector
        def _(v, i=i):
            wait(v, "proj")
            wait(v, "tab", 16)
            if i > 0:
                wait(v, "qpk", 16 * i)
            cos = TAB[:, i * 64:i * 64 + 32].unsqueeze(1).broadcast_to([128, 8, 32])
            sin = TAB[:, i * 64 + 32:i * 64 + 64].unsqueeze(1).broadcast_to([128, 8, 32])
            for dst, blo, bhi in ((QT, 2, 3), (KT, 4, 5)):
                for half, bank in ((0, blo), (1, bhi)):
                    pr = PS[bank][:, :].rearrange("p (h d) -> p h d", d=64)
                    u1, u2 = pr[:, :, 0:32], pr[:, :, 32:64]
                    do = dst[:, half * 512:(half + 1) * 512].rearrange("p (h d) -> p h d", d=64)
                    o1, o2 = do[:, :, 0:32], do[:, :, 32:64]
                    ta = TMPA[:, 0:256].rearrange("p (h d) -> p h d", d=32)
                    v.tensor_tensor(ta, u2, sin, ALU.mult)
                    v.tensor_tensor(o1, u1, cos, ALU.mult)
                    v.tensor_tensor(o1, o1, ta, ALU.subtract)
                    v.tensor_tensor(ta, u1, sin, ALU.mult)
                    v.tensor_tensor(o2, u2, cos, ALU.mult)
                    v.tensor_tensor(o2, o2, ta, ALU.add)
            inc(nc.vector.drain(), "rope", 1)
        @blk.vector
        def _(v, i=i):
            wait(v, "rope")
            v.tensor_single_scalar(QS[:, 0:1024], QT[:, :], 0.0, ALU.min)
            inc(v.tensor_single_scalar(QS[:, 1024:2048], KT[:, :], 0.0, ALU.min), "phi0", 1)
        @blk.scalar
        def _(sc, i=i):
            wait(sc, "phi0")
            sc.activation(QS[:, 0:1024], QS[:, 0:1024], AF.Exp)
            inc(sc.activation(QS[:, 1024:2048], QS[:, 1024:2048], AF.Exp), "phi1", 1)
        @blk.vector
        def _(v, i=i):
            wait(v, "phi1")
            v.scalar_tensor_tensor(QT[:, :], QT[:, :], 0.0, QS[:, 0:1024], ALU.max, ALU.add)
            inc(v.scalar_tensor_tensor(KT[:, :], KT[:, :], 0.0, QS[:, 1024:2048], ALU.max, ALU.add), "phi", 1)
        @blk.gpsimd
        def _(g, i=i):
            wait(g, "proj")
            inc(g.memset(VB[:, :].rearrange("p (h e) -> p h e", e=65)[:, :, 64:65], 1.0), "vone", 1)
        @blk.scalar
        def _(sc, i=i):
            wait(sc, "vone"); wait(sc, "phi1")
            sc.copy(VB[:, :].rearrange("p (h e) -> p h e", e=65)[:, 0:8, 0:64], PS[6][:, :].rearrange("p (h d) -> p h d", d=64))
            inc(sc.copy(VB[:, :].rearrange("p (h e) -> p h e", e=65)[:, 8:16, 0:64], PS[7][:, :].rearrange("p (h d) -> p h d", d=64)), "vev", 1)
        @blk.gpsimd
        def _(g, i=i):
            wait(g, "phi")
            inc(g.dma_start(out=t["q_park"][i], in_=QT[:, :]), "qpk", 16)
        @blk.vector
        def _(v, i=i):
            wait(v, "proj")
            inc(v.tensor_tensor(LB[:, :], PS[0][:, 0:32], BBC[:, :], ALU.add), "b0", 1)
        @blk.scalar
        def _(sc, i=i):
            wait(sc, "b0")
            inc(sc.activation(LB[:, :], LB[:, :], AF.Sigmoid), "b1", 1)
        @blk.vector
        def _(v, i=i):
            wait(v, "b1")
            inc(v.tensor_scalar(LB[:, :], LB[:, :], 0.999, 0.001, ALU.min, ALU.max), "b2", 1)
        @blk.scalar
        def _(sc, i=i):
            wait(sc, "b2")
            inc(sc.activation(LB[:, :], LB[:, :], AF.Ln), "b3", 1)
        @blk.tensor
        def _(te, i=i):
            wait(te, "b3"); wait(te, "vev")
            inc(te.transpose(PS[7][0:32, 0:128], LB[:, :], IDN[:, :]), "btr", 1)
        @blk.vector
        def _(v, i=i):
            wait(v, "btr")
            inc(v.tensor_copy(LBT[:, :], PS[7][0:32, 0:128]), "btr2", 1)
        @blk.vector
        def _(v, i=i):
            wait(v, "btr2")
            init = 0.0 if i == 0 else LAM[:, i * 128 - 1:i * 128]
            inc(v.tensor_tensor_scan(LAM[:, i * 128:(i + 1) * 128], LBT[:, :], LBT[:, :], init, ALU.add, ALU.bypass), "lam", 1)
        @blk.vector
        def _(v, i=i):
            wait(v, "lam")
            if i == 0:
                v.tensor_copy(LROWS[:, :], LAM[:, 0:128])
            else:
                v.tensor_scalar(LROWS[:, :], LAM[:, i * 128:(i + 1) * 128], LAM[:, i * 128 - 1:i * 128], None, ALU.subtract)
            inc(nc.vector.drain(), "lrel", 1)
        for slot, (scale_, use_bias, dst0) in enumerate((( 1.0, False, 0), (-1.0, False, 32), (-1.0, True, 64))):
            @blk.scalar
            def _(sc, i=i, slot=slot, scale_=scale_, use_bias=use_bias):
                wait(sc, "lrel")
                if slot > 0:
                    wait(sc, "ett", sems["ett"])
                kw = dict(scale=scale_)
                if use_bias:
                    kw["bias"] = LROWS[:, 127:128]
                inc(sc.activation(LBT[:, :], LROWS[:, :], AF.Exp, **kw), "erow", 1)
            @blk.tensor
            def _(te, i=i, slot=slot):
                wait(te, "erow")
                inc(te.transpose(PS[7][:, 0:32], LBT[:, :], IDN[0:32, 0:32]), "etr", 1)
            @blk.vector
            def _(v, i=i, slot=slot, dst0=dst0):
                wait(v, "etr")
                inc(v.tensor_copy(LTT[:, dst0:dst0 + 32], PS[7][:, 0:32]), "ett", 1)
        @blk.scalar
        def _(sc, i=i):
            wait(sc, "ett")
            inc(sc.activation(LBT[:, :], LAM[:, i * 128:(i + 1) * 128], AF.Exp), "erow2", 1)
        @blk.tensor
        def _(te, i=i):
            wait(te, "erow2")
            inc(te.transpose(PS[7][:, 0:32], LBT[:, :], IDN[0:32, 0:32]), "etr2", 1)
        @blk.vector
        def _(v, i=i):
            wait(v, "etr2")
            inc(v.tensor_copy(ECORR[:, i * 32:(i + 1) * 32], PS[7][:, 0:32]), "ett2", 1)
        @blk.vector
        def _(v, i=i):
            wait(v, "ett2"); wait(v, "phi")
            for c in range(2):
                v.tensor_tensor(QS[:, c * 1024:(c + 1) * 1024].rearrange("p (h d) -> p h d", d=64),
                                QT[:, :].rearrange("p (h d) -> p h d", d=64),
                                LTT[:, c:32:2].unsqueeze(2).broadcast_to([128, 16, 64]), ALU.mult)
                v.tensor_tensor(KH[:, c * 1024:(c + 1) * 1024].rearrange("p (h d) -> p h d", d=64),
                                KT[:, :].rearrange("p (h d) -> p h d", d=64),
                                LTT[:, 64 + c:96:2].unsqueeze(2).broadcast_to([128, 16, 64]), ALU.mult)
            inc(nc.vector.drain(), "scal", 1)
        for c in range(2):
            @blk.vector
            def _(v, i=i, c=c):
                wait(v, "scal")
                if c == 1:
                    wait(v, "qktr")
                inc(v.tensor_tensor(XLN[:, :].rearrange("p (h d) -> p h d", d=64),
                                    KT[:, :].rearrange("p (h d) -> p h d", d=64),
                                    LTT[:, 32 + c:64:2].unsqueeze(2).broadcast_to([128, 16, 64]), ALU.mult), "ksc", 1)
            @blk.tensor
            def _(te, i=i, c=c):
                wait(te, "ksc")
                if c == 1:
                    wait(te, "qkev"); wait(te, "qkevs")
                for j in range(4):
                    te.transpose(PS[2][:, j * 128:(j + 1) * 128], QS[:, c * 1024 + j * 128:c * 1024 + (j + 1) * 128], IDN[:, :])
                    te.transpose(PS[3][:, j * 128:(j + 1) * 128], QS[:, c * 1024 + 512 + j * 128:c * 1024 + 512 + (j + 1) * 128], IDN[:, :])
                    te.transpose(PS[4][:, j * 128:(j + 1) * 128], XLN[:, j * 128:(j + 1) * 128], IDN[:, :])
                    mm = te.transpose(PS[5][:, j * 128:(j + 1) * 128], XLN[:, 512 + j * 128:512 + (j + 1) * 128], IDN[:, :])
                inc(mm, "qktr", 1)
            @blk.vector
            def _(v, i=i, c=c):
                wait(v, "qktr")
                v.tensor_copy(QTT[:, c * 1024:c * 1024 + 512], PS[2][:, :])
                inc(v.tensor_copy(QTT[:, c * 1024 + 512:c * 1024 + 1024], PS[3][:, :]), "qkev", 1)
            @blk.scalar
            def _(sc, i=i, c=c):
                wait(sc, "qktr")
                sc.copy(KTT[:, c * 1024:c * 1024 + 512], PS[4][:, :])
                inc(sc.copy(KTT[:, c * 1024 + 512:c * 1024 + 1024], PS[5][:, :]), "qkevs", 1)
        for h in range(H):
            base = (h % 2) * 64
            jt = h // 2
            nb_i, ncol = num_slot(h)
            nb = 4 + nb_i
            ib = (7, 0, 1)[nb_i]
            @blk.tensor
            def _(te, i=i, h=h, base=base, jt=jt):
                wait(te, "qkev"); wait(te, "qkevs")
                if h > 0:
                    wait(te, "ssev")
                else:
                    wait(te, "xtev"); wait(te, "xtevs")
                    wait(te, "b0")
                te.matmul(PS[2][:, 0:128], KTT[base:base + 64, jt * 128:(jt + 1) * 128],
                          QTT[base:base + 64, jt * 128:(jt + 1) * 128], start=True, stop=True)
                inc(te.matmul(PS[3][:, 0:128], KTT[base:base + 64, 1024 + jt * 128:1024 + (jt + 1) * 128],
                              QTT[base:base + 64, 1024 + jt * 128:1024 + (jt + 1) * 128], start=True, stop=True), "scor", 1)
            @blk.vector
            def _(v, i=i, h=h):
                wait(v, "scor")
                v.tensor_tensor(SS[:, 0:128], PS[2][:, 0:128], CMASK[:, :], ALU.mult)
                inc(v.tensor_tensor(SS[:, 128:256], PS[3][:, 0:128], CMASK[:, :], ALU.mult), "ssev", 1)
            @blk.tensor
            def _(te, i=i, h=h, base=base, jt=jt, nb=nb, ncol=ncol, ib=ib):
                wait(te, "ssev")
                te.matmul(PS[nb][:, ncol:ncol + 65], SS[:, 0:128], VB[:, h * 65:(h + 1) * 65], start=True, stop=False)
                te.matmul(PS[nb][:, ncol:ncol + 65], SS[:, 128:256], VB[:, h * 65:(h + 1) * 65], start=False, stop=False)
                stm_a = STM if base == 0 else STM2
                stm_b = STM2 if base == 0 else STM
                te.matmul(PS[nb][:, ncol:ncol + 65], QTT[base:base + 64, jt * 128:(jt + 1) * 128],
                          stm_a[base:base + 64, h * 65:(h + 1) * 65], start=False, stop=False)
                te.matmul(PS[nb][:, ncol:ncol + 65], QTT[base:base + 64, 1024 + jt * 128:1024 + (jt + 1) * 128],
                          stm_b[base:base + 64, h * 65:(h + 1) * 65], start=False, stop=True)
                te.matmul(PS[ib][0:64, ncol:ncol + 65], KH[:, h * 64:(h + 1) * 64], VB[:, h * 65:(h + 1) * 65],
                          start=True, stop=True)
                inc(te.matmul(PS[ib][64:128, ncol:ncol + 65], KH[:, 1024 + h * 64:1024 + (h + 1) * 64], VB[:, h * 65:(h + 1) * 65],
                              start=True, stop=True, tile_position=(0, 64)), "inj", 1)
        @blk.vector
        def _(v, i=i):
            wait(v, "inj")
            if i > 0:
                wait(v, "numpk", 16 * i)
            v.tensor_copy(NUMT[:, 0:455], PS[4][:, 0:455])
            v.tensor_copy(NUMT[:, 455:910], PS[5][:, 0:455])
            inc(v.tensor_copy(NUMT[:, 910:1040], PS[6][:, 0:130]), "numev", 1)
        @blk.gpsimd
        def _(g, i=i):
            wait(g, "numev")
            inc(g.dma_start(out=t["num_park"][i], in_=NUMT[:, :]), "numpk", 16)
        @blk.scalar
        def _(sc, i=i):
            wait(sc, "numev")
            inc(sc.activation(LBT[:, 0:1], LROWS[:, 127:128], AF.Exp), "wp1", 1)
        @blk.vector
        def _(v, i=i):
            wait(v, "wp1")
            inc(v.tensor_scalar(LTT[0:32, 0:128], CPAT[:, :], LBT[:, 0:1], None, ALU.mult), "wpat", 1)
        for sp, (nb_i, col0, w) in enumerate(SPANS):
            ib = (7, 0, 1)[nb_i]
            @blk.tensor
            def _(te, i=i, sp=sp, col0=col0, w=w):
                wait(te, "wpat"); wait(te, "ssev")
                if sp > 0:
                    wait(te, "stupv")
                inc(te.matmul(PS[2][:, 0:w], LTT[0:32, 0:128], HSEL[:, col0:col0 + w], start=True, stop=True), "edec", 1)
            @blk.vector
            def _(v, i=i, sp=sp, col0=col0, w=w, ib=ib):
                wait(v, "edec")
                v.tensor_tensor(STM[:, col0:col0 + w], STM[:, col0:col0 + w], PS[2][:, 0:w], ALU.mult)
                inc(v.tensor_tensor(STM[:, col0:col0 + w], STM[:, col0:col0 + w], PS[ib][:, 0:w], ALU.add), "stupv", 1)
        @blk.gpsimd
        def _(g, i=i):
            wait(g, "stupv")
            inc(g.dma_start(out=STM2[0:64, :], in_=STM[64:128, :]), "stupd", 16)
            inc(g.dma_start(out=STM2[64:128, :], in_=STM[0:64, :]), "stupd", 16)
        @blk.gpsimd
        def _(g, i=i):
            wait(g, "stupd")
            inc(nc.gpsimd.drain(), "stup", 1)

    def SEC(cond, deco):
        return deco if cond else (lambda fn: None)

    def _stub_out():
        @blk.gpsimd
        def _(g):
            for nm in ["qpk", "numpk", "r1w", "stupd", "agw", "xld2A", "xld2B"]:
                if nm in sems:
                    wait(g, nm)
            g.memset(TMPA[:, :], 0.0)
            for i2 in range(16):
                inc(g.dma_start(out=t["out"][i2 * 128:(i2 + 1) * 128, :], in_=TMPA[:, :]), "outw", 16)
            wait(g, "outw")
            inc(nc.gpsimd.drain(), "done", 1)

    if not do_ag:
        _stub_out()
        st3.close()
        st2.close()
        return nc, t, stack, sems

    @blk.tensor
    def _(te):
        wait(te, "stup")
        inc(te.transpose(PS[2][0:1, 0:32], LAM[:, 2047:2048], IDN[0:32, 0:32]), "lttr", 1)
    @blk.vector
    def _(v):
        wait(v, "lttr")
        inc(v.tensor_copy(LBT[0:1, 0:32], PS[2][0:1, 0:32]), "lttc", 1)
    @blk.gpsimd
    def _(g):
        wait(g, "stup"); wait(g, "lttc")
        g.memset(LTT[:, 0:32], 0.0)
        inc(g.dma_start(out=t["ag3_in"][:, 0:1040], in_=STM[:, :]), "agw", 16)
        inc(g.dma_start(out=t["ag3_in"][1:128, 1040:1072], in_=LTT[1:128, 0:32]), "agw", 16)
        inc(g.dma_start(out=t["ag3_in"][0:1, 1040:1072], in_=LBT[0:1, 0:32]), "agw", 16)
    @blk.gpsimd
    def _(g):
        wait(g, "agw")
        inc(g.collective_compute("AllGather", ALU.bypass, replica_groups=[list(range(8))],
                                 ins=[t["ag3_in"].opt()], outs=[t["ag3_out"].opt()]), "cc3", 1)
    st3.close()
    MGE = sb2("MGE", [128, 1040])
    MGE2 = sb2("MGE2", [128, 1040])
    with nc.sbuf_tensor("LT8", [8, 32], F32) as LT8, nc.sbuf_tensor("PRD", [8, 1], F32) as PRD, \
         nc.sbuf_tensor("BWT", [8, 8], F32) as BWT, nc.sbuf_tensor("COEF", [8, 32], F32) as COEF, \
         nc.sbuf_tensor("MS", [128, 1040], F32) as MS, nc.sbuf_tensor("WPC", [32, 128], F32) as WPC:
        @blk.sync
        def _(s):
            wait(s, "cc3")
            inc(s.dma_start(out=LT8[:, :], in_=bass_rust.AP(t["ag3_out"].tensor, 1040, [[128 * 1072, 8], [1, 32]])), "agr", 16)
            inc(s.dma_start(out=PRD[:, :], in_=t["chain_pred"][:, :]), "agr", 16)
            inc(s.dma_start(out=BWT[:, :], in_=t["chain_betwT"][:, :]), "agr", 16)
        @blk.tensor
        def _(te):
            wait(te, "agr")
            inc(te.matmul(PS[2][0:8, 0:32], BWT[:, :], LT8[:, :], start=True, stop=True), "cmb0", 1)
        @blk.scalar
        def _(sc):
            wait(sc, "cmb0")
            inc(sc.activation(COEF[:, :], PS[2][0:8, 0:32], AF.Exp), "cmb1", 1)
        @blk.vector
        def _(v):
            wait(v, "cmb1")
            inc(v.tensor_scalar(COEF[:, :], COEF[:, :], PRD[:, 0:1], None, ALU.mult), "cmb2", 1)
        @blk.gpsimd
        def _(g):
            g.memset(MGE[:, :], 0.0)
            inc(nc.gpsimd.drain(), "mgz", 1)
        for j in range(8):
            @blk.sync
            def _(s, j=j):
                wait(s, "cc3")
                if j > 0:
                    wait(s, "mgj", j)
                inc(s.dma_start(out=MS[:, :], in_=t["ag3_out"][j * 128:(j + 1) * 128, 0:1040]), "msl", 16)
            @blk.tensor
            def _(te, j=j):
                wait(te, "msl"); wait(te, "cmb2"); wait(te, "mgz")
                inc(te.transpose(PS[3][0:32, 0:8], COEF[:, :], IDN[0:8, 0:8]), "ctr", 1)
            @blk.vector
            def _(v, j=j):
                wait(v, "ctr")
                inc(v.tensor_scalar(WPC[:, :], CPAT[:, :], PS[3][0:32, j:j + 1], None, ALU.mult), "wpc", 1)
            for sp, (nb_i, col0, w) in enumerate(SPANS):
                @blk.tensor
                def _(te, j=j, sp=sp, col0=col0, w=w):
                    wait(te, "wpc")
                    if sp > 0:
                        wait(te, "mgv")
                    inc(te.matmul(PS[2][:, 0:w], WPC[:, :], HSEL[:, col0:col0 + w], start=True, stop=True), "mgd", 1)
                @blk.vector
                def _(v, j=j, sp=sp, col0=col0, w=w):
                    wait(v, "mgd")
                    v.tensor_tensor(MS[:, col0:col0 + w] if False else TMPA[:, 0:w], MS[:, col0:col0 + w], PS[2][:, 0:w], ALU.mult)
                    mgv = v.tensor_tensor(MGE[:, col0:col0 + w], MGE[:, col0:col0 + w], TMPA[:, 0:w], ALU.add)
                    inc(mgv, "mgv", 1)
            @blk.vector
            def _(v, j=j):
                wait(v, "mgv")
                inc(nc.vector.drain(), "mgj", 1)
        @blk.gpsimd
        def _(g):
            wait(g, "mgj", 8)
            inc(g.dma_start(out=MGE2[0:64, :], in_=MGE[64:128, :]), "mged", 16)
            inc(g.dma_start(out=MGE2[64:128, :], in_=MGE[0:64, :]), "mged", 16)
        @blk.gpsimd
        def _(g):
            wait(g, "mged")
            inc(nc.gpsimd.drain(), "mge", 1)

    WO = WQ
    Z1 = sb2("Z1", [1, 128])
    BOC = sb2("BOC", [128, 1024])
    GBC = sb2("GBC", [128, 1024])
    BBC2 = sb2("BBC2", [128, 1024])
    YT = sb2("YT", [128, 1024])
    with nc.sbuf_tensor("BROW", [1, 1024], F32) as BROW:
        @blk.sync
        def _(s):
            wait(s, "mge")
            for j in range(8):
                inc(s.dma_start(out=WO[:, j * 1024:(j + 1) * 1024], in_=t["L0_attn_Wo"][j * 128:(j + 1) * 128, :]), "wold", 16)
            inc(s.dma_start(out=BROW[:, :], in_=t["L0_attn_bo"][None, :]), "wold", 16)
        @blk.tensor
        def _(te):
            wait(te, "wold")
            te.matmul(PS[2][:, :], ONES1[:, :], BROW[:, 0:512], start=True, stop=True)
            inc(te.matmul(PS[3][:, :], ONES1[:, :], BROW[:, 512:1024], start=True, stop=True), "boc0", 1)
        @blk.vector
        def _(v):
            wait(v, "boc0")
            v.tensor_copy(BOC[:, 0:512], PS[2][:, :])
            inc(v.tensor_copy(BOC[:, 512:1024], PS[3][:, :]), "boc", 1)
        @blk.sync
        def _(s):
            wait(s, "boc")
            inc(s.dma_start(out=BROW[:, :], in_=t["L0_attn_ln_gamma"][None, :]), "gld", 16)
        @blk.tensor
        def _(te):
            wait(te, "gld")
            te.matmul(PS[2][:, :], ONES1[:, :], BROW[:, 0:512], start=True, stop=True)
            inc(te.matmul(PS[3][:, :], ONES1[:, :], BROW[:, 512:1024], start=True, stop=True), "gbc0", 1)
        @blk.vector
        def _(v):
            wait(v, "gbc0")
            v.tensor_copy(GBC[:, 0:512], PS[2][:, :])
            inc(v.tensor_copy(GBC[:, 512:1024], PS[3][:, :]), "gbc", 1)
        @blk.sync
        def _(s):
            wait(s, "gbc")
            inc(s.dma_start(out=BROW[:, :], in_=t["L0_attn_ln_beta"][None, :]), "bld", 16)
        @blk.tensor
        def _(te):
            wait(te, "bld")
            te.matmul(PS[2][:, :], ONES1[:, :], BROW[:, 0:512], start=True, stop=True)
            inc(te.matmul(PS[3][:, :], ONES1[:, :], BROW[:, 512:1024], start=True, stop=True), "bbc0", 1)
        @blk.vector
        def _(v):
            wait(v, "bbc0")
            v.tensor_copy(BBC2[:, 0:512], PS[2][:, :])
            inc(v.tensor_copy(BBC2[:, 512:1024], PS[3][:, :]), "bb2", 1)

    @blk.gpsimd
    def _(g):
        inc(g.memset(Z1[:, :], 0.0), "z1", 1)

    if not do_fin:
        _stub_out()
        st2.close()
        return nc, t, stack, sems

    for i in range(n_fin):
        @blk.sync
        def _(s, i=i):
            wait(s, "bb2")
            if i < 2:
                wait(s, "qpk")
                wait(s, "numpk")
            if i > 0:
                wait(s, "cscal", i)
                wait(s, "yt", i)
            if i >= 2:
                wait(s, "x1", i - 1)
            nm = "cqldA" if i % 2 == 0 else "cqldB"
            inc(s.dma_start(out=QT[:, :], in_=t["q_park"][i]), nm, 16)
            inc(s.dma_start(out=NUMT[:, :], in_=t["num_park"][i]), nm, 16)
            inc(s.dma_start(out=XIN[:, (i % 2) * 1024:(i % 2) * 1024 + 1024], in_=t["x_in"][i * 128:(i + 1) * 128, :]), nm, 16)
        @blk.vector
        def _(v, i=i):
            wait(v, "cqldA" if i % 2 == 0 else "cqldB", 48 * (i // 2 + 1))
            for c in range(2):
                v.tensor_tensor(QS[:, c * 1024:(c + 1) * 1024].rearrange("p (h d) -> p h d", d=64),
                                QT[:, :].rearrange("p (h d) -> p h d", d=64),
                                ECORR[:, i * 32 + c:i * 32 + 32:2].unsqueeze(2).broadcast_to([128, 16, 64]), ALU.mult)
            inc(nc.vector.drain(), "cscal", 1)
        for c in range(2):
            @SEC(fstage >= 2, blk.tensor)
            def _(te, i=i, c=c):
                wait(te, "cscal")
                if c == 1 or i > 0:
                    wait(te, "cqev")
                for j in range(4):
                    te.transpose(PS[2][:, j * 128:(j + 1) * 128], QS[:, c * 1024 + j * 128:c * 1024 + (j + 1) * 128], IDN[:, :])
                    mm = te.transpose(PS[3][:, j * 128:(j + 1) * 128], QS[:, c * 1024 + 512 + j * 128:c * 1024 + 512 + (j + 1) * 128], IDN[:, :])
                inc(mm, "cqtr", 1)
            @SEC(fstage >= 2, blk.vector)
            def _(v, i=i, c=c):
                wait(v, "cqtr")
                v.tensor_copy(QTT[:, c * 1024:c * 1024 + 512], PS[2][:, :])
                inc(v.tensor_copy(QTT[:, c * 1024 + 512:c * 1024 + 1024], PS[3][:, :]), "cqev", 1)
        @SEC(fstage >= 3 or fstage == 36, blk.tensor)
        def _(te, i=i):
            wait(te, "cqev"); wait(te, "z1")
            if i > 0:
                wait(te, "cnume")
                wait(te, "zt", i)
            mm = None
            for h in range(H):
                base = (h % 2) * 64
                jt = h // 2
                nb_i, ncol = num_slot(h)
                nb = 4 + nb_i
                mge_a = MGE if base == 0 else MGE2
                mge_b = MGE2 if base == 0 else MGE
                te.matmul(PS[nb][:, ncol:ncol + 65], Z1[:, :], ONES1[:, 0:65], start=True, stop=False)
                te.matmul(PS[nb][:, ncol:ncol + 65], QTT[base:base + 64, jt * 128:(jt + 1) * 128],
                          mge_a[base:base + 64, h * 65:(h + 1) * 65], start=False, stop=False)
                mm = te.matmul(PS[nb][:, ncol:ncol + 65], QTT[base:base + 64, 1024 + jt * 128:1024 + (jt + 1) * 128],
                               mge_b[base:base + 64, h * 65:(h + 1) * 65], start=False, stop=True)
            inc(mm, "corr", 1)
        @SEC(fstage >= 35, blk.vector)
        def _(v, i=i):
            wait(v, "corr")
            v.tensor_tensor(NUMT[:, 0:455], NUMT[:, 0:455], PS[4][:, 0:455], ALU.add)
            v.tensor_tensor(NUMT[:, 455:910], NUMT[:, 455:910], PS[5][:, 0:455], ALU.add)
            inc(v.tensor_tensor(NUMT[:, 910:1040], NUMT[:, 910:1040], PS[6][:, 0:130], ALU.add), "cnume", 1)
        @SEC(fstage >= 4, blk.vector)
        def _(v, i=i):
            wait(v, "cnume")
            v.tensor_scalar(RD[:, :], NUMT[:, :].rearrange("p (h e) -> p h e", e=65)[:, :, 64], EPS_DN, None, ALU.add)
            v.reciprocal(RD[:, :], RD[:, :])
            inc(v.tensor_tensor(YT[:, :].rearrange("p (h d) -> p h d", d=64),
                                NUMT[:, :].rearrange("p (h e) -> p h e", e=65)[:, :, 0:64],
                                RD[:, :].unsqueeze(2).broadcast_to([128, 16, 64]), ALU.mult), "yt", 1)
        @SEC(fstage >= 5, blk.tensor)
        def _(te, i=i):
            wait(te, "yt")
            for j in range(4):
                te.transpose(PS[2][:, j * 128:(j + 1) * 128], YT[:, j * 128:(j + 1) * 128], IDN[:, :])
                mm = te.transpose(PS[3][:, j * 128:(j + 1) * 128], YT[:, 512 + j * 128:512 + (j + 1) * 128], IDN[:, :])
            inc(mm, "ytr", 1)
        @SEC(fstage >= 5, blk.vector)
        def _(v, i=i):
            wait(v, "ytr")
            v.tensor_copy(QS[:, 0:512], PS[2][:, :])
            inc(v.tensor_copy(QS[:, 512:1024], PS[3][:, :]), "ytev", 1)
        @SEC(fstage >= 6, blk.tensor)
        def _(te, i=i):
            wait(te, "ytev")
            for j in range(8):
                st = (j == 0); sp = (j == 7)
                lhsT = QS[:, j * 128:(j + 1) * 128]
                te.matmul(PS[5][:, :], lhsT, WO[:, j * 1024:j * 1024 + 512], start=st, stop=sp)
                mm = te.matmul(PS[6][:, :], lhsT, WO[:, j * 1024 + 512:j * 1024 + 1024], start=st, stop=sp)
            inc(mm, "oproj", 1)
        @SEC(fstage >= 7, blk.scalar)
        def _(sc, i=i):
            wait(sc, "cqldA" if i % 2 == 0 else "cqldB", 48 * (i // 2 + 1))
            inc(sc.activation(XLN[:, :], XIN[:, (i % 2) * 1024:(i % 2) * 1024 + 1024], AF.Identity,
                              bias=MU[:, 64 + i:65 + i], scale=MU[:, 48 + i:49 + i]), "cxln", 1)
        @SEC(fstage >= 7, blk.vector)
        def _(v, i=i):
            wait(v, "oproj"); wait(v, "cxln")
            v.tensor_tensor(YT[:, 0:512], XLN[:, 0:512], PS[5][:, :], ALU.add)
            v.tensor_tensor(YT[:, 512:1024], XLN[:, 512:1024], PS[6][:, :], ALU.add)
            inc(v.tensor_tensor(YT[:, :], YT[:, :], BOC[:, :], ALU.add), "zt", 1)
        @SEC(fstage >= 8, blk.scalar)
        def _(sc, i=i):
            wait(sc, "zt")
            if i > 0:
                wait(sc, "r1w", 16 * i)
                if upto == "l0a":
                    wait(sc, "outw", 16 * i)
            sc.activation(TMPA[:, :], YT[:, :], AF.Identity, accum_out=MU[:, 80:81])
            inc(sc.activation(TMPA[:, :], YT[:, :], AF.Square, accum_out=MU[:, 81:82]), "zst", 1)
        @SEC(fstage >= 8, blk.vector)
        def _(v, i=i):
            wait(v, "zst")
            v.tensor_scalar(MU[:, 82:83], MU[:, 80:81], 1.0 / D, None, ALU.mult)
            v.tensor_scalar(MU[:, 83:84], MU[:, 81:82], 1.0 / D, None, ALU.mult)
            v.tensor_tensor(MU[:, 84:85], MU[:, 82:83], MU[:, 82:83], ALU.mult)
            v.tensor_tensor(MU[:, 83:84], MU[:, 83:84], MU[:, 84:85], ALU.subtract)
            inc(v.tensor_scalar(MU[:, 83:84], MU[:, 83:84], LN_EPS, None, ALU.add), "zv", 1)
        @SEC(fstage >= 8, blk.scalar)
        def _(sc, i=i):
            wait(sc, "zv")
            inc(sc.activation(MU[:, 83:84], MU[:, 83:84], AF.Sqrt), "zs2", 1)
        @SEC(fstage >= 8, blk.vector)
        def _(v, i=i):
            wait(v, "zs2")
            v.reciprocal(MU[:, 83:84], MU[:, 83:84])
            v.tensor_tensor(MU[:, 84:85], MU[:, 82:83], MU[:, 83:84], ALU.mult)
            inc(v.tensor_scalar(MU[:, 84:85], MU[:, 84:85], -1.0, None, ALU.mult), "zr", 1)
        @SEC(fstage >= 8, blk.scalar)
        def _(sc, i=i):
            wait(sc, "zr")
            inc(sc.activation(TMPA[:, :], YT[:, :], AF.Identity,
                              bias=MU[:, 84:85], scale=MU[:, 83:84]), "zn", 1)
        @SEC(fstage >= 8, blk.vector)
        def _(v, i=i):
            wait(v, "zn")
            v.tensor_tensor(TMPA[:, :], TMPA[:, :], GBC[:, :], ALU.mult)
            v.tensor_tensor(TMPA[:, :], TMPA[:, :], BBC2[:, :], ALU.add)
            inc(v.tensor_tensor(TMPA[:, :], TMPA[:, :], XIN[:, (i % 2) * 1024:(i % 2) * 1024 + 1024], ALU.add), "x1", 1)
        @SEC(fstage >= 8, blk.gpsimd)
        def _(g, i=i):
            wait(g, "x1")
            inc(g.dma_start(out=t["resid1"][i], in_=TMPA[:, :]), "r1w", 16)
            if upto == "l0a":
                inc(g.dma_start(out=t["out"][i * 128:(i + 1) * 128, :], in_=TMPA[:, :]), "outw", 16)
            inc(nc.gpsimd.drain(), "fin", 1)

    if upto == "l0a":
        @blk.gpsimd
        def _(g):
            if n_fin < 16:
                if "r1w" in sems:
                    wait(g, "r1w")
                if "outw" in sems:
                    wait(g, "outw")
                g.memset(TMPA[:, :], 0.0)
                for i2 in range(n_fin, 16):
                    inc(g.dma_start(out=t["out"][i2 * 128:(i2 + 1) * 128, :], in_=TMPA[:, :]), "outw", 16)
            wait(g, "outw")
            if "r1w" in sems:
                wait(g, "r1w")
            wait(g, "qpk")
            wait(g, "numpk")
            inc(nc.gpsimd.drain(), "done", 1)
        st2.close()
        return nc, t, stack, sems

    @blk.gpsimd
    def _(g):
        wait(g, "fin")
        wait(g, "r1w")
        wait(g, "qpk")
        wait(g, "numpk")
        inc(nc.gpsimd.drain(), "ph1", 1)
    st2.close()

    AX = mybir.AxisListType.X

    stp = ExitStack()
    sbp = lambda name, shape, dtype=F32: stp.enter_context(nc.sbuf_tensor(name, shape, dtype))
    PIN = sbp("PIN", [128, I1])
    POUT = sbp("POUT", [128, I0])
    PTMP = sbp("PTMP", [128, I1])
    PTI = sbp("PTI", [128, I1], I32)
    PSGN = sbp("PSGN", [128, I0])
    PQz = sbp("PQz", [128, I1])
    PRz = sbp("PRz", [128, I1])
    PPP = sbp("PPP", [128, I0])
    PSCL = sbp("PSCL", [128, 384])
    PST = sbp("PST", [128, 1024])
    PSTB = sbp("PSTB", [128, 1024], BF16)
    RC6 = sbp("RC6", [1, 16])
    RCB = sbp("RCB", [128, 16])
    AMX = sbp("AMX", [128, 16])
    RSB = sbp("RSB", [128, 8])

    @blk.gpsimd
    def _(g):
        wait(g, "ph1")
        g.memset(RC6[:, :], 0.0)
        g.memset(RC6[:, 0:1], 6.0)
        g.memset(RC6[:, 1:2], 127.0)
        g.memset(RC6[:, 2:3], 255.0)
        inc(nc.gpsimd.drain(), "cst", 1)
    @blk.vector
    def _(v):
        wait(v, "cst")
        inc(v.reciprocal(RC6[:, 4:7], RC6[:, 0:3]), "rcp6", 1)
    @blk.tensor
    def _(te):
        wait(te, "rcp6"); wait(te, "ph1")
        inc(te.matmul(PS[1][:, 0:16], ONES1[:, :], RC6[:, :], start=True, stop=True), "pabt", 1)
    @blk.vector
    def _(v):
        wait(v, "pabt")
        inc(v.tensor_copy(RCB[:, :], PS[1][:, 0:16]), "rcp6", 1)

    def emit_fp4q(P_, N, IN, OUT, SCL, TMP, TMPI, SGN, Q, R, PP, extra_wait=None):
        nb = N // 16
        @blk.vector
        def _(v):
            if extra_wait is not None:
                extra_wait(v)
            v.tensor_reduce(SCL[0:P_, 0:nb].unsqueeze(2), IN[0:P_, 0:N].rearrange("p (b k) -> p b k", k=16),
                            AX, ALU.max, apply_absolute_value=True)
            v.tensor_scalar(SCL[0:P_, 0:nb], SCL[0:P_, 0:nb], RCB[0:P_, 4:5], 1e-12, ALU.mult, ALU.max)
            v.reciprocal(SCL[0:P_, 176:176 + nb], SCL[0:P_, 0:nb])
            v.tensor_tensor(TMP[0:P_, 0:N].rearrange("p (b k) -> p b k", k=16),
                            IN[0:P_, 0:N].rearrange("p (b k) -> p b k", k=16),
                            SCL[0:P_, 176:176 + nb].unsqueeze(2).broadcast_to([P_, nb, 16]), ALU.mult)
            inc(nc.vector.drain(), "pq", 1)
        @blk.scalar
        def _(sc):
            wait(sc, "pq")
            sc.activation(SGN[0:P_, 0:N], TMP[0:P_, 0:N], AF.Sign)
            inc(sc.activation(TMP[0:P_, 0:N], TMP[0:P_, 0:N], AF.Abs), "pqs", 1)
        @blk.vector
        def _(v):
            wait(v, "pqs")
            v.tensor_scalar(OUT[0:P_, 0:N], TMP[0:P_, 0:N], 2.0, 0.5, ALU.mult, ALU.add)
            v.tensor_copy(TMPI[0:P_, 0:N], OUT[0:P_, 0:N])
            v.tensor_copy(Q[0:P_, 0:N], TMPI[0:P_, 0:N])
            v.tensor_tensor(R[0:P_, 0:N], Q[0:P_, 0:N], OUT[0:P_, 0:N], ALU.is_gt)
            v.tensor_tensor(Q[0:P_, 0:N], Q[0:P_, 0:N], R[0:P_, 0:N], ALU.subtract)
            v.tensor_scalar(OUT[0:P_, 0:N], TMP[0:P_, 0:N], 1.0, 0.5, ALU.mult, ALU.add)
            v.tensor_copy(TMPI[0:P_, 0:N], OUT[0:P_, 0:N])
            v.tensor_copy(R[0:P_, 0:N], TMPI[0:P_, 0:N])
            v.tensor_tensor(PP[0:P_, 0:N], R[0:P_, 0:N], OUT[0:P_, 0:N], ALU.is_gt)
            v.tensor_tensor(R[0:P_, 0:N], R[0:P_, 0:N], PP[0:P_, 0:N], ALU.subtract)
            v.tensor_scalar(OUT[0:P_, 0:N], Q[0:P_, 0:N], 0.5, None, ALU.mult)
            v.tensor_single_scalar(PP[0:P_, 0:N], TMP[0:P_, 0:N], 2.25, ALU.is_gt)
            v.tensor_tensor(Q[0:P_, 0:N], R[0:P_, 0:N], OUT[0:P_, 0:N], ALU.subtract)
            v.tensor_tensor(PP[0:P_, 0:N], PP[0:P_, 0:N], Q[0:P_, 0:N], ALU.mult)
            v.tensor_tensor(OUT[0:P_, 0:N], OUT[0:P_, 0:N], PP[0:P_, 0:N], ALU.add)
            v.tensor_single_scalar(PP[0:P_, 0:N], TMP[0:P_, 0:N], 4.5, ALU.is_gt)
            v.tensor_single_scalar(Q[0:P_, 0:N], TMP[0:P_, 0:N], 5.0, ALU.is_gt)
            v.tensor_scalar(Q[0:P_, 0:N], Q[0:P_, 0:N], 2.0, 4.0, ALU.mult, ALU.add)
            v.tensor_tensor(Q[0:P_, 0:N], Q[0:P_, 0:N], R[0:P_, 0:N], ALU.subtract)
            v.tensor_tensor(PP[0:P_, 0:N], PP[0:P_, 0:N], Q[0:P_, 0:N], ALU.mult)
            v.tensor_tensor(OUT[0:P_, 0:N], OUT[0:P_, 0:N], PP[0:P_, 0:N], ALU.add)
            v.tensor_tensor(OUT[0:P_, 0:N], OUT[0:P_, 0:N], SGN[0:P_, 0:N], ALU.mult)
            v.tensor_tensor(OUT[0:P_, 0:N].rearrange("p (b k) -> p b k", k=16),
                            OUT[0:P_, 0:N].rearrange("p (b k) -> p b k", k=16),
                            SCL[0:P_, 0:nb].unsqueeze(2).broadcast_to([P_, nb, 16]), ALU.mult)
            inc(nc.vector.drain(), "pq", 1)

    @blk.sync
    def _(s):
        wait(s, "ph1")
        for u in range(4):
            inc(s.dma_start(out=PIN[:, u * 1024:(u + 1) * 1024], in_=t["w1_sl"][u * 128:(u + 1) * 128, :]), "pwld", 16)
    @blk.vector
    def _(v):
        wait(v, "pwld")
        v.tensor_reduce(AMX[:, 0:4].unsqueeze(2), PIN[:, 0:4096].rearrange("p (a n) -> p a n", a=4), AX, ALU.max, apply_absolute_value=True)
        inc(v.tensor_reduce(AMX[:, 4:5].unsqueeze(2), AMX[:, 0:4].unsqueeze(1), AX, ALU.max, apply_absolute_value=True), "pabs", 1)
    @blk.sync
    def _(s):
        wait(s, "pabs")
        inc(s.dma_start(out=PIN[:, 0:4096], in_=t["w2_sl"][:, :]), "pwld", 16)
    @blk.vector
    def _(v):
        wait(v, "pwld")
        inc(v.tensor_reduce(AMX[:, 5:6].unsqueeze(2), PIN[:, 0:4096].unsqueeze(1), AX, ALU.max, apply_absolute_value=True), "pabs", 1)
    @blk.tensor
    def _(te):
        wait(te, "pabs")
        inc(te.transpose(PS[0][0:2, 0:128], AMX[:, 4:6], IDN[:, :]), "pabt", 1)
    @blk.vector
    def _(v):
        wait(v, "pabt")
        inc(v.tensor_reduce(AMX[0:2, 1:2].unsqueeze(2), PS[0][0:2, 0:128].unsqueeze(1), AX, ALU.max), "pabe2", 1)
    @blk.gpsimd
    def _(g):
        wait(g, "pabe2")
        g.memset(AMX[0:1, 2:16], 0.0)
        inc(g.dma_start(out=t["ag1_in"][0:1, 2:16], in_=AMX[0:1, 2:16]), "ag1w", 16)
        inc(g.dma_start(out=t["ag1_in"][0:1, 0:2], in_=AMX[0:2, 1:2]), "ag1w", 16)
        wait(g, "ag1w")
        inc(g.collective_compute("AllGather", ALU.bypass, replica_groups=[list(range(8))],
                                 ins=[t["ag1_in"].opt()], outs=[t["ag1_out"].opt()]), "cc1", 1)
    @blk.sync
    def _(s):
        wait(s, "cc1")
        inc(s.dma_start(out=AMX[0:8, 0:2], in_=t["ag1_out"][:, 0:2]), "ag1r", 16)
    @blk.tensor
    def _(te):
        wait(te, "ag1r")
        inc(te.transpose(PS[0][0:2, 0:8], AMX[0:8, 0:2], IDN[0:8, 0:8]), "pabt", 1)
    @blk.vector
    def _(v):
        wait(v, "pabt")
        inc(v.tensor_reduce(AMX[0:2, 8:9].unsqueeze(2), PS[0][0:2, 0:8].unsqueeze(1), AX, ALU.max), "pabe", 1)
    @blk.tensor
    def _(te):
        wait(te, "pabe")
        inc(te.transpose(PS[1][0:1, 0:2], AMX[0:2, 8:9], IDN[0:2, 0:2]), "pabt", 1)
    @blk.vector
    def _(v):
        wait(v, "pabt")
        v.tensor_copy(RC6[:, 8:10], PS[1][0:1, 0:2])
        v.tensor_tensor(RC6[:, 10:12], RC6[:, 8:10], RC6[:, 5:6].unsqueeze(2).broadcast_to([1, 2, 1])[:, :, 0], ALU.mult)
        v.tensor_single_scalar(RC6[:, 10:12], RC6[:, 10:12], 1e-8, ALU.max)
        v.reciprocal(RC6[:, 12:14], RC6[:, 10:12])
        inc(nc.vector.drain(), "pabe", 1)
    @blk.tensor
    def _(te):
        wait(te, "pabe")
        inc(te.matmul(PS[1][:, 0:8], ONES1[:, :], RC6[:, 8:16], start=True, stop=True), "pabt", 1)
    @blk.vector
    def _(v):
        wait(v, "pabt")
        inc(v.tensor_copy(RSB[:, :], PS[1][:, 0:8]), "agc1", 1)

    def prep_fp4_slice(src_name, nrows, NN, off_base, colstride, col0_fn):
        nsub = (nrows + 127) // 128
        for u in range(nsub):
            p_ = min(128, nrows - u * 128)
            @blk.sync
            def _(s, u=u, p_=p_):
                if "pq" in sems:
                    wait(s, "pq")
                inc(s.dma_start(out=PIN[0:p_, 0:NN], in_=t[src_name][u * 128:u * 128 + p_, :]), "pwld", 16)
            def ew(v, u=u):
                wait(v, "pwld")
            emit_fp4q(p_, NN, PIN, POUT, PSCL, PTMP, PTI, PSGN, PQz, PRz, PPP, extra_wait=ew)
            for kt in range(NN // 128):
                @blk.tensor
                def _(te, u=u, p_=p_, kt=kt):
                    wait(te, "pq")
                    if "pev" in sems and kt % 4 == 0:
                        wait(te, "pev")
                    inc(te.transpose(PS[kt % 4][:, 0:p_], POUT[0:p_, kt * 128:(kt + 1) * 128], IDN[0:p_, 0:p_]), "ptr", 1)
                @blk.vector
                def _(v, u=u, p_=p_, kt=kt):
                    wait(v, "ptr")
                    if "ag2w" in sems and kt % 4 == 0:
                        wait(v, "ag2w")
                    inc(v.tensor_copy(PST[:, (kt % 4) * 256:(kt % 4) * 256 + p_], PS[kt % 4][:, 0:p_]), "pev", 1)
                @blk.gpsimd
                def _(g, u=u, p_=p_, kt=kt):
                    wait(g, "pev")
                    dst = bass_rust.AP(t["ag2a_in"].tensor,
                                      off_base + kt * 128 * colstride + col0_fn(u),
                                      [[colstride, 128], [1, p_]])
                    inc(g.dma_start(out=dst, in_=PST[:, (kt % 4) * 256:(kt % 4) * 256 + p_]), "ag2w", 16)
    prep_fp4_slice("wg_sl", 352, D, OFF_G, 352, lambda u: u * 128)
    prep_fp4_slice("wu_sl", 352, D, OFF_U, 352, lambda u: u * 128)
    prep_fp4_slice("wd_sl", 128, I0, OFF_DN, 128, lambda u: 0)

    def prep_int_slice(src_name, nrows, NN, off_base, colstride, amx_row):
        nsub = (nrows + 127) // 128
        for u in range(nsub):
            @blk.sync
            def _(s, u=u):
                if "w1q" in sems:
                    wait(s, "w1q")
                else:
                    wait(s, "pq")
                inc(s.dma_start(out=PIN[:, 0:NN], in_=t[src_name][u * 128:(u + 1) * 128, :]), "pwld", 16)
            @blk.vector
            def _(v, u=u):
                wait(v, "pwld"); wait(v, "agc1")
                v.tensor_scalar(PTMP[:, 0:NN], PIN[:, 0:NN], RSB[:, 4 + amx_row:5 + amx_row], None, ALU.mult)
                v.tensor_copy(PTI[:, 0:NN], PTMP[:, 0:NN])
                v.tensor_copy(PQz[:, 0:NN], PTI[:, 0:NN])
                v.tensor_tensor(PRz[:, 0:NN], PTMP[:, 0:NN], PQz[:, 0:NN], ALU.subtract)
                v.tensor_single_scalar(PRz[:, 0:NN], PRz[:, 0:NN], 0.5, ALU.is_ge)
                v.tensor_tensor(PQz[:, 0:NN], PQz[:, 0:NN], PRz[:, 0:NN], ALU.add)
                v.tensor_tensor(PRz[:, 0:NN], PQz[:, 0:NN], PTMP[:, 0:NN], ALU.subtract)
                v.tensor_single_scalar(PRz[:, 0:NN], PRz[:, 0:NN], 0.5, ALU.is_gt)
                v.tensor_tensor(PQz[:, 0:NN], PQz[:, 0:NN], PRz[:, 0:NN], ALU.subtract)
                v.tensor_scalar(PQz[:, 0:NN], PQz[:, 0:NN], 127.0, -127.0, ALU.min, ALU.max)
                inc(nc.vector.drain(), "w1q", 1)
            for kt in range(NN // 128):
                @blk.tensor
                def _(te, u=u, kt=kt):
                    wait(te, "w1q")
                    if "w1ev" in sems and kt % 4 == 0:
                        wait(te, "w1ev")
                    inc(te.transpose(PS[4 + kt % 4][:, 0:128], PQz[:, kt * 128:(kt + 1) * 128], IDN[:, :]), "w1tr", 1)
                @blk.vector
                def _(v, u=u, kt=kt):
                    wait(v, "w1tr")
                    if "ag2w" in sems and kt % 4 == 0:
                        wait(v, "ag2w")
                    inc(v.tensor_copy(PSTB[:, (kt % 4) * 128:(kt % 4) * 128 + 128], PS[4 + kt % 4][:, 0:128]), "w1ev", 1)
                @blk.gpsimd
                def _(g, u=u, kt=kt):
                    wait(g, "w1ev")
                    dst = bass_rust.AP(t["ag2b_in"].tensor,
                                      off_base + kt * 128 * colstride + u * 128,
                                      [[colstride, 128], [1, 128]])
                    inc(g.dma_start(out=dst, in_=PSTB[:, (kt % 4) * 128:(kt % 4) * 128 + 128]), "ag2w", 16)
    prep_int_slice("w1_sl", 512, D, OFF_W1, 512, 0)
    prep_int_slice("w2_sl", 128, I1, OFF_W2, 128, 1)

    @blk.gpsimd
    def _(g):
        wait(g, "ag2w")
        inc(g.collective_compute("AllGather", ALU.bypass, replica_groups=[list(range(8))],
                                 ins=[t["ag2a_in"].opt()], outs=[t["ag2a_out"].opt()]), "cc2a", 1)
        inc(g.collective_compute("AllGather", ALU.bypass, replica_groups=[list(range(8))],
                                 ins=[t["ag2b_in"].opt()], outs=[t["ag2b_out"].opt()]), "cc2b", 1)

    @blk.gpsimd
    def _(g):
        wait(g, "ag2w"); wait(g, "pwld"); wait(g, "ag1w"); wait(g, "ag1r")
        wait(g, "w1q"); wait(g, "pq"); wait(g, "agc1")
        inc(nc.gpsimd.drain(), "ph2", 1)

    stp.close()

    if upto == "prep":
        @blk.gpsimd
        def _(g):
            wait(g, "cc2a"); wait(g, "cc2b")
            with nc.sbuf_tensor("ZOUT", [128, 1024], F32) as ZO:
                g.memset(ZO[:, :], 0.0)
                for i2 in range(16):
                    inc(g.dma_start(out=t["out"][i2 * 128:(i2 + 1) * 128, :], in_=ZO[:, :]), "outw", 16)
                wait(g, "outw")
                inc(nc.gpsimd.drain(), "done", 1)
        return nc, t, stack, sems

    stf = ExitStack()
    sbf = lambda name, shape, dtype=F32: stf.enter_context(nc.sbuf_tensor(name, shape, dtype))
    XF = sbf("XF", [128, 2048])
    XL2 = sbf("XL2", [128, 1024])
    FOUT = sbf("FOUT", [128, 1024])
    FT = sbf("FT", [128, 1024])
    FTI = sbf("FTI", [128, 1024], I32)
    FSGN = sbf("FSGN", [128, 1024])
    FQ2 = sbf("FQ2", [128, 1024])
    FR2 = sbf("FR2", [128, 1024])
    FP2 = sbf("FP2", [128, 1024])
    FSC = sbf("FSC", [128, 144])
    MU2 = sbf("MU2", [128, 8])
    RCB2 = sbf("RCB2", [128, 16])
    HQ = sbf("HQ", [128, 352])
    HB = sbf("HB", [128, 1056])
    SIL = sbf("SIL", [128, 352])
    HS = sbf("HS", [128, 384])

    @blk.gpsimd
    def _(g):
        wait(g, "ph2")
        g.memset(FSC[0:1, 128:144], 0.0)
        g.memset(FSC[0:1, 128:129], 6.0)
        inc(nc.gpsimd.drain(), "cst", 1)
    @blk.vector
    def _(v):
        wait(v, "cst")
        inc(v.reciprocal(FSC[0:1, 129:130], FSC[0:1, 128:129]), "rcp6", 1)
    @blk.tensor
    def _(te):
        wait(te, "rcp6"); wait(te, "pev"); wait(te, "w1ev")
        inc(te.matmul(PS[1][:, 0:16], ONES1[:, :], FSC[0:1, 128:144], start=True, stop=True), "pabt", 1)
    @blk.vector
    def _(v):
        wait(v, "pabt")
        inc(v.tensor_copy(RCB2[:, :], PS[1][:, 0:16]), "rcp6", 1)

    def emit_fp4q2(N, IN, OUT, extra_wait=None):
        nb = N // 16
        @blk.vector
        def _(v):
            if extra_wait is not None:
                extra_wait(v)
            v.tensor_reduce(FSC[:, 0:nb].unsqueeze(2), IN[:, 0:N].rearrange("p (b k) -> p b k", k=16),
                            AX, ALU.max, apply_absolute_value=True)
            v.tensor_scalar(FSC[:, 0:nb], FSC[:, 0:nb], RCB2[:, 1:2], 1e-12, ALU.mult, ALU.max)
            v.reciprocal(FSC[:, 64:64 + nb], FSC[:, 0:nb])
            v.tensor_tensor(FT[:, 0:N].rearrange("p (b k) -> p b k", k=16),
                            IN[:, 0:N].rearrange("p (b k) -> p b k", k=16),
                            FSC[:, 64:64 + nb].unsqueeze(2).broadcast_to([128, nb, 16]), ALU.mult)
            inc(nc.vector.drain(), "f2q", 1)
        @blk.scalar
        def _(sc):
            wait(sc, "f2q")
            sc.activation(FSGN[:, 0:N], FT[:, 0:N], AF.Sign)
            inc(sc.activation(FT[:, 0:N], FT[:, 0:N], AF.Abs), "f2st", 1)
        @blk.vector
        def _(v):
            wait(v, "f2st")
            v.tensor_scalar(OUT[:, 0:N], FT[:, 0:N], 2.0, 0.5, ALU.mult, ALU.add)
            v.tensor_copy(FTI[:, 0:N], OUT[:, 0:N])
            v.tensor_copy(FQ2[:, 0:N], FTI[:, 0:N])
            v.tensor_tensor(FR2[:, 0:N], FQ2[:, 0:N], OUT[:, 0:N], ALU.is_gt)
            v.tensor_tensor(FQ2[:, 0:N], FQ2[:, 0:N], FR2[:, 0:N], ALU.subtract)
            v.tensor_scalar(OUT[:, 0:N], FT[:, 0:N], 1.0, 0.5, ALU.mult, ALU.add)
            v.tensor_copy(FTI[:, 0:N], OUT[:, 0:N])
            v.tensor_copy(FR2[:, 0:N], FTI[:, 0:N])
            v.tensor_tensor(FP2[:, 0:N], FR2[:, 0:N], OUT[:, 0:N], ALU.is_gt)
            v.tensor_tensor(FR2[:, 0:N], FR2[:, 0:N], FP2[:, 0:N], ALU.subtract)
            v.tensor_scalar(OUT[:, 0:N], FQ2[:, 0:N], 0.5, None, ALU.mult)
            v.tensor_single_scalar(FP2[:, 0:N], FT[:, 0:N], 2.25, ALU.is_gt)
            v.tensor_tensor(FQ2[:, 0:N], FR2[:, 0:N], OUT[:, 0:N], ALU.subtract)
            v.tensor_tensor(FP2[:, 0:N], FP2[:, 0:N], FQ2[:, 0:N], ALU.mult)
            v.tensor_tensor(OUT[:, 0:N], OUT[:, 0:N], FP2[:, 0:N], ALU.add)
            v.tensor_single_scalar(FP2[:, 0:N], FT[:, 0:N], 4.5, ALU.is_gt)
            v.tensor_single_scalar(FQ2[:, 0:N], FT[:, 0:N], 5.0, ALU.is_gt)
            v.tensor_scalar(FQ2[:, 0:N], FQ2[:, 0:N], 2.0, 4.0, ALU.mult, ALU.add)
            v.tensor_tensor(FQ2[:, 0:N], FQ2[:, 0:N], FR2[:, 0:N], ALU.subtract)
            v.tensor_tensor(FP2[:, 0:N], FP2[:, 0:N], FQ2[:, 0:N], ALU.mult)
            v.tensor_tensor(OUT[:, 0:N], OUT[:, 0:N], FP2[:, 0:N], ALU.add)
            v.tensor_tensor(OUT[:, 0:N], OUT[:, 0:N], FSGN[:, 0:N], ALU.mult)
            v.tensor_tensor(OUT[:, 0:N].rearrange("p (b k) -> p b k", k=16),
                            OUT[:, 0:N].rearrange("p (b k) -> p b k", k=16),
                            FSC[:, 0:nb].unsqueeze(2).broadcast_to([128, nb, 16]), ALU.mult)
            inc(nc.vector.drain(), "f2q", 1)

    stf1 = ExitStack()
    sbf1 = lambda name, shape, dtype=F32: stf1.enter_context(nc.sbuf_tensor(name, shape, dtype))
    QXT = sbf1("QXT", [128, 16384])
    WG = sbf1("WG", [128, 2816])
    WU = sbf1("WU", [128, 2816])

    for i in range(16):
        @blk.sync
        def _(s, i=i):
            wait(s, "ph2")
            if i >= 2:
                wait(s, "f2q", sems["f2q"] - 1)
            inc(s.dma_start(out=XF[:, (i % 2) * 1024:(i % 2 + 1) * 1024], in_=t["resid1"][i]), "x2ldA" if i % 2 == 0 else "x2ldB", 16)
        @blk.scalar
        def _(sc, i=i):
            wait(sc, "x2ldA" if i % 2 == 0 else "x2ldB", 16 * (i // 2 + 1))
            sc.activation(XL2[:, :], XF[:, (i % 2) * 1024:(i % 2 + 1) * 1024], AF.Identity, accum_out=MU2[:, 0:1])
            inc(sc.activation(XL2[:, :], XF[:, (i % 2) * 1024:(i % 2 + 1) * 1024], AF.Square, accum_out=MU2[:, 1:2]), "f2st", 1)
        @blk.vector
        def _(v, i=i):
            wait(v, "f2st")
            v.tensor_scalar(MU2[:, 2:3], MU2[:, 0:1], 1.0 / D, None, ALU.mult)
            v.tensor_scalar(MU2[:, 3:4], MU2[:, 1:2], 1.0 / D, None, ALU.mult)
            v.tensor_tensor(MU2[:, 4:5], MU2[:, 2:3], MU2[:, 2:3], ALU.mult)
            v.tensor_tensor(MU2[:, 3:4], MU2[:, 3:4], MU2[:, 4:5], ALU.subtract)
            inc(v.tensor_scalar(MU2[:, 3:4], MU2[:, 3:4], LN_EPS, None, ALU.add), "f2ln", 1)
        @blk.scalar
        def _(sc, i=i):
            wait(sc, "f2ln")
            inc(sc.activation(MU2[:, 3:4], MU2[:, 3:4], AF.Sqrt), "f2st", 1)
        @blk.vector
        def _(v, i=i):
            wait(v, "f2st")
            v.reciprocal(MU2[:, 3:4], MU2[:, 3:4])
            v.tensor_tensor(MU2[:, 4:5], MU2[:, 2:3], MU2[:, 3:4], ALU.mult)
            inc(v.tensor_scalar(MU2[:, 4:5], MU2[:, 4:5], -1.0, None, ALU.mult), "f2ln", 1)
        @blk.scalar
        def _(sc, i=i):
            wait(sc, "f2ln")
            inc(sc.activation(XL2[:, :], XF[:, (i % 2) * 1024:(i % 2 + 1) * 1024], AF.Identity,
                              bias=MU2[:, 4:5], scale=MU2[:, 3:4]), "f2st", 1)
        def ew(v, i=i):
            wait(v, "f2st")
        emit_fp4q2(1024, XL2, FOUT, extra_wait=ew)
        for jp in range(2):
            @blk.tensor
            def _(te, i=i, jp=jp):
                wait(te, "f2q")
                if "f2ev" in sems:
                    wait(te, "f2ev")
                mm = None
                for j in range(4):
                    mm = te.transpose(PS[jp][:, j * 128:(j + 1) * 128], FOUT[:, jp * 512 + j * 128:jp * 512 + (j + 1) * 128], IDN[:, :])
                inc(mm, "f2tr", 1)
            @blk.vector
            def _(v, i=i, jp=jp):
                wait(v, "f2tr")
                inc(v.tensor_copy(QXT[:, i * 1024 + jp * 512:i * 1024 + (jp + 1) * 512], PS[jp][:, :]), "f2ev", 1)

    QTICK = {}
    HTICK = {}
    def emit_tail(kk):
        c_, i_ = kk // 16, kk % 16
        off = (kk % 3) * 352
        @blk.tensor
        def _(te, c=c_, i=i_, off=off, kk=kk):
            wait(te, "f2q", QTICK[kk])
            if "hqev" in sems:
                wait(te, "hqev")
            te.transpose(PS[4][:, 0:128], HB[:, off:off + 128], IDN[:, :])
            te.transpose(PS[5][:, 0:128], HB[:, off + 128:off + 256], IDN[:, :])
            inc(te.transpose(PS[6][0:96, 0:128], HB[:, off + 256:off + 352], IDN[:, :]), "hqtr", 1)
        @blk.vector
        def _(v, c=c_, i=i_):
            wait(v, "hqtr")
            if "hqpk" in sems:
                wait(v, "hqpk", sems["hqpk"])
            v.tensor_copy(HS[:, 0:128], PS[4][:, 0:128])
            v.tensor_copy(HS[:, 128:256], PS[5][:, 0:128])
            inc(v.tensor_copy(HS[0:96, 256:384], PS[6][0:96, 0:128]), "hqev", 1)
        @blk.gpsimd
        def _(g, c=c_, i=i_):
            wait(g, "hqev")
            inc(g.dma_start(out=t["hqt_park"][c, 0, :, i * 128:(i + 1) * 128], in_=HS[:, 0:128]), "hqpk", 16)
            inc(g.dma_start(out=t["hqt_park"][c, 1, :, i * 128:(i + 1) * 128], in_=HS[:, 128:256]), "hqpk", 16)
            inc(g.dma_start(out=t["hqt_park"][c, 2, 0:96, i * 128:(i + 1) * 128], in_=HS[0:96, 256:384]), "hqpk", 16)
        HTICK[kk] = sems["hqtr"]
    for c in range(8):
        @blk.sync
        def _(s, c=c):
            if c > 0:
                wait(s, "gmm")
            else:
                wait(s, "cc2a")
            for j in range(8):
                src_g = bass_rust.AP(t["ag2a_out"].tensor, c * AG2N + OFF_G + j * 128 * 352, [[352, 128], [1, 352]])
                src_u = bass_rust.AP(t["ag2a_out"].tensor, c * AG2N + OFF_U + j * 128 * 352, [[352, 128], [1, 352]])
                inc(s.dma_start(out=WG[:, j * 352:(j + 1) * 352], in_=src_g), "wgldA", 16)
                inc(s.dma_start(out=WU[:, j * 352:(j + 1) * 352], in_=src_u), "wgldA", 16)
        for i in range(16):
            k14 = c * 16 + i
            bg = 2 if k14 % 2 == 0 else 0
            bu = 3 if k14 % 2 == 0 else 1
            if k14 >= 2:
                emit_tail(k14 - 2)
            @blk.tensor
            def _(te, c=c, i=i, k14=k14, bg=bg, bu=bu):
                wait(te, "wgldA")
                wait(te, "f2ev")
                if k14 >= 2:
                    wait(te, "hmul", sems["hmul"] - 1)
                mm = None
                for j in range(8):
                    st_ = (j == 0); sp_ = (j == 7)
                    lhsT = QXT[:, i * 1024 + j * 128:i * 1024 + (j + 1) * 128]
                    te.matmul(PS[bg][:, 0:352], lhsT, WG[:, j * 352:(j + 1) * 352], start=st_, stop=sp_)
                    mm = te.matmul(PS[bu][:, 0:352], lhsT, WU[:, j * 352:(j + 1) * 352], start=st_, stop=sp_)
                inc(mm, "gmm", 1)
            @blk.scalar
            def _(sc, c=c, i=i, k14=k14, bg=bg):
                wait(sc, "gmm", sems["gmm"])
                inc(sc.activation(SIL[:, :], PS[bg][:, 0:352], AF.Silu), "silu", 1)
            @blk.vector
            def _(v, c=c, i=i, bu=bu):
                wait(v, "silu")
                inc(v.tensor_tensor(HQ[:, :], SIL[:, :], PS[bu][:, 0:352], ALU.mult), "hmul", 1)
            def ewh(v, c=c, i=i, k14=k14):
                wait(v, "hmul")
                if k14 >= 3:
                    wait(v, "hqtr", HTICK[k14 - 3])
            emit_fp4q2(352, HQ, HB[:, (k14 % 3) * 352:(k14 % 3) * 352 + 352], extra_wait=ewh)
            QTICK[k14] = sems["f2q"]
    emit_tail(126)
    emit_tail(127)

    stf1.close()
    stf2 = ExitStack()
    sbf2 = lambda name, shape, dtype=F32: stf2.enter_context(nc.sbuf_tensor(name, shape, dtype))
    WD = sbf2("WD", [128, 24 * 1024])
    HQT = sbf2("HQT", [128, 2 * 3072])
    PCS = [(cc_, sub, 96 if sub == 2 else 128) for cc_ in range(8) for sub in range(3)]
    @blk.sync
    def _(s):
        wait(s, "cc2a")
        wait(s, "hqpk")
        wait(s, "gmm")
        wait(s, "hqev")
        for pi, (cc_, sub, psz) in enumerate(PCS):
            kg = cc_ * 352 + sub * 128
            for cp in range(8):
                srcw = bass_rust.AP(t["ag2a_out"].tensor, cp * AG2N + OFF_DN + kg * 128, [[128, psz], [1, 128]])
                inc(s.dma_start(out=WD[0:psz, pi * 1024 + cp * 128:pi * 1024 + (cp + 1) * 128], in_=srcw), "wdld", 16)
    for i in range(16):
        @blk.sync
        def _(s, i=i):
            if i == 0:
                wait(s, "hqpk")
            if i >= 2:
                wait(s, "dmm")
            nmh = "hqldA" if i % 2 == 0 else "hqldB"
            off = (i % 2) * 3072
            for pi, (cc_, sub, psz) in enumerate(PCS):
                inc(s.dma_start(out=HQT[0:psz, off + pi * 128:off + (pi + 1) * 128],
                                in_=t["hqt_park"][cc_, sub, 0:psz, i * 128:(i + 1) * 128]), nmh, 16)
            inc(s.dma_start(out=XF[:, (i % 2) * 1024:(i % 2 + 1) * 1024], in_=t["resid1"][i]),
                "r2ldA" if i % 2 == 0 else "r2ldB", 16)
        @blk.tensor
        def _(te, i=i):
            wait(te, "hqldA" if i % 2 == 0 else "hqldB", 24 * 16 * (i // 2 + 1))
            wait(te, "wdld")
            if i > 0:
                wait(te, "dev")
            off = (i % 2) * 3072
            mm = None
            for pi, (cc_, sub, psz) in enumerate(PCS):
                st_ = (pi == 0); sp_ = (pi == 23)
                lhsT = HQT[0:psz, off + pi * 128:off + (pi + 1) * 128]
                te.matmul(PS[2][:, :], lhsT, WD[0:psz, pi * 1024:pi * 1024 + 512], start=st_, stop=sp_)
                mm = te.matmul(PS[3][:, :], lhsT, WD[0:psz, pi * 1024 + 512:pi * 1024 + 1024], start=st_, stop=sp_)
            inc(mm, "dmm", 1)
        @blk.vector
        def _(v, i=i):
            wait(v, "dmm")
            wait(v, "r2ldA" if i % 2 == 0 else "r2ldB", 16 * (i // 2 + 1))
            if i > 0:
                wait(v, "r2w", 16 * i)
                if upto == "l0f":
                    wait(v, "o2w", 16 * i)
            v.tensor_tensor(XL2[:, 0:512], XF[:, (i % 2) * 1024:(i % 2) * 1024 + 512], PS[2][:, :], ALU.add)
            inc(v.tensor_tensor(XL2[:, 512:1024], XF[:, (i % 2) * 1024 + 512:(i % 2 + 1) * 1024], PS[3][:, :], ALU.add), "dev", 1)
        @blk.gpsimd
        def _(g, i=i):
            wait(g, "dev")
            inc(g.dma_start(out=t["resid2"][i], in_=XL2[:, :]), "r2w", 16)
            if upto == "l0f":
                inc(g.dma_start(out=t["out"][i * 128:(i + 1) * 128, :], in_=XL2[:, :]), "o2w", 16)

    if upto == "l0f":
        @blk.gpsimd
        def _(g):
            wait(g, "o2w"); wait(g, "r2w")
            inc(nc.gpsimd.drain(), "done", 1)
        stf2.close(); stf.close()
        return nc, t, stack, sems

    stf2.close(); stf.close()

    stq = ExitStack()
    sbq = lambda n_, sh, dt_=F32: stq.enter_context(nc.sbuf_tensor(n_, sh, dt_))
    NMT = sbq("NMT", [64, 1792])
    O_DTR, O_DTI, O_CRE, O_CIM = 0, 128, 256, 384
    O_PW = {8: 512, 16: 768, 32: 1024, 64: 1280, 128: 1536}
    ATB = sbq("ATB", [128, 256])
    ARB = sbq("ARB", [128, 256])
    XHB = sbq("XHB", [128, 256])
    HTB = sbq("HTB", [128, 256])
    URB = sbq("URB", [128, 128])
    UIB = sbq("UIB", [128, 128])
    TS1 = sbq("TS1", [64, 128])
    TS2 = sbq("TS2", [64, 128])
    IOT8I = stq.enter_context(nc.sbuf_tensor("IOT8I", [64, 8], I32))
    IOT8 = sbq("IOT8", [64, 8])

    def emit_cw(v, VW, KI, KF):
        v.tensor_scalar(KF, VW, INV2PI, None, ALU.mult)
        v.tensor_copy(KI, KF)
        v.tensor_copy(KF, KI)
        v.scalar_tensor_tensor(VW, KF, -CW1, VW, ALU.mult, ALU.add)
        v.scalar_tensor_tensor(VW, KF, -CW2, VW, ALU.mult, ALU.add)
        v.scalar_tensor_tensor(VW, KF, -CW3, VW, ALU.mult, ALU.add)
        v.tensor_single_scalar(KF, VW, PI, ALU.is_gt)
        v.scalar_tensor_tensor(VW, KF, -(CW1 + CW2), VW, ALU.mult, ALU.add)
        v.tensor_single_scalar(KF, VW, -PI, ALU.is_lt)
        v.scalar_tensor_tensor(VW, KF, (CW1 + CW2), VW, ALU.mult, ALU.add)
        v.tensor_scalar(VW, VW, 3.1415925, -3.1415925, ALU.min, ALU.max)

    st5 = ExitStack()
    sb5 = lambda n_, sh, dt_=F32: st5.enter_context(nc.sbuf_tensor(n_, sh, dt_))
    XF3 = sb5("XF3", [128, 2048])
    XL3 = sb5("XL3", [128, 1024])
    XT3 = sb5("XT3", [128, 1024])
    MU3 = sb5("MU3", [128, 8])
    for i in range(16):
        @blk.sync
        def _(s, i=i):
            if i == 0:
                wait(s, "r2w")
            if i >= 2:
                wait(s, "s3st", sems["s3st"] - 3)
            inc(s.dma_start(out=XF3[:, (i % 2) * 1024:(i % 2 + 1) * 1024], in_=t["resid2"][i]),
                "x3ldA" if i % 2 == 0 else "x3ldB", 16)
        @blk.scalar
        def _(sc, i=i):
            wait(sc, "x3ldA" if i % 2 == 0 else "x3ldB", sems["x3ldA" if i % 2 == 0 else "x3ldB"])
            sc.activation(XL3[:, :], XF3[:, (i % 2) * 1024:(i % 2 + 1) * 1024], AF.Identity, accum_out=MU3[:, 0:1])
            inc(sc.activation(XL3[:, :], XF3[:, (i % 2) * 1024:(i % 2 + 1) * 1024], AF.Square, accum_out=MU3[:, 1:2]), "s3st", 1)
        @blk.vector
        def _(v, i=i):
            wait(v, "s3st", sems["s3st"])
            v.tensor_scalar(MU3[:, 2:3], MU3[:, 0:1], 1.0 / D, None, ALU.mult)
            v.tensor_scalar(MU3[:, 3:4], MU3[:, 1:2], 1.0 / D, None, ALU.mult)
            v.tensor_tensor(MU3[:, 4:5], MU3[:, 2:3], MU3[:, 2:3], ALU.mult)
            v.tensor_tensor(MU3[:, 3:4], MU3[:, 3:4], MU3[:, 4:5], ALU.subtract)
            inc(v.tensor_scalar(MU3[:, 3:4], MU3[:, 3:4], LN_EPS, None, ALU.add), "s3ln", 1)
        @blk.scalar
        def _(sc, i=i):
            wait(sc, "s3ln", sems["s3ln"])
            inc(sc.activation(MU3[:, 3:4], MU3[:, 3:4], AF.Sqrt), "s3st", 1)
        @blk.vector
        def _(v, i=i):
            wait(v, "s3st", sems["s3st"])
            v.reciprocal(MU3[:, 3:4], MU3[:, 3:4])
            v.tensor_tensor(MU3[:, 4:5], MU3[:, 2:3], MU3[:, 3:4], ALU.mult)
            inc(v.tensor_scalar(MU3[:, 4:5], MU3[:, 4:5], -1.0, None, ALU.mult), "s3ln", 1)
        @blk.scalar
        def _(sc, i=i):
            wait(sc, "s3ln", sems["s3ln"])
            inc(sc.activation(XL3[:, :], XF3[:, (i % 2) * 1024:(i % 2 + 1) * 1024], AF.Identity,
                              bias=MU3[:, 4:5], scale=MU3[:, 3:4]), "s3st", 1)
        @blk.tensor
        def _(te, i=i):
            wait(te, "s3st", sems["s3st"])
            if i > 0:
                wait(te, "s3ev", sems["s3ev"])
            for j in range(4):
                te.transpose(PS[0][:, j * 128:(j + 1) * 128], XL3[:, j * 128:(j + 1) * 128], IDN[:, :])
            for j in range(3):
                te.transpose(PS[1][:, j * 128:(j + 1) * 128], XL3[:, 512 + j * 128:512 + (j + 1) * 128], IDN[:, :])
            inc(te.transpose(PS[1][:, 384:512], XL3[:, 896:1024], IDN[:, :]), "s3tr", 1)
        @blk.vector
        def _(v, i=i):
            wait(v, "s3tr", sems["s3tr"])
            if i > 0:
                wait(v, "agw4", sems["agw4"])
            v.tensor_copy(XT3[:, 0:512], PS[0][:, :])
            inc(v.tensor_copy(XT3[:, 512:1024], PS[1][:, :]), "s3ev", 1)
        @blk.sync
        def _(s, i=i):
            wait(s, "s3ev", sems["s3ev"])
            for j in range(8):
                inc(s.dma_start(out=t["ag4_in"][j * 128:(j + 1) * 128, i * 128:(i + 1) * 128],
                                in_=XT3[:, j * 128:(j + 1) * 128]), "agw4", 16)
    @blk.gpsimd
    def _(g_):
        wait(g_, "agw4")
        inc(g_.collective_compute("AllToAll", ALU.bypass, replica_groups=[list(range(8))],
                                  ins=[t["ag4_in"].opt()], outs=[t["ag4_out"].opt()]), "cc4", 1)
    @blk.gpsimd
    def _(g_):
        wait(g_, "cc4")
        for j in range(8):
            src = bass_rust.AP(t["ag4_out"].tensor, j * 128 * 2048, [[2048, 128], [1, 2048]])
            dst = bass_rust.AP(t["xch_d"].tensor, j * 2048, [[16384, 128], [1, 2048]])
            inc(g_.dma_start(out=dst, in_=src), "xchw", 16)
    st5.close()

    stcu = ExitStack()
    sbc = lambda n_, sh, dt_=F32: stcu.enter_context(nc.sbuf_tensor(n_, sh, dt_))
    LARc = sbc("LARc", [128, 64]); AIMc = sbc("AIMc", [128, 64])
    CREc = sbc("CREc", [128, 64]); CIMc = sbc("CIMc", [128, 64])
    LDTc = sbc("LDTc", [128, 1]); DSLc = sbc("DSLc", [128, 1]); DTTc = sbc("DTTc", [128, 1])
    AREc = sbc("AREc", [128, 64]); DTAR = sbc("DTAR", [128, 64]); DTAI = sbc("DTAI", [128, 64])
    ERSc = sbc("ERSc", [128, 64]); ARG2 = sbc("ARG2", [128, 128])
    KI2 = stcu.enter_context(nc.sbuf_tensor("KI2", [128, 128], I32))
    KF2 = sbc("KF2", [128, 128]); SC2 = sbc("SC2", [128, 128])
    EDRE = sbc("EDRE", [128, 64]); EDIM = sbc("EDIM", [128, 64])
    T64A = sbc("T64A", [128, 64]); T64B = sbc("T64B", [128, 64]); T64C = sbc("T64C", [128, 64])
    PRODRE = sbc("PRODRE", [128, 64]); PRODIM = sbc("PRODIM", [128, 64]); DENc = sbc("DENc", [128, 64])
    C2RE = sbc("C2RE", [128, 64]); C2IM = sbc("C2IM", [128, 64])
    PIRc = sbc("PIRc", [128, 64]); PIIc = sbc("PIIc", [128, 64])
    CBRE = sbc("CBRE", [128, 8256]); CBIM = sbc("CBIM", [128, 8256])
    TD1c = sbc("TD1c", [128, 4160]); TD2c = sbc("TD2c", [128, 4160])
    KKc = sbc("KKc", [128, 132])

    @blk.sync
    def _(s):
        wait(s, "agw4")
        for nm_, dst in (("lar_sl", LARc), ("aim_sl", AIMc), ("cre_sl", CREc), ("cim_sl", CIMc),
                         ("ldt_sl", LDTc), ("d_sl", DSLc)):
            inc(s.dma_start(out=dst[:, :], in_=t[nm_][:, :]), "prm4", 16)
    @blk.scalar
    def _(sc):
        wait(sc, "prm4", sems["prm4"])
        sc.activation(DTTc[:, :], LDTc[:, :], AF.Exp)
        inc(sc.activation(AREc[:, :], LARc[:, :], AF.Exp), "sprs", 1)
    @blk.vector
    def _(v):
        wait(v, "sprs", sems["sprs"])
        v.tensor_scalar(AREc[:, :], AREc[:, :], -1.0, None, ALU.mult)
        v.tensor_scalar(DTAR[:, :], AREc[:, :], DTTc[:, 0:1], None, ALU.mult)
        v.tensor_scalar(DTAI[:, :], AIMc[:, :], DTTc[:, 0:1], None, ALU.mult)
        v.tensor_copy(ARG2[:, 0:64], DTAI[:, :])
        v.tensor_scalar(ARG2[:, 64:128], DTAI[:, :], 1.0, PI / 2.0, ALU.mult, ALU.add)
        emit_cw(v, ARG2[:, :], KI2[:, :], KF2[:, :])
        inc(nc.vector.drain(), "sprm", 1)
    @blk.scalar
    def _(sc):
        wait(sc, "sprm", sems["sprm"])
        sc.activation(SC2[:, :], ARG2[:, :], AF.Sin)
        inc(sc.activation(ERSc[:, :], DTAR[:, :], AF.Exp), "sprs", 1)
    @blk.vector
    def _(v):
        wait(v, "sprs", sems["sprs"])
        v.tensor_tensor(EDRE[:, :], ERSc[:, :], SC2[:, 64:128], ALU.mult)
        v.tensor_tensor(EDIM[:, :], ERSc[:, :], SC2[:, 0:64], ALU.mult)
        v.tensor_scalar(T64A[:, :], EDRE[:, :], -1.0, None, ALU.add)
        v.tensor_tensor(PRODRE[:, :], CREc[:, :], T64A[:, :], ALU.mult)
        v.tensor_tensor(T64B[:, :], CIMc[:, :], EDIM[:, :], ALU.mult)
        v.tensor_tensor(PRODRE[:, :], PRODRE[:, :], T64B[:, :], ALU.subtract)
        v.tensor_tensor(PRODIM[:, :], CREc[:, :], EDIM[:, :], ALU.mult)
        v.tensor_tensor(T64B[:, :], CIMc[:, :], T64A[:, :], ALU.mult)
        v.tensor_tensor(PRODIM[:, :], PRODIM[:, :], T64B[:, :], ALU.add)
        v.tensor_tensor(DENc[:, :], AREc[:, :], AREc[:, :], ALU.mult)
        v.tensor_tensor(T64B[:, :], AIMc[:, :], AIMc[:, :], ALU.mult)
        v.tensor_tensor(DENc[:, :], DENc[:, :], T64B[:, :], ALU.add)
        v.reciprocal(DENc[:, :], DENc[:, :])
        v.tensor_scalar(DENc[:, :], DENc[:, :], 2.0, None, ALU.mult)
        v.tensor_tensor(C2RE[:, :], PRODRE[:, :], AREc[:, :], ALU.mult)
        v.tensor_tensor(T64B[:, :], PRODIM[:, :], AIMc[:, :], ALU.mult)
        v.tensor_tensor(C2RE[:, :], C2RE[:, :], T64B[:, :], ALU.add)
        v.tensor_tensor(C2RE[:, :], C2RE[:, :], DENc[:, :], ALU.mult)
        v.tensor_tensor(C2IM[:, :], PRODIM[:, :], AREc[:, :], ALU.mult)
        v.tensor_tensor(T64B[:, :], PRODRE[:, :], AIMc[:, :], ALU.mult)
        v.tensor_tensor(C2IM[:, :], C2IM[:, :], T64B[:, :], ALU.subtract)
        v.tensor_tensor(C2IM[:, :], C2IM[:, :], DENc[:, :], ALU.mult)
        v.memset(CBRE[:, 0:64], 1.0)
        v.memset(CBIM[:, 0:64], 0.0)
        v.tensor_copy(CBRE[:, 64:128], EDRE[:, :])
        v.tensor_copy(CBIM[:, 64:128], EDIM[:, :])
        v.tensor_copy(PIRc[:, :], EDRE[:, :])
        v.tensor_copy(PIIc[:, :], EDIM[:, :])
        for k in range(1, 7):
            w = 1 << k
            v.tensor_tensor(T64A[:, :], PIRc[:, :], PIIc[:, :], ALU.mult)
            v.tensor_tensor(T64B[:, :], PIRc[:, :], PIRc[:, :], ALU.mult)
            v.tensor_tensor(T64C[:, :], PIIc[:, :], PIIc[:, :], ALU.mult)
            v.tensor_tensor(PIRc[:, :], T64B[:, :], T64C[:, :], ALU.subtract)
            v.tensor_tensor(PIIc[:, :], T64A[:, :], T64A[:, :], ALU.add)
            s_re = CBRE[:, 0:w * 64].rearrange("p (l n) -> p l n", n=64)
            s_im = CBIM[:, 0:w * 64].rearrange("p (l n) -> p l n", n=64)
            d_re = CBRE[:, w * 64:2 * w * 64].rearrange("p (l n) -> p l n", n=64)
            d_im = CBIM[:, w * 64:2 * w * 64].rearrange("p (l n) -> p l n", n=64)
            pirb = PIRc[:, :].unsqueeze(1).broadcast_to([128, w, 64])
            piib = PIIc[:, :].unsqueeze(1).broadcast_to([128, w, 64])
            td1 = TD1c[:, 0:w * 64].rearrange("p (l n) -> p l n", n=64)
            td2 = TD2c[:, 0:w * 64].rearrange("p (l n) -> p l n", n=64)
            v.tensor_tensor(td1, s_re, pirb, ALU.mult)
            v.tensor_tensor(td2, s_im, piib, ALU.mult)
            v.tensor_tensor(d_re, td1, td2, ALU.subtract)
            v.tensor_tensor(td1, s_re, piib, ALU.mult)
            v.tensor_tensor(td2, s_im, pirb, ALU.mult)
            v.tensor_tensor(d_im, td1, td2, ALU.add)
        v.tensor_tensor(T64A[:, :], CBRE[:, 64 * 64:65 * 64], PIRc[:, :], ALU.mult)
        v.tensor_tensor(T64B[:, :], CBIM[:, 64 * 64:65 * 64], PIIc[:, :], ALU.mult)
        v.tensor_tensor(CBRE[:, 128 * 64:129 * 64], T64A[:, :], T64B[:, :], ALU.subtract)
        v.tensor_tensor(T64A[:, :], CBRE[:, 64 * 64:65 * 64], PIIc[:, :], ALU.mult)
        v.tensor_tensor(T64B[:, :], CBIM[:, 64 * 64:65 * 64], PIRc[:, :], ALU.mult)
        v.tensor_tensor(CBIM[:, 128 * 64:129 * 64], T64A[:, :], T64B[:, :], ALU.add)
        for c0, cn in ((0, 65), (65, 64)):
            cb3 = CBRE[:, c0 * 64:(c0 + cn) * 64].rearrange("p (l n) -> p l n", n=64)
            ci3 = CBIM[:, c0 * 64:(c0 + cn) * 64].rearrange("p (l n) -> p l n", n=64)
            c2rb = C2RE[:, :].unsqueeze(1).broadcast_to([128, cn, 64])
            c2ib = C2IM[:, :].unsqueeze(1).broadcast_to([128, cn, 64])
            td1 = TD1c[:, 0:cn * 64].rearrange("p (l n) -> p l n", n=64)
            td2 = TD2c[:, 0:cn * 64].rearrange("p (l n) -> p l n", n=64)
            v.tensor_tensor(td1, cb3, c2rb, ALU.mult)
            v.tensor_tensor(td2, ci3, c2ib, ALU.mult)
            v.tensor_tensor(td1, td1, td2, ALU.subtract)
            v.tensor_reduce(KKc[:, c0:c0 + cn].unsqueeze(2), td1, AX, ALU.add)
        v.tensor_scalar(KKc[:, 0:1], KKc[:, 0:1], DSLc[:, 0:1], None, ALU.add)
        v.memset(TD1c[:, 0:127], 0.0)
        inc(nc.vector.drain(), "kcub", 1)
    @blk.gpsimd
    def _(g_):
        wait(g_, "kcub", sems["kcub"])
        inc(g_.dma_start(out=t["khank"][:, 127:256], in_=KKc[:, 0:129]), "pd2", 16)
        inc(g_.dma_start(out=t["khank"][:, 0:127], in_=TD1c[:, 0:127]), "pd2", 16)
        inc(g_.dma_start(out=t["cub_re"][:, :], in_=CBRE[:, :]), "pd2", 16)
        inc(g_.dma_start(out=t["cub_im"][:, :], in_=CBIM[:, :]), "pd2", 16)
    TRBATCH = [[(DTAR, 0), (DTAI, 0), (C2RE, 0), (C2IM, 0)],
               [(CBRE, 8 * 64), (CBIM, 8 * 64), (CBRE, 16 * 64), (CBIM, 16 * 64)],
               [(CBRE, 32 * 64), (CBIM, 32 * 64), (CBRE, 64 * 64), (CBIM, 64 * 64)],
               [(CBRE, 128 * 64), (CBIM, 128 * 64)]]
    for bi, batch in enumerate(TRBATCH):
        @blk.tensor
        def _(te, bi=bi, batch=batch):
            wait(te, "kcub", sems["kcub"])
            if bi > 0:
                wait(te, "sprm", sems["sprm"])
            mm = None
            for q, (src, off) in enumerate(batch):
                mm = te.transpose(PS[7][0:64, q * 128:(q + 1) * 128], src[:, off:off + 64], IDN[:, :])
            inc(mm, "sptr", 1)
        @blk.vector
        def _(v, bi=bi, batch=batch):
            wait(v, "sptr", sems["sptr"])
            inc(v.tensor_copy(NMT[:, bi * 512:bi * 512 + len(batch) * 128], PS[7][0:64, 0:len(batch) * 128]), "sprm", 1)
    stcu.close()
    stq2 = ExitStack()
    SSTRE = stq2.enter_context(nc.sbuf_tensor("SSTRE", [64, 8320], F32))
    SSTIM = stq2.enter_context(nc.sbuf_tensor("SSTIM", [64, 8320], F32))
    ZBUF = stq2.enter_context(nc.sbuf_tensor("ZBUF", [128, 8192], F32))

    @blk.gpsimd
    def _(g_):
        wait(g_, "r2w")
        g_.iota(IOT8I[:, :], [[1, 8]], base=0, channel_multiplier=0)
        g_.tensor_copy(IOT8[:, :], IOT8I[:, :])
        inc(nc.gpsimd.drain(), "riot", 1)

    for gidx in range(2):
        @blk.gpsimd
        def _(g_, gidx=gidx):
            if gidx == 0:
                wait(g_, "r2w")
                wait(g_, "pd2", sems["pd2"])
                wait(g_, "sprm", sems["sprm"])
            else:
                wait(g_, "smm2", sems["smm2"])
            for b_ in range(2):
                g_.memset(bass_rust.AP(SSTRE, b_ * 4160, [[8320, 64], [65, 64]]), 0.0)
                g_.memset(bass_rust.AP(SSTIM, b_ * 4160, [[8320, 64], [65, 64]]), 0.0)
            inc(nc.gpsimd.drain(), "szro", 1)

        strt = ExitStack()
        sbr = lambda n_, sh, dt_=F32: strt.enter_context(nc.sbuf_tensor(n_ + str(gidx), sh, dt_))
        RTRE = sbr("RTRE", [64, 8192])
        RTIM = sbr("RTIM", [64, 8192])
        ARS = sbr("ARS", [64, 512]); ARC = sbr("ARC", [64, 512])
        KI8 = strt.enter_context(nc.sbuf_tensor("KI8" + str(gidx), [64, 512], I32))
        KF8 = sbr("KF8", [64, 512])
        SN8 = sbr("SN8", [64, 512]); CS8 = sbr("CS8", [64, 512])
        ER8 = sbr("ER8", [64, 512]); EX8 = sbr("EX8", [64, 512])
        TD1 = sbr("TD1", [64, 1024]); TD2 = sbr("TD2", [64, 1024])

        @blk.vector
        def _(v, gidx=gidx):
            wait(v, "riot", sems["riot"])
            wait(v, "sprm", sems["sprm"])
            if gidx == 0:
                wait(v, "pd2", sems["pd2"])
            else:
                wait(v, "rtg", sems["rtg"])
                wait(v, "smm2", sems["smm2"])
            i3 = IOT8[:, :].unsqueeze(1).broadcast_to([64, 64, 8])
            a3 = ARS[:, :].rearrange("p (c t) -> p c t", t=8)
            dti_b = NMT[:, O_DTI + gidx * 64:O_DTI + gidx * 64 + 64].unsqueeze(2).broadcast_to([64, 64, 8])
            v.tensor_tensor(a3, i3, dti_b, ALU.mult)
            v.tensor_scalar(ARC[:, :], ARS[:, :], 1.0, PI / 2.0, ALU.mult, ALU.add)
            emit_cw(v, ARS[:, :], KI8[:, :], KF8[:, :])
            emit_cw(v, ARC[:, :], KI8[:, :], KF8[:, :])
            e3 = ER8[:, :].rearrange("p (c t) -> p c t", t=8)
            dtr_b = NMT[:, O_DTR + gidx * 64:O_DTR + gidx * 64 + 64].unsqueeze(2).broadcast_to([64, 64, 8])
            v.tensor_tensor(e3, i3, dtr_b, ALU.mult)
            inc(nc.vector.drain(), "rtg", 1)
        @blk.scalar
        def _(sc, gidx=gidx):
            wait(sc, "rtg", sems["rtg"])
            sc.activation(SN8[:, :], ARS[:, :], AF.Sin)
            sc.activation(CS8[:, :], ARC[:, :], AF.Sin)
            inc(sc.activation(EX8[:, :], ER8[:, :], AF.Exp), "rts", 1)
        @blk.vector
        def _(v, gidx=gidx):
            wait(v, "rts", sems["rts"])
            v.tensor_tensor(SN8[:, :], SN8[:, :], EX8[:, :], ALU.mult)
            v.tensor_tensor(CS8[:, :], CS8[:, :], EX8[:, :], ALU.mult)
            t13 = CS8[:, :].rearrange("p (c t) -> p c t", t=8)
            t23 = SN8[:, :].rearrange("p (c t) -> p c t", t=8)
            c2r_b = NMT[:, O_CRE + gidx * 64:O_CRE + gidx * 64 + 64].unsqueeze(2).broadcast_to([64, 64, 8])
            c2i_b = NMT[:, O_CIM + gidx * 64:O_CIM + gidx * 64 + 64].unsqueeze(2).broadcast_to([64, 64, 8])
            w1 = TD1[:, 0:512].rearrange("p (c t) -> p c t", t=8)
            w2 = TD2[:, 0:512].rearrange("p (c t) -> p c t", t=8)
            rre = bass_rust.AP(RTRE, 0, [[8192, 64], [128, 64], [1, 8]])
            rim = bass_rust.AP(RTIM, 0, [[8192, 64], [128, 64], [1, 8]])
            v.tensor_tensor(w1, t13, c2r_b, ALU.mult)
            v.tensor_tensor(w2, t23, c2i_b, ALU.mult)
            v.tensor_tensor(rre, w1, w2, ALU.subtract)
            v.tensor_tensor(w1, t23, c2r_b, ALU.mult)
            v.tensor_tensor(w2, t13, c2i_b, ALU.mult)
            v.tensor_tensor(rim, w1, w2, ALU.add)
            v.tensor_scalar(rim, rim, -1.0, None, ALU.mult)
            for w in (8, 16, 32, 64):
                pc = min(64, 1024 // w)
                for c0 in range(0, 64, pc):
                    sr = bass_rust.AP(RTRE, c0 * 128, [[8192, 64], [128, pc], [1, w]])
                    si = bass_rust.AP(RTIM, c0 * 128, [[8192, 64], [128, pc], [1, w]])
                    dr = bass_rust.AP(RTRE, c0 * 128 + w, [[8192, 64], [128, pc], [1, w]])
                    di = bass_rust.AP(RTIM, c0 * 128 + w, [[8192, 64], [128, pc], [1, w]])
                    wr = NMT[:, O_PW[w] + gidx * 64 + c0:O_PW[w] + gidx * 64 + c0 + pc].unsqueeze(2).broadcast_to([64, pc, w])
                    wi = NMT[:, O_PW[w] + 128 + gidx * 64 + c0:O_PW[w] + 128 + gidx * 64 + c0 + pc].unsqueeze(2).broadcast_to([64, pc, w])
                    td1 = TD1[:, 0:pc * w].rearrange("p (c t) -> p c t", t=w)
                    td2 = TD2[:, 0:pc * w].rearrange("p (c t) -> p c t", t=w)
                    v.tensor_tensor(td1, sr, wr, ALU.mult)
                    v.tensor_tensor(td2, si, wi, ALU.mult)
                    v.tensor_tensor(dr, td1, td2, ALU.add)
                    v.tensor_tensor(td1, si, wr, ALU.mult)
                    v.tensor_tensor(td2, sr, wi, ALU.mult)
                    v.tensor_tensor(di, td1, td2, ALU.subtract)
            inc(nc.vector.drain(), "rtg", 1)

        for chl in range(64):
            ch = gidx * 64 + chl
            par = chl % 2
            nmA = "aldA" if par == 0 else "aldB"
            co = par * 128
            cu = par * 64
            @blk.sync
            def _(s, gidx=gidx, chl=chl, ch=ch, par=par, nmA=nmA, co=co, cu=cu):
                if gidx == 0 and chl == 0:
                    wait(s, "xchw")
                    wait(s, "pd2", sems["pd2"])
                if gidx > 0 or chl >= 2:
                    wait(s, "arev", sems["arev"] - 1)
                    wait(s, "smm", sems["smm"] - 1)
                inc(s.dma_start(out=ATB[:, co:co + 128],
                                in_=bass_rust.AP(t["xch_d"].tensor, ch * 16384, [[128, 128], [1, 128]])), nmA, 16)
                inc(s.dma_start(out=HTB[:, co:co + 128],
                                in_=bass_rust.AP(t["khank"].tensor, ch * 256, [[1, 128], [1, 128]])), nmA, 16)
                inc(s.dma_start(out=URB[:, cu:cu + 64],
                                in_=bass_rust.AP(t["cub_re"].tensor, ch * 8256 + 64, [[64, 128], [1, 64]])), nmA, 16)
                inc(s.dma_start(out=UIB[:, cu:cu + 64],
                                in_=bass_rust.AP(t["cub_im"].tensor, ch * 8256 + 64, [[64, 128], [1, 64]])), nmA, 16)
            @blk.vector
            def _(v, gidx=gidx, chl=chl, par=par, nmA=nmA, co=co):
                wait(v, nmA, sems[nmA])
                if gidx > 0 or chl >= 2:
                    wait(v, "atr", sems["atr"] - 1)
                inc(v.tensor_copy(ARB[:, co:co + 128],
                                  bass_rust.AP(ATB, co + 127, [[256, 128], [-1, 128]])), "arev", 1)
            @blk.tensor
            def _(te, gidx=gidx, chl=chl, par=par, co=co):
                wait(te, "arev", sems["arev"])
                if gidx > 0 or chl >= 2:
                    wait(te, "aev", sems["aev"] - 1)
                inc(te.transpose(PS[par][:, 0:128], ARB[:, co:co + 128], IDN[:, :]), "atr", 1)
            @blk.vector
            def _(v, gidx=gidx, chl=chl, par=par, co=co):
                wait(v, "atr", sems["atr"])
                if gidx > 0 or chl >= 2:
                    wait(v, "smm", sems["smm"] - 1)
                inc(v.tensor_copy(XHB[:, co:co + 128], PS[par][:, 0:128]), "aev", 1)
            @blk.tensor
            def _(te, gidx=gidx, chl=chl, par=par, nmA=nmA, co=co, cu=cu):
                wait(te, "aev", sems["aev"])
                wait(te, nmA, sems[nmA])
                if gidx > 0 or chl >= 2:
                    wait(te, "sev", sems["sev"] - 1)
                te.matmul(PS[2 + par][:, 0:128], HTB[:, co:co + 128], XHB[:, co:co + 128], start=True, stop=True)
                te.matmul(PS[4 + par][0:64, 0:128], URB[:, cu:cu + 64], XHB[:, co:co + 128], start=True, stop=True)
                inc(te.matmul(PS[4 + par][0:64, 128:256], UIB[:, cu:cu + 64], XHB[:, co:co + 128], start=True, stop=True), "smm", 1)
            @blk.vector
            def _(v, gidx=gidx, chl=chl, par=par):
                wait(v, "smm", sems["smm"])
                if gidx == 1 and chl == 0:
                    wait(v, "agr4", 32)
                    wait(v, "smm2", sems["smm2"])
                if gidx == 0 and chl == 0:
                    wait(v, "szro", sems["szro"])
                zv = bass_rust.AP(ZBUF, chl, [[8192, 128], [4096, 2], [64, 64]])
                v.tensor_copy(zv, PS[2 + par][:, 0:128].rearrange("p (b k) -> p b k", b=2))
                sre = bass_rust.AP(SSTRE, chl * 65 + 1, [[8320, 64], [4160, 2], [1, 64]])
                v.tensor_copy(sre, PS[4 + par][0:64, 0:128].rearrange("p (b k) -> p b k", b=2))
                sIm = bass_rust.AP(SSTIM, chl * 65 + 1, [[8320, 64], [4160, 2], [1, 64]])
                v.tensor_copy(sIm, PS[4 + par][0:64, 128:256].rearrange("p (b k) -> p b k", b=2))
                inc(nc.vector.drain(), "sev", 1)

        @blk.vector
        def _(v, gidx=gidx):
            wait(v, "sev", sems["sev"])
            wait(v, "szro", sems["szro"])
            pr = NMT[:, O_PW[128] + gidx * 64:O_PW[128] + gidx * 64 + 64].unsqueeze(1).broadcast_to([64, 2, 64])
            pi = NMT[:, O_PW[128] + 128 + gidx * 64:O_PW[128] + 128 + gidx * 64 + 64].unsqueeze(1).broadcast_to([64, 2, 64])
            t1v = TS1[:, :].rearrange("p (b c) -> p b c", b=2)
            t2v = TS2[:, :].rearrange("p (b c) -> p b c", b=2)
            for s_ in range(1, 64):
                pre_ = bass_rust.AP(SSTRE, s_ - 1, [[8320, 64], [4160, 2], [65, 64]])
                cre_ = bass_rust.AP(SSTRE, s_, [[8320, 64], [4160, 2], [65, 64]])
                pim_ = bass_rust.AP(SSTIM, s_ - 1, [[8320, 64], [4160, 2], [65, 64]])
                cim_ = bass_rust.AP(SSTIM, s_, [[8320, 64], [4160, 2], [65, 64]])
                v.tensor_tensor(t1v, pre_, pr, ALU.mult)
                v.tensor_tensor(t2v, pim_, pi, ALU.mult)
                v.tensor_tensor(cre_, cre_, t1v, ALU.add)
                v.tensor_tensor(cre_, cre_, t2v, ALU.subtract)
                v.tensor_tensor(t1v, pim_, pr, ALU.mult)
                v.tensor_tensor(t2v, pre_, pi, ALU.mult)
                v.tensor_tensor(cim_, cim_, t1v, ALU.add)
                v.tensor_tensor(cim_, cim_, t2v, ALU.add)
            inc(nc.vector.drain(), "scan", 1)

        for chl in range(64):
            par = chl % 2
            @blk.tensor
            def _(te, gidx=gidx, chl=chl, par=par):
                if chl == 0:
                    wait(te, "scan", sems["scan"])
                    wait(te, "rtg", sems["rtg"])
                if chl >= 2:
                    wait(te, "sev2", sems["sev2"] - 1)
                rhs_re = SSTRE[:, :].rearrange("p (b c s) -> p b c s", b=2, c=64, s=65)[:, :, chl:chl + 1, 0:64]
                rhs_im = SSTIM[:, :].rearrange("p (b c s) -> p b c s", b=2, c=64, s=65)[:, :, chl:chl + 1, 0:64]
                te.matmul(PS[2 + par][:, 0:128], RTRE[:, chl * 128:(chl + 1) * 128], rhs_re, start=True, stop=False)
                inc(te.matmul(PS[2 + par][:, 0:128], RTIM[:, chl * 128:(chl + 1) * 128], rhs_im, start=False, stop=True), "smm2", 1)
            @blk.vector
            def _(v, gidx=gidx, chl=chl, par=par):
                wait(v, "smm2", sems["smm2"])
                zv = bass_rust.AP(ZBUF, chl, [[8192, 128], [4096, 2], [64, 64]])
                v.tensor_tensor(zv, zv, PS[2 + par][:, 0:128].rearrange("p (b k) -> p b k", b=2), ALU.add)
                inc(nc.vector.drain(), "sev2", 1)
        @blk.gpsimd
        def _(g_, gidx=gidx):
            wait(g_, "sev2", sems["sev2"])
            for b_ in range(2):
                dst = bass_rust.AP(t["az_in"].tensor, b_ * 1048576 + gidx * 64, [[128, 128], [16384, 64], [1, 64]])
                inc(g_.dma_start(out=dst, in_=ZBUF[:, b_ * 4096:(b_ + 1) * 4096]), "agr4", 16)
        strt.close()

    stq2.close()
    @blk.gpsimd
    def _(g_):
        wait(g_, "agr4", 64)
        inc(g_.collective_compute("AllToAll", ALU.bypass, replica_groups=[list(range(8))],
                                  ins=[t["az_in"].opt()], outs=[t["az_out"].opt()]), "cc5", 1)
    stq.close()

    str_ = ExitStack()
    sbz = lambda n_, sh, dt_=F32: str_.enter_context(nc.sbuf_tensor(n_, sh, dt_))
    XF3r = sbz("XF3r", [128, 2048])
    ZTr = sbz("ZTr", [128, 2048])
    XL3r = sbz("XL3r", [128, 1024])
    RRr = sbz("RRr", [128, 1024])
    OTr = sbz("OTr", [128, 1024])
    LNWr = sbz("LNWr", [128, 1024])
    LNBr = sbz("LNBr", [128, 1024])
    LROWr = sbz("LROWr", [1, 2048])
    MU3r = sbz("MU3r", [128, 8])
    @blk.sync
    def _(s):
        wait(s, "agr4", 64)
        inc(s.dma_start(out=LROWr[:, 0:1024], in_=t["L1_attn_ln_w"][None, :]), "prm4", 16)
        inc(s.dma_start(out=LROWr[:, 1024:2048], in_=t["L1_attn_ln_b"][None, :]), "prm4", 16)
    @blk.tensor
    def _(te):
        wait(te, "prm4", sems["prm4"])
        wait(te, "sev2", sems["sev2"])
        te.matmul(PS[6][:, 0:512], ONES1[:, :], LROWr[0:1, 0:512], start=True, stop=True)
        te.matmul(PS[7][:, 0:512], ONES1[:, :], LROWr[0:1, 512:1024], start=True, stop=True)
        inc(te.matmul(PS[4][:, 0:512], ONES1[:, :], LROWr[0:1, 1024:1536], start=True, stop=True), "glt", 1)
        inc(te.matmul(PS[5][:, 0:512], ONES1[:, :], LROWr[0:1, 1536:2048], start=True, stop=True), "glt", 1)
    @blk.vector
    def _(v):
        wait(v, "glt", sems["glt"])
        wait(v, "agr4", 64)
        v.tensor_copy(LNWr[:, 0:512], PS[6][:, :])
        v.tensor_copy(LNWr[:, 512:1024], PS[7][:, :])
        v.tensor_copy(LNBr[:, 0:512], PS[4][:, :])
        inc(v.tensor_copy(LNBr[:, 512:1024], PS[5][:, :]), "gln", 1)
    for i in range(16):
        par = i % 2
        nmx = "x3ldA" if par == 0 else "x3ldB"
        nmz = "zrd" if par == 0 else "zrdB"
        @blk.sync
        def _(s, i=i, par=par, nmx=nmx, nmz=nmz):
            if i == 0:
                wait(s, "cc5", sems["cc5"])
            if i >= 2:
                wait(s, "rfin", sems["rfin"] - 1)
            inc(s.dma_start(out=XF3r[:, par * 1024:(par + 1) * 1024], in_=t["resid2"][i]), nmx, 16)
            for j in range(8):
                src = bass_rust.AP(t["az_out"].tensor, (j * 2048 + i * 128) * 128, [[128, 128], [1, 128]])
                inc(s.dma_start(out=ZTr[:, par * 1024 + j * 128:par * 1024 + (j + 1) * 128], in_=src), nmz, 16)
        @blk.scalar
        def _(sc, i=i, par=par, nmx=nmx):
            wait(sc, nmx, sems[nmx])
            if i > 0:
                wait(sc, "rfin", sems["rfin"])
            sc.activation(XL3r[:, :], XF3r[:, par * 1024:(par + 1) * 1024], AF.Identity, accum_out=MU3r[:, 0:1])
            inc(sc.activation(XL3r[:, :], XF3r[:, par * 1024:(par + 1) * 1024], AF.Square, accum_out=MU3r[:, 1:2]), "gls", 1)
        @blk.vector
        def _(v, i=i):
            wait(v, "gls", sems["gls"])
            v.tensor_scalar(MU3r[:, 2:3], MU3r[:, 0:1], 1.0 / D, None, ALU.mult)
            v.tensor_scalar(MU3r[:, 3:4], MU3r[:, 1:2], 1.0 / D, None, ALU.mult)
            v.tensor_tensor(MU3r[:, 4:5], MU3r[:, 2:3], MU3r[:, 2:3], ALU.mult)
            v.tensor_tensor(MU3r[:, 3:4], MU3r[:, 3:4], MU3r[:, 4:5], ALU.subtract)
            inc(v.tensor_scalar(MU3r[:, 3:4], MU3r[:, 3:4], LN_EPS, None, ALU.add), "gln", 1)
        @blk.scalar
        def _(sc, i=i):
            wait(sc, "gln", sems["gln"])
            inc(sc.activation(MU3r[:, 3:4], MU3r[:, 3:4], AF.Sqrt), "gls", 1)
        @blk.vector
        def _(v, i=i):
            wait(v, "gls", sems["gls"])
            v.reciprocal(MU3r[:, 3:4], MU3r[:, 3:4])
            v.tensor_tensor(MU3r[:, 4:5], MU3r[:, 2:3], MU3r[:, 3:4], ALU.mult)
            inc(v.tensor_scalar(MU3r[:, 4:5], MU3r[:, 4:5], -1.0, None, ALU.mult), "gln", 1)
        @blk.scalar
        def _(sc, i=i, par=par):
            wait(sc, "gln", sems["gln"])
            inc(sc.activation(XL3r[:, :], XF3r[:, par * 1024:(par + 1) * 1024], AF.Identity,
                              bias=MU3r[:, 4:5], scale=MU3r[:, 3:4]), "gls", 1)
        @blk.vector
        def _(v, i=i, par=par, nmz=nmz):
            wait(v, "gls", sems["gls"])
            wait(v, nmz, sems[nmz])
            inc(v.tensor_tensor(RRr[:, :], XL3r[:, :], ZTr[:, par * 1024:(par + 1) * 1024], ALU.add), "gln", 1)
        @blk.scalar
        def _(sc, i=i):
            wait(sc, "gln", sems["gln"])
            sc.activation(XL3r[:, :], RRr[:, :], AF.Identity, accum_out=MU3r[:, 0:1])
            inc(sc.activation(XL3r[:, :], RRr[:, :], AF.Square, accum_out=MU3r[:, 1:2]), "gls", 1)
        @blk.vector
        def _(v, i=i):
            wait(v, "gls", sems["gls"])
            v.tensor_scalar(MU3r[:, 2:3], MU3r[:, 0:1], 1.0 / D, None, ALU.mult)
            v.tensor_scalar(MU3r[:, 3:4], MU3r[:, 1:2], 1.0 / D, None, ALU.mult)
            v.tensor_tensor(MU3r[:, 4:5], MU3r[:, 2:3], MU3r[:, 2:3], ALU.mult)
            v.tensor_tensor(MU3r[:, 3:4], MU3r[:, 3:4], MU3r[:, 4:5], ALU.subtract)
            inc(v.tensor_scalar(MU3r[:, 3:4], MU3r[:, 3:4], LN_EPS, None, ALU.add), "gln", 1)
        @blk.scalar
        def _(sc, i=i):
            wait(sc, "gln", sems["gln"])
            inc(sc.activation(MU3r[:, 3:4], MU3r[:, 3:4], AF.Sqrt), "gls", 1)
        @blk.vector
        def _(v, i=i):
            wait(v, "gls", sems["gls"])
            v.reciprocal(MU3r[:, 3:4], MU3r[:, 3:4])
            v.tensor_tensor(MU3r[:, 4:5], MU3r[:, 2:3], MU3r[:, 3:4], ALU.mult)
            inc(v.tensor_scalar(MU3r[:, 4:5], MU3r[:, 4:5], -1.0, None, ALU.mult), "gln", 1)
        @blk.scalar
        def _(sc, i=i):
            wait(sc, "gln", sems["gln"])
            inc(sc.activation(RRr[:, :], RRr[:, :], AF.Identity, bias=MU3r[:, 4:5], scale=MU3r[:, 3:4]), "gls", 1)
        @blk.vector
        def _(v, i=i, par=par):
            wait(v, "gls", sems["gls"])
            wait(v, "gln", sems["gln"])
            if i > 0:
                wait(v, "r3w", sems["r3w"])
                if upto == "ssm":
                    wait(v, "o3w", sems["o3w"])
            v.tensor_tensor(OTr[:, :], RRr[:, :], LNWr[:, :], ALU.mult)
            v.tensor_tensor(OTr[:, :], OTr[:, :], LNBr[:, :], ALU.add)
            inc(v.tensor_tensor(OTr[:, :], OTr[:, :], XF3r[:, par * 1024:(par + 1) * 1024], ALU.add), "rfin", 1)
        @blk.gpsimd
        def _(g_, i=i):
            wait(g_, "rfin", sems["rfin"])
            inc(g_.dma_start(out=t["resid3"][i], in_=OTr[:, :]), "r3w", 16)
            if upto == "ssm":
                inc(g_.dma_start(out=t["out"][i * 128:(i + 1) * 128, :], in_=OTr[:, :]), "o3w", 16)

    if upto == "ssm":
        @blk.gpsimd
        def _(g_):
            wait(g_, "o3w"); wait(g_, "r3w")
            inc(nc.gpsimd.drain(), "done", 1)
        str_.close()
        return nc, t, stack, sems
    str_.close()

    stv4 = ExitStack()
    sv4 = lambda n_, sh, dt_=F32: stv4.enter_context(nc.sbuf_tensor(n_, sh, dt_))
    QC4A = sv4("QC4A", [128, 16])
    QC4B = sv4("QC4B", [128, 16])
    R4 = sv4("R4", [1, 16])
    RI4 = sv4("RI4", [1, 4])
    KIQ = stv4.enter_context(nc.sbuf_tensor("KIQ", [1, 4], I32))
    MMX4 = sv4("MMX4", [128, 4])
    T41 = sv4("T41", [128, 2])
    AM8 = sv4("AM8", [8, 4])
    MU4ALL = sv4("MU4ALL", [128, 32])
    B2B = sv4("B2B", [128, 1024])
    BR2 = sv4("BR2", [1, 1024])

    def round_safe_v(eng, VV, QI, Q0, TB, FL):
        eng.tensor_copy(QI, VV)
        eng.tensor_copy(Q0, QI)
        eng.tensor_tensor(TB, VV, Q0, ALU.subtract)
        eng.tensor_single_scalar(FL, TB, 0.5, ALU.is_gt)
        eng.tensor_tensor(Q0, Q0, FL, ALU.add)
        eng.tensor_single_scalar(FL, TB, -0.5, ALU.is_lt)
        eng.tensor_tensor(Q0, Q0, FL, ALU.subtract)

    @blk.gpsimd
    def _(g_):
        wait(g_, "r3w")
        g_.memset(R4[:, :], 0.0)
        g_.memset(RI4[:, 0:1], 127.0)
        g_.memset(RI4[:, 1:2], 255.0)
        inc(nc.gpsimd.drain(), "h4q", 1)
    @blk.vector
    def _(v):
        wait(v, "h4q", sems["h4q"])
        wait(v, "r3w")
        inc(v.reciprocal(RI4[:, 2:4], RI4[:, 0:2]), "w1c", 1)
    @blk.sync
    def _(s):
        wait(s, "r3w")
        inc(s.dma_start(out=AM8[:, 0:2], in_=t["ag1_out"][:, 0:2]), "a1r4", 16)
        inc(s.dma_start(out=BR2[:, 0:1024], in_=t["L1_ffn_b2"][None, :]), "b14", 16)
    @blk.tensor
    def _(te):
        wait(te, "a1r4", sems["a1r4"])
        wait(te, "w1c", sems["w1c"])
        inc(te.transpose(PS[7][0:2, 0:8], AM8[0:8, 0:2], IDN[0:8, 0:8]), "f4tr", 1)
    @blk.vector
    def _(v):
        wait(v, "f4tr", sems["f4tr"])
        inc(v.tensor_reduce(MMX4[0:2, 2:3].unsqueeze(2), PS[7][0:2, 0:8].unsqueeze(1), AX, ALU.max), "f4mx", 1)
    @blk.tensor
    def _(te):
        wait(te, "f4mx", sems["f4mx"])
        inc(te.transpose(PS[6][0:1, 0:2], MMX4[0:2, 2:3], IDN[0:2, 0:2]), "f4tr", 1)
    @blk.vector
    def _(v):
        wait(v, "f4tr", sems["f4tr"])
        v.tensor_copy(R4[0:1, 2:4], PS[6][0:1, 0:2])
        inc(v.tensor_scalar(R4[0:1, 2:4], R4[0:1, 2:4], RI4[0:1, 2:3], 1e-8, ALU.mult, ALU.max), "f4mx", 1)
    st41 = ExitStack()
    s41 = lambda n_, sh, dt_=F32: st41.enter_context(nc.sbuf_tensor(n_, sh, dt_))
    XF4 = s41("XF4", [128, 2048])
    XL4 = s41("XL4", [128, 1024])
    XSC4 = s41("XSC4", [128, 1024])
    MU4 = s41("MU4", [128, 8])
    for i in range(16):
        par = i % 2
        nmx = "x4ldA" if par == 0 else "x4ldB"
        @blk.sync
        def _(s, i=i, par=par, nmx=nmx):
            if i == 0:
                wait(s, "r3w")
            if i >= 2:
                wait(s, "f4st", sems["f4st"] - 3)
            inc(s.dma_start(out=XF4[:, par * 1024:(par + 1) * 1024], in_=t["resid3"][i]), nmx, 16)
        @blk.scalar
        def _(sc, i=i, par=par, nmx=nmx):
            wait(sc, nmx, sems[nmx])
            sc.activation(XSC4[:, :], XF4[:, par * 1024:(par + 1) * 1024], AF.Identity, accum_out=MU4[:, 0:1])
            inc(sc.activation(XSC4[:, :], XF4[:, par * 1024:(par + 1) * 1024], AF.Square, accum_out=MU4[:, 1:2]), "f4st", 1)
        @blk.vector
        def _(v, i=i):
            wait(v, "f4st", sems["f4st"])
            v.tensor_scalar(MU4[:, 2:3], MU4[:, 0:1], 1.0 / D, None, ALU.mult)
            v.tensor_scalar(MU4[:, 3:4], MU4[:, 1:2], 1.0 / D, None, ALU.mult)
            v.tensor_tensor(MU4[:, 4:5], MU4[:, 2:3], MU4[:, 2:3], ALU.mult)
            v.tensor_tensor(MU4[:, 3:4], MU4[:, 3:4], MU4[:, 4:5], ALU.subtract)
            inc(v.tensor_scalar(MU4[:, 3:4], MU4[:, 3:4], LN_EPS, None, ALU.add), "f4ln", 1)
        @blk.scalar
        def _(sc, i=i):
            wait(sc, "f4ln", sems["f4ln"])
            inc(sc.activation(MU4[:, 3:4], MU4[:, 3:4], AF.Sqrt), "f4st", 1)
        @blk.vector
        def _(v, i=i):
            wait(v, "f4st", sems["f4st"])
            v.reciprocal(MU4ALL[:, 2 * i:2 * i + 1], MU4[:, 3:4])
            v.tensor_tensor(MU4ALL[:, 2 * i + 1:2 * i + 2], MU4[:, 2:3], MU4ALL[:, 2 * i:2 * i + 1], ALU.mult)
            inc(v.tensor_scalar(MU4ALL[:, 2 * i + 1:2 * i + 2], MU4ALL[:, 2 * i + 1:2 * i + 2], -1.0, None, ALU.mult), "f4ln", 1)
        @blk.scalar
        def _(sc, i=i, par=par):
            wait(sc, "f4ln", sems["f4ln"])
            if i > 0:
                wait(sc, "f4mx", sems["f4mx"])
            inc(sc.activation(XL4[:, :], XF4[:, par * 1024:(par + 1) * 1024], AF.Identity,
                              bias=MU4ALL[:, 2 * i + 1:2 * i + 2], scale=MU4ALL[:, 2 * i:2 * i + 1]), "f4st", 1)
        @blk.vector
        def _(v, i=i):
            wait(v, "f4st", sems["f4st"])
            v.tensor_reduce(T41[:, 0:1].unsqueeze(2), XL4[:, :].unsqueeze(1), AX, ALU.max)
            v.tensor_reduce(T41[:, 1:2].unsqueeze(2), XL4[:, :].unsqueeze(1), AX, ALU.min)
            if i == 0:
                v.tensor_copy(MMX4[:, 0:2], T41[:, 0:2])
            else:
                v.tensor_tensor(MMX4[:, 0:1], MMX4[:, 0:1], T41[:, 0:1], ALU.max)
                v.tensor_tensor(MMX4[:, 1:2], MMX4[:, 1:2], T41[:, 1:2], ALU.min)
            inc(nc.vector.drain(), "f4mx", 1)

    def emit_armax(col_src, ar_in, ar_out, cc_nm, arw_nm, arr_nm, rdst):
        @blk.vector
        def _(v):
            wait(v, "f4mx", sems["f4mx"])
            inc(v.tensor_scalar(MMX4[:, col_src + 1:col_src + 2], MMX4[:, col_src + 1:col_src + 2], -1.0, None, ALU.mult), "f4mx", 1)
        @blk.tensor
        def _(te):
            wait(te, "f4mx", sems["f4mx"])
            inc(te.transpose(PS[7][0:2, 0:128], MMX4[:, col_src:col_src + 2], IDN[:, :]), "f4tr", 1)
        @blk.vector
        def _(v):
            wait(v, "f4tr", sems["f4tr"])
            inc(v.tensor_reduce(MMX4[0:2, 2:3].unsqueeze(2), PS[7][0:2, 0:128].unsqueeze(1), AX, ALU.max), "f4mx", 1)
        @blk.tensor
        def _(te):
            wait(te, "f4mx", sems["f4mx"])
            inc(te.transpose(PS[6][0:1, 0:2], MMX4[0:2, 2:3], IDN[0:2, 0:2]), "f4tr", 1)
        @blk.vector
        def _(v):
            wait(v, "f4tr", sems["f4tr"])
            inc(v.tensor_copy(R4[0:1, 0:2], PS[6][0:1, 0:2]), "f4mx", 1)
        @blk.gpsimd
        def _(g_):
            wait(g_, "f4mx", sems["f4mx"])
            inc(g_.dma_start(out=t[ar_in][:, :], in_=R4[0:1, 0:2]), arw_nm, 16)
            wait(g_, arw_nm, sems[arw_nm])
            inc(g_.collective_compute("AllReduce", ALU.max, replica_groups=[list(range(8))],
                                      ins=[t[ar_in].opt()], outs=[t[ar_out].opt()]), cc_nm, 1)
        @blk.sync
        def _(s):
            wait(s, cc_nm, sems[cc_nm])
            inc(s.dma_start(out=R4[0:1, rdst:rdst + 2], in_=t[ar_out][:, :]), arr_nm, 16)

    def emit_scale_math(rdst, sw_col, qc_dst_getter):
        @blk.vector
        def _(v):
            wait(v, qc_dst_getter, sems[qc_dst_getter])
            v.tensor_tensor(R4[0:1, 6:7], R4[0:1, rdst:rdst + 1], R4[0:1, rdst + 1:rdst + 2], ALU.add)
            v.tensor_scalar(R4[0:1, 6:7], R4[0:1, 6:7], RI4[0:1, 3:4], 1e-8, ALU.mult, ALU.max)
            v.reciprocal(R4[0:1, 7:8], R4[0:1, 6:7])
            v.tensor_tensor(R4[0:1, 8:9], R4[0:1, rdst + 1:rdst + 2], R4[0:1, 7:8], ALU.mult)
            round_safe_v(v, R4[0:1, 8:9], KIQ[0:1, 0:1], R4[0:1, 9:10], R4[0:1, 10:11], R4[0:1, 11:12])
            v.tensor_scalar(R4[0:1, 9:10], R4[0:1, 9:10], -128.0, None, ALU.add)
            v.tensor_single_scalar(R4[0:1, 9:10], R4[0:1, 9:10], -128.0, ALU.max)
            v.tensor_single_scalar(R4[0:1, 9:10], R4[0:1, 9:10], 127.0, ALU.min)
            v.tensor_scalar(R4[0:1, 12:13], R4[0:1, 9:10], -1.0, -128.0, ALU.mult, ALU.add)
            v.tensor_scalar(R4[0:1, 13:14], R4[0:1, 9:10], -1.0, 127.0, ALU.mult, ALU.add)
            v.tensor_tensor(R4[0:1, 14:15], R4[0:1, 6:7], R4[0:1, sw_col:sw_col + 1], ALU.mult)
            inc(nc.vector.drain(), "f4mx", 1)

    def emit_qc_bcast(QCdst):
        @blk.tensor
        def _(te):
            wait(te, "f4mx", sems["f4mx"])
            inc(te.matmul(PS[6][:, 0:16], ONES1[:, :], R4[0:1, 0:16], start=True, stop=True), "f4tr", 1)
        @blk.vector
        def _(v):
            wait(v, "f4tr", sems["f4tr"])
            inc(v.tensor_copy(QCdst[:, :], PS[6][:, 0:16]), "q4", 1)

    emit_armax(0, "ar6_in", "ar6_out", "cc6", "ar6w", "ar6r", 4)
    emit_scale_math(4, 2, "ar6r")
    emit_qc_bcast(QC4A)

    st42 = ExitStack()
    s42 = lambda n_, sh, dt_=F32: st42.enter_context(nc.sbuf_tensor(n_, sh, dt_))
    XQT = st42.enter_context(nc.sbuf_tensor("XQT", [128, 16384], BF16))
    W1TB = st42.enter_context(nc.sbuf_tensor("W1TB", [128, 8192], BF16))
    BR4 = st42.enter_context(nc.sbuf_tensor("BR4", [1, 4096], F32))
    B1B = st42.enter_context(nc.sbuf_tensor("B1B", [128, 4096], F32))
    XQ4 = s42("XQ4", [128, 1024])
    @blk.sync
    def _(s):
        wait(s, "f4st", sems["f4st"])
        inc(s.dma_start(out=BR4[:, 0:4096], in_=t["L1_ffn_b1"][None, :]), "b14", 16)
    for r in range(5):
        @blk.tensor
        def _(te, r=r):
            wait(te, "b14", sems["b14"])
            if r > 0:
                wait(te, "w1c", sems["w1c"])
            else:
                wait(te, "q4", sems["q4"])
            srcrow = BR4[0:1, r * 1024:r * 1024 + 512] if r < 4 else BR2[0:1, 0:512]
            srcrow2 = BR4[0:1, r * 1024 + 512:(r + 1) * 1024] if r < 4 else BR2[0:1, 512:1024]
            te.matmul(PS[6][:, 0:512], ONES1[:, :], srcrow, start=True, stop=True)
            inc(te.matmul(PS[7][:, 0:512], ONES1[:, :], srcrow2, start=True, stop=True), "arbc", 1)
        @blk.vector
        def _(v, r=r):
            wait(v, "arbc", sems["arbc"])
            DSTB = B1B if r < 4 else B2B
            c0 = r * 1024 if r < 4 else 0
            v.tensor_copy(DSTB[:, c0:c0 + 512], PS[6][:, :])
            inc(v.tensor_copy(DSTB[:, c0 + 512:c0 + 1024], PS[7][:, :]), "w1c", 1)

    QI4a = st42.enter_context(nc.sbuf_tensor("QI4a", [128, 1024], I32))
    Q04a = s42("Q04a", [128, 1024])
    TB4a = s42("TB4a", [128, 1024])
    FL4a = s42("FL4a", [128, 1024])
    HV4 = s42("HV4", [128, 1024])
    HG4 = s42("HG4", [128, 1024])
    for i in range(16):
        par = i % 2
        nmx = "x4ldA" if par == 0 else "x4ldB"
        @blk.sync
        def _(s, i=i, par=par, nmx=nmx):
            if i >= 2:
                wait(s, "q4s", sems["q4s"] - 1)
            else:
                wait(s, "f4st", sems["f4st"])
            inc(s.dma_start(out=XF4[:, par * 1024:(par + 1) * 1024], in_=t["resid3"][i]), nmx, 16)
        @blk.scalar
        def _(sc, i=i, par=par, nmx=nmx):
            wait(sc, nmx, sems[nmx])
            if i > 0:
                wait(sc, "q4", sems["q4"])
            else:
                wait(sc, "f4mx", sems["f4mx"])
            inc(sc.activation(XL4[:, :], XF4[:, par * 1024:(par + 1) * 1024], AF.Identity,
                              bias=MU4ALL[:, 2 * i + 1:2 * i + 2], scale=MU4ALL[:, 2 * i:2 * i + 1]), "q4s", 1)
        @blk.vector
        def _(v, i=i):
            wait(v, "q4s", sems["q4s"])
            if i > 0:
                wait(v, "f4tr2", sems["f4tr2"])
            v.tensor_scalar(XQ4[:, :], XL4[:, :], QC4A[:, 7:8], None, ALU.mult)
            round_safe_v(v, XQ4[:, :], QI4a[:, :], Q04a[:, :], TB4a[:, :], FL4a[:, :])
            v.tensor_scalar(Q04a[:, :], Q04a[:, :], QC4A[:, 12:13], None, ALU.max)
            inc(v.tensor_scalar(Q04a[:, :], Q04a[:, :], QC4A[:, 13:14], None, ALU.min), "q4", 1)
        @blk.tensor
        def _(te, i=i):
            wait(te, "q4", sems["q4"])
            if i > 0:
                wait(te, "f4ev", sems["f4ev"])
            mm = None
            for j in range(4):
                te.transpose(PS[0][:, j * 128:(j + 1) * 128], Q04a[:, j * 128:(j + 1) * 128], IDN[:, :])
                mm = te.transpose(PS[1][:, j * 128:(j + 1) * 128], Q04a[:, 512 + j * 128:512 + (j + 1) * 128], IDN[:, :])
            inc(mm, "f4tr2", 1)
        @blk.vector
        def _(v, i=i):
            wait(v, "f4tr2", sems["f4tr2"])
            v.tensor_copy(XQT[:, i * 1024:i * 1024 + 512], PS[0][:, :])
            inc(v.tensor_copy(XQT[:, i * 1024 + 512:(i + 1) * 1024], PS[1][:, :]), "f4ev", 1)
    for ob in range(8):
        parw = ob % 2
        nmw = "w1ld4A" if parw == 0 else "w1ld4B"
        @blk.sync
        def _(s, ob=ob, parw=parw, nmw=nmw):
            if ob >= 2:
                wait(s, "g4mm", sems["g4mm"] - 16)
            else:
                wait(s, "cc2b", sems["cc2b"])
            for hb in range(8):
                src = bass_rust.AP(t["ag2b_out"].tensor, ob * AG2BN + OFF_W1 + hb * 128 * 512, [[512, 128], [1, 512]])
                inc(s.dma_start(out=W1TB[:, parw * 4096 + hb * 512:parw * 4096 + (hb + 1) * 512], in_=src), nmw, 16)
        for i in range(16):
            k = ob * 16 + i
            par2 = k % 2
            @blk.tensor
            def _(te, ob=ob, i=i, k=k, par2=par2, parw=parw, nmw=nmw):
                wait(te, nmw, sems[nmw])
                if ob == 0 and i == 0:
                    wait(te, "f4ev", sems["f4ev"])
                if k >= 2:
                    wait(te, "g4e", sems["g4e"] - 1)
                mm = None
                for hb in range(8):
                    mm = te.matmul(PS[2 + par2][:, 0:512], XQT[:, i * 1024 + hb * 128:i * 1024 + (hb + 1) * 128],
                                   W1TB[:, parw * 4096 + hb * 512:parw * 4096 + (hb + 1) * 512],
                                   start=(hb == 0), stop=(hb == 7))
                inc(mm, "g4mm", 1)
            @blk.vector
            def _(v, ob=ob, i=i, k=k, par2=par2):
                wait(v, "g4mm", sems["g4mm"])
                if k >= 2:
                    wait(v, "gel4", sems["gel4"] - 1)
                v.tensor_scalar(HV4[:, par2 * 512:(par2 + 1) * 512], PS[2 + par2][:, 0:512], QC4A[:, 14:15], None, ALU.mult)
                inc(v.tensor_tensor(HV4[:, par2 * 512:(par2 + 1) * 512], HV4[:, par2 * 512:(par2 + 1) * 512],
                                    B1B[:, ob * 512:(ob + 1) * 512], ALU.add), "g4e", 1)
            @blk.scalar
            def _(sc, ob=ob, i=i, k=k, par2=par2):
                wait(sc, "g4e", sems["g4e"])
                if k >= 2:
                    wait(sc, "hpwA" if par2 == 0 else "hpwB", sems["hpwA" if par2 == 0 else "hpwB"])
                inc(sc.activation(HG4[:, par2 * 512:(par2 + 1) * 512], HV4[:, par2 * 512:(par2 + 1) * 512], AF.Gelu), "gel4", 1)
            @blk.vector
            def _(v, ob=ob, i=i, k=k, par2=par2):
                wait(v, "gel4", sems["gel4"])
                v.tensor_reduce(T41[:, 0:1].unsqueeze(2), HG4[:, par2 * 512:(par2 + 1) * 512].unsqueeze(1), AX, ALU.max)
                v.tensor_reduce(T41[:, 1:2].unsqueeze(2), HG4[:, par2 * 512:(par2 + 1) * 512].unsqueeze(1), AX, ALU.min)
                if k == 0:
                    v.tensor_copy(MMX4[:, 2:4], T41[:, 0:2])
                else:
                    v.tensor_tensor(MMX4[:, 2:3], MMX4[:, 2:3], T41[:, 0:1], ALU.max)
                    v.tensor_tensor(MMX4[:, 3:4], MMX4[:, 3:4], T41[:, 1:2], ALU.min)
                inc(nc.vector.drain(), "f4mx2", 1)
            @blk.gpsimd
            def _(g_, ob=ob, i=i, k=k, par2=par2):
                wait(g_, "f4mx2", sems["f4mx2"])
                inc(g_.dma_start(out=t["hpark"][i, :, ob * 512:(ob + 1) * 512],
                                 in_=HG4[:, par2 * 512:(par2 + 1) * 512]), "hpwA" if par2 == 0 else "hpwB", 16)
    st42.close()

    @blk.vector
    def _(v):
        wait(v, "f4mx2", sems["f4mx2"])
        v.tensor_copy(MMX4[:, 0:2], MMX4[:, 2:4])
        inc(nc.vector.drain(), "f4mx", 1)
    emit_armax(0, "ar7_in", "ar7_out", "cc7", "ar7w", "ar7r", 4)
    emit_scale_math(4, 3, "ar7r")
    emit_qc_bcast(QC4B)

    st43 = ExitStack()
    s43 = lambda n_, sh, dt_=F32: st43.enter_context(nc.sbuf_tensor(n_, sh, dt_))
    W2TB = st43.enter_context(nc.sbuf_tensor("W2TB", [128, 32768], BF16))
    HP4 = s43("HP4", [128, 8192])
    QI4b = st43.enter_context(nc.sbuf_tensor("QI4b", [128, 4096], I32))
    Q04b = s43("Q04b", [128, 4096])
    TB4b = s43("TB4b", [128, 4096])
    HQT4 = st43.enter_context(nc.sbuf_tensor("HQT4", [128, 4096], BF16))
    OUT4 = s43("OUT4", [128, 1024])
    @blk.sync
    def _(s):
        wait(s, "hpwA"); wait(s, "hpwB")
        for oq in range(32):
            for cp in range(8):
                src = bass_rust.AP(t["ag2b_out"].tensor, cp * AG2BN + OFF_W2 + oq * 128 * 128, [[128, 128], [1, 128]])
                inc(s.dma_start(out=W2TB[:, oq * 1024 + cp * 128:oq * 1024 + (cp + 1) * 128], in_=src), "w2ld4", 16)
    for i in range(16):
        par = i % 2
        nmh = "hpldA" if par == 0 else "hpldB"
        nmx = "x4ldA" if par == 0 else "x4ldB"
        @blk.sync
        def _(s, i=i, par=par, nmh=nmh, nmx=nmx):
            if i >= 2:
                wait(s, "h4qv", sems["h4qv"] - 1)
                wait(s, "d4e", sems["d4e"] - 1)
            else:
                wait(s, "q4s", sems["q4s"])
            inc(s.dma_start(out=HP4[:, par * 4096:(par + 1) * 4096], in_=t["hpark"][i]), nmh, 16)
            inc(s.dma_start(out=XF4[:, par * 1024:(par + 1) * 1024], in_=t["resid3"][i]), nmx, 16)
        @blk.vector
        def _(v, i=i, par=par, nmh=nmh):
            wait(v, nmh, sems[nmh])
            if i > 0:
                wait(v, "h4tr", sems["h4tr"])
            hp = HP4[:, par * 4096:(par + 1) * 4096]
            v.tensor_scalar(hp, hp, QC4B[:, 7:8], None, ALU.mult)
            v.tensor_copy(QI4b[:, :], hp)
            v.tensor_copy(Q04b[:, :], QI4b[:, :])
            v.tensor_tensor(TB4b[:, :], hp, Q04b[:, :], ALU.subtract)
            v.tensor_single_scalar(hp, TB4b[:, :], 0.5, ALU.is_gt)
            v.tensor_tensor(Q04b[:, :], Q04b[:, :], hp, ALU.add)
            v.tensor_single_scalar(hp, TB4b[:, :], -0.5, ALU.is_lt)
            v.tensor_tensor(Q04b[:, :], Q04b[:, :], hp, ALU.subtract)
            v.tensor_scalar(Q04b[:, :], Q04b[:, :], QC4B[:, 12:13], None, ALU.max)
            v.tensor_scalar(Q04b[:, :], Q04b[:, :], QC4B[:, 13:14], None, ALU.min)
            inc(nc.vector.drain(), "h4qv", 1)
        for tb in range(8):
            @blk.tensor
            def _(te, i=i, tb=tb):
                wait(te, "h4qv", sems["h4qv"])
                if i > 0 or tb >= 2:
                    wait(te, "h4ev", sems["h4ev"] - 1)
                mm = None
                for q in range(4):
                    mm = te.transpose(PS[tb % 2][:, q * 128:(q + 1) * 128],
                                      Q04b[:, tb * 512 + q * 128:tb * 512 + (q + 1) * 128], IDN[:, :])
                inc(mm, "h4tr", 1)
            @blk.vector
            def _(v, i=i, tb=tb):
                wait(v, "h4tr", sems["h4tr"])
                if i > 0 and tb == 0:
                    wait(v, "d4mm", sems["d4mm"])
                inc(v.tensor_copy(HQT4[:, tb * 512:(tb + 1) * 512], PS[tb % 2][:, :]), "h4ev", 1)
        b4a = 4 if par == 0 else 6
        b4b = 5 if par == 0 else 7
        @blk.tensor
        def _(te, i=i, b4a=b4a, b4b=b4b):
            wait(te, "h4ev", sems["h4ev"])
            wait(te, "w2ld4", sems["w2ld4"])
            if i == 1:
                wait(te, "q4", sems["q4"])
            if i >= 2:
                wait(te, "d4e", sems["d4e"] - 1)
            mm = None
            for oq in range(32):
                te.matmul(PS[b4a][:, 0:512], HQT4[:, oq * 128:(oq + 1) * 128],
                          W2TB[:, oq * 1024:oq * 1024 + 512], start=(oq == 0), stop=(oq == 31))
                mm = te.matmul(PS[b4b][:, 0:512], HQT4[:, oq * 128:(oq + 1) * 128],
                               W2TB[:, oq * 1024 + 512:(oq + 1) * 1024], start=(oq == 0), stop=(oq == 31))
            inc(mm, "d4mm", 1)
        @blk.vector
        def _(v, i=i, par=par, nmx=nmx, b4a=b4a, b4b=b4b):
            wait(v, "d4mm", sems["d4mm"])
            wait(v, nmx, sems[nmx])
            if i > 0:
                wait(v, "o4w", sems["o4w"])
            v.tensor_scalar(OUT4[:, 0:512], PS[b4a][:, 0:512], QC4B[:, 14:15], None, ALU.mult)
            v.tensor_scalar(OUT4[:, 512:1024], PS[b4b][:, 0:512], QC4B[:, 14:15], None, ALU.mult)
            v.tensor_tensor(OUT4[:, :], OUT4[:, :], B2B[:, :], ALU.add)
            inc(v.tensor_tensor(OUT4[:, :], OUT4[:, :], XF4[:, par * 1024:(par + 1) * 1024], ALU.add), "d4e", 1)
        @blk.gpsimd
        def _(g_, i=i):
            wait(g_, "d4e", sems["d4e"])
            inc(g_.dma_start(out=t["out"][i * 128:(i + 1) * 128, :], in_=OUT4[:, :]), "o4w", 16)
    @blk.gpsimd
    def _(g_):
        wait(g_, "o4w")
        inc(nc.gpsimd.drain(), "done", 1)
    st43.close()
    st41.close()
    stv4.close()
    return nc, t, stack, sems




from concourse.bass_utils import run_bass_kernel_spmd


def kernel(**inputs):
    nc, t, stack, sems = build(upto="full")
    in_maps = host_inputs(inputs)
    res = run_bass_kernel_spmd(nc, in_maps, core_ids=list(range(NC)))
    outs = [np.asarray(res.results[c]["out"]) for c in range(NC)]
    full = np.empty((B, S, D), dtype=np.float32)
    for c in range(NC):
        b, qd = c // 4, c % 4
        full[b, qd * TPC:(qd + 1) * TPC, :] = outs[c]
    stack.close()
    return full
